# version 19
# baseline (speedup 1.0000x reference)
import numpy as np
from contextlib import ExitStack

import concourse.bass as bass
import concourse.mybir as mybir
from concourse.bass_utils import run_bass_kernel_spmd
import ml_dtypes

F32 = mybir.dt.float32
BF16 = mybir.dt.bfloat16
AF = mybir.ActivationFunctionType
ALU = mybir.AluOpType
AX = mybir.AxisListType

NCORE = 8
P = 128
D = 1024
KC = 8
TLOC = 2048
NRT = 16
POS = 1024
NB = 2
RANK = 64
RA = 65
CROWS = 66
QBS = 256
NSLOT = 12
NQB = POS // QBS
NEP = NB * NQB
NKT = 64
VCH = 4
NVC = NKT // VCH
EPS = 1e-5
INV32 = 1.0 / 32.0

CONV_W = ("uf", "ui", "uc", "wf", "wi", "wc")
CIDX = {nm: i for i, nm in enumerate(CONV_W)}


def build_nc(ph=5, dbg=False):
    nc = bass.Bass("TRN2", target_bir_lowering=False)

    x_d = nc.dram_tensor("x", [TLOC, D], F32, kind="ExternalInput").ap()
    qt_d = nc.dram_tensor("qt", [P, KC, RANK], F32, kind="ExternalInput").ap()
    kt_d = nc.dram_tensor("kt", [P, KC, RANK], F32, kind="ExternalInput").ap()
    vo_d = nc.dram_tensor("vo", [P, KC, D], F32, kind="ExternalInput").ap()
    wo_d = nc.dram_tensor("wo", [P, KC, D], F32, kind="ExternalInput").ap()
    uo_d = nc.dram_tensor("uo", [P, KC, D], F32, kind="ExternalInput").ap()
    ws_d = {}
    for nm in ("wf", "wi", "wc", "uf", "ui", "uc"):
        ws_d[nm] = nc.dram_tensor(nm, [KC, P, KC, P], F32,
                                  kind="ExternalInput").ap()
    b_d = {}
    for nm in ("bf", "bi", "bc", "bo"):
        b_d[nm] = nc.dram_tensor(nm, [1, D], F32, kind="ExternalInput").ap()
    inc_d = nc.dram_tensor("in_c", [P, KC], F32, kind="ExternalInput").ap()
    out_d = nc.dram_tensor("out", [TLOC, D], F32, kind="ExternalOutput").ap()

    wbf_d = {nm: nc.dram_tensor(f"wbf_{nm}", [KC, P, KC * P], BF16,
                                kind="Internal").ap()
             for nm in CONV_W}

    CTR = TLOC + 2 * CROWS
    ctb_d = nc.dram_tensor("ctb", [CTR, D], BF16, kind="Internal")
    cgat_d = nc.dram_tensor("cgat", [NCORE * CTR, D], BF16, kind="Internal")

    ident_d = nc.inline_tensor(np.eye(P, dtype=ml_dtypes.bfloat16),
                               "ident").ap()
    ones_d = nc.inline_tensor(np.ones((P, 512), dtype=ml_dtypes.bfloat16),
                              "onesc").ap()

    rg = [list(range(NCORE))]

    dbg_d = {}
    if dbg:
        dbg_d["xn"] = nc.dram_tensor("dbg_xn", [P, KC * TLOC], BF16,
                                     kind="ExternalOutput").ap()
        dbg_d["xq"] = nc.dram_tensor("dbg_xq", [RA, TLOC], BF16,
                                     kind="ExternalOutput").ap()
        dbg_d["c2"] = nc.dram_tensor("dbg_c2", [P, KC * POS], BF16,
                                     kind="ExternalOutput").ap()
        dbg_d["c1"] = nc.dram_tensor("dbg_c1", [P, KC * POS], F32,
                                     kind="ExternalOutput").ap()
        dbg_d["stats"] = nc.dram_tensor("dbg_stats", [P, 2 * NRT], F32,
                                        kind="ExternalOutput").ap()
        dbg_d["wo"] = nc.dram_tensor("dbg_wo", [P, KC * D], BF16,
                                     kind="ExternalOutput").ap()
        dbg_d["uo"] = nc.dram_tensor("dbg_uo", [P, KC * D], BF16,
                                     kind="ExternalOutput").ap()
        dbg_d["kall"] = nc.dram_tensor("dbg_kall", [RA, NCORE * TLOC], BF16,
                                       kind="ExternalOutput").ap()
        dbg_d["mkv"] = nc.dram_tensor("dbg_mkv", [1, 1], F32,
                                      kind="ExternalOutput").ap()
        dbg_d["hsb"] = nc.dram_tensor("dbg_hsb", [P, 2 * D], F32,
                                      kind="ExternalOutput").ap()
        dbg_d["kctb"] = nc.dram_tensor("dbg_kctb", [CROWS, TLOC], BF16,
                                       kind="ExternalOutput").ap()
        dbg_d["xkc"] = nc.dram_tensor("dbg_xkc", [RA, TLOC], BF16,
                                      kind="ExternalOutput").ap()

    with ExitStack() as S0:
        ident_sb = S0.enter_context(nc.sbuf_tensor("ident_sb", [P, P], BF16))
        ones_sb = S0.enter_context(nc.sbuf_tensor("ones_sb", [P, 512], BF16))
        xn_fm = S0.enter_context(nc.sbuf_tensor("xn_fm", [P, KC * TLOC], BF16))
        c2_bf = S0.enter_context(nc.sbuf_tensor("c2_bf", [P, KC * POS], BF16))
        xq_aug = S0.enter_context(nc.sbuf_tensor("xq_aug", [RA, TLOC], BF16))
        qn_bf = S0.enter_context(nc.sbuf_tensor("qn_bf", [1, TLOC], BF16))
        inc_f = S0.enter_context(nc.sbuf_tensor("inc_f", [P, KC], F32))
        inc_b = S0.enter_context(nc.sbuf_tensor("inc_b", [P, KC], BF16))
        b_bf = {nm: S0.enter_context(nc.sbuf_tensor(f"b_{nm}", [1, D], BF16))
                for nm in ("f", "i", "c", "o")}
        rf_bf = {nm: S0.enter_context(nc.sbuf_tensor(f"r_{nm}", [1, D], BF16))
                 for nm in ("f", "i", "c")}
        st_sum = S0.enter_context(nc.sbuf_tensor("st_sum", [P, NRT], F32))
        st_sq = S0.enter_context(nc.sbuf_tensor("st_sq", [P, NRT], F32))
        st_mu = S0.enter_context(nc.sbuf_tensor("st_mu", [P, NRT], F32))
        st_t0 = S0.enter_context(nc.sbuf_tensor("st_t0", [P, NRT], F32))
        st_var = S0.enter_context(nc.sbuf_tensor("st_var", [P, NRT], F32))
        st_rs = S0.enter_context(nc.sbuf_tensor("st_rs", [P, NRT], F32))
        tmp_row = S0.enter_context(nc.sbuf_tensor("tmp_row", [1, 512], F32))
        mk4 = S0.enter_context(nc.sbuf_tensor("mk4", [1, 4], F32))
        lmax_bf = S0.enter_context(nc.sbuf_tensor("lmax_bf", [1, 1], BF16))
        mk8 = S0.enter_context(nc.sbuf_tensor("mk8", [1, NCORE], BF16))
        mkv = S0.enter_context(nc.sbuf_tensor("mkv", [1, 1], F32))
        mkv2 = S0.enter_context(nc.sbuf_tensor("mkv2", [1, 1], F32))
        eps_col = S0.enter_context(nc.sbuf_tensor("eps_col", [P, 1], F32))
        onesf = S0.enter_context(nc.sbuf_tensor("onesf", [1, 4], F32))

        sem_names = [
            "s_const", "s_x", "s_xb", "s_sum", "s_sq", "s_var", "s_rstd",
            "s_ap", "s_tp", "s_ev",
            "s_qkst", "s_qk", "s_conv", "s_wb", "s_slg",
            "s_p2mm", "s_xqe", "s_xke", "s_sq2xk", "s_nk", "s_nkr",
            "s_sq2xq", "s_nq", "s_qnl", "s_qnx",
            "s_vmm", "s_vev", "s_ct", "s_cc", "s_kg", "s_mk8r", "s_mkv",
            "s_mhat",
            "s_sl", "s_mv", "s_rfe",
            "s_zmm", "s_gact", "s_cdve",
            "s_wol",
            "s_smm", "s_exp", "s_pfree", "s_vs", "s_vfree", "s_lrec", "s_ltp",
            "s_att", "s_hmm", "s_tanh", "s_eadd", "s_pool", "s_xr", "s_st",
            "s_vch", "s_ach",
        ]
        sem = {nm: S0.enter_context(nc.semaphore(nm)) for nm in sem_names}
        vch_n = [0]
        ach_n = [0]

        def vfence(vector, inst):
            vch_n[0] += 1
            inst.then_inc(sem["s_vch"], 1)
            vector.wait_ge(sem["s_vch"], vch_n[0])

        def afence(scalar, inst):
            ach_n[0] += 1
            inst.then_inc(sem["s_ach"], 1)
            scalar.wait_ge(sem["s_ach"], ach_n[0])

        block = S0.enter_context(nc.Block())

        slice_src = []
        for nm in ("uf", "ui", "uc"):
            for o in range(KC):
                slice_src.append((nm, o))
        for o in range(KC):
            for nm in ("wf", "wi", "wc"):
                slice_src.append((nm, o))
        for o in range(KC):
            for nm in ("wf", "wi", "wc", "uf", "ui", "uc"):
                slice_src.append((nm, o))
        NSL = len(slice_src)

        def slice_owner_wait(eng, i):
            j = i - NSLOT
            if j < 0:
                return
            if j < 24:
                eng.wait_ge(sem["s_mv"], j + 1)
            elif j < 48:
                o_j = (j - 24) // 3
                eng.wait_ge(sem["s_gact"], 3 * (2 * o_j + 2))
            else:
                o_j = (j - 48) // 6
                eng.wait_ge(sem["s_gact"], 3 * (16 + 2 * o_j + 2))

        def wait_slices(eng, imax):
            if imax < 48:
                eng.wait_ge(sem["s_sl"], 16 * (imax + 1))
            else:
                eng.wait_ge(sem["s_sl"], 16 * (48 + (imax - 48) // 2 + 1))
                eng.wait_ge(sem["s_slg"], 16 * ((imax - 47) // 2))

        with ExitStack() as S1:
            x_full = S1.enter_context(nc.sbuf_tensor("x_full",
                                                     [P, NRT * D], F32))
            xn_tm = [S1.enter_context(nc.sbuf_tensor(f"xn_tm{i}", [P, D],
                                                     BF16)) for i in range(2)]
            sq_dump = S1.enter_context(nc.sbuf_tensor("sq_dump", [P, D], BF16))
            qt_sb = S1.enter_context(nc.sbuf_tensor("qt_sb", [P, KC * RANK],
                                                    BF16))
            kt_sb = S1.enter_context(nc.sbuf_tensor("kt_sb", [P, KC * RANK],
                                                    BF16))
            vo_sb = S1.enter_context(nc.sbuf_tensor("vo_sb", [P, KC * D],
                                                    BF16))
            xkc = S1.enter_context(nc.sbuf_tensor("xkc", [RA, TLOC], BF16))
            sq2 = S1.enter_context(nc.sbuf_tensor("sq2", [RANK, TLOC], BF16))
            xvc = [S1.enter_context(nc.sbuf_tensor(f"xvc{i}", [P, D], BF16))
                   for i in range(4)]
            ps_t = [S1.enter_context(nc.psum_tensor(f"ps_t{i}", [P, D], BF16))
                    for i in range(2)]
            ps_q = [S1.enter_context(nc.psum_tensor(f"ps_q{i}", [RANK, 512],
                                                    F32)) for i in range(2)]
            ps_v = [S1.enter_context(nc.psum_tensor(f"ps_v{i}", [P, 512], F32))
                    for i in range(2)]
            ps_n = [S1.enter_context(nc.psum_tensor(f"ps_n{i}", [1, 512], F32))
                    for i in range(2)]

            def p12_sync(sync):
                sync.dma_start(out=ident_sb[:], in_=ident_d).then_inc(
                    sem["s_const"], 16)
                sync.dma_start(out=ones_sb[:], in_=ones_d).then_inc(
                    sem["s_const"], 16)
                for rt in range(NRT):
                    sync.dma_start(out=x_full[:, rt * D:(rt + 1) * D],
                                   in_=x_d[rt * P:(rt + 1) * P, :]).then_inc(
                        sem["s_x"], 16)
                sync.dma_start(out=inc_f[:], in_=inc_d).then_inc(
                    sem["s_qkst"], 16)
                for rc in range(NRT):
                    sync.wait_ge(sem["s_vev"], 2 * rc + 2)
                    sync.dma_start(out=ctb_d.ap()[rc * P:(rc + 1) * P, :],
                                   in_=xvc[rc % 4][:]).then_inc(sem["s_ct"],
                                                                16)
                sync.wait_ge(sem["s_xke"], 5)
                sync.dma_start(out=ctb_d.ap()[TLOC + 2:CTR, :].rearrange(
                    "(a b) c -> a (b c)", b=2),
                    in_=xkc[0:RA, :]).then_inc(sem["s_ct"], 16)
                sync.wait_ge(sem["s_nkr"], 5)
                sync.dma_start(out=ctb_d.ap()[TLOC:TLOC + 1, 0:1],
                               in_=lmax_bf[0:1, 0:1]).then_inc(sem["s_ct"], 16)
                if dbg:
                    sync.dma_start(out=dbg_d["xkc"],
                                   in_=xkc[0:RA, :]).then_inc(sem["s_const"],
                                                              16)

            block.sync(p12_sync)

            def p1_gp(gpsimd):
                for nm, dst in (("bf", b_bf["f"]), ("bi", b_bf["i"]),
                                ("bc", b_bf["c"]), ("bo", b_bf["o"])):
                    gpsimd.dma_start(out=dst[0:1, :], in_=b_d[nm]).then_inc(
                        sem["s_wb"], 16)
                gpsimd.dma_start(out=qt_sb[:], in_=qt_d.rearrange(
                    "p k c -> p (k c)")).then_inc(sem["s_qk"], 16)
                gpsimd.dma_start(out=kt_sb[:], in_=kt_d.rearrange(
                    "p k c -> p (k c)")).then_inc(sem["s_qk"], 16)
                gpsimd.dma_start(out=inc_b[:], in_=inc_d).then_inc(
                    sem["s_qk"], 16)
                gpsimd.wait_ge(sem["s_x"], 16 * NRT)
                gpsimd.dma_start(out=vo_sb[:], in_=vo_d.rearrange(
                    "p k c -> p (k c)")).then_inc(sem["s_qk"], 16)
                for nm in CONV_W:
                    gpsimd.dma_start(
                        out=wbf_d[nm].rearrange("o p f -> (o p) f"),
                        in_=ws_d[nm].rearrange("o p k c -> (o p) (k c)"),
                    ).then_inc(sem["s_conv"], 16)
                gpsimd.dma_start(out=sq_dump[0:1, 0:2],
                                 in_=x_d[0:1, 0:2]).then_inc(sem["s_conv"], 16)

            block.gpsimd(p1_gp)

            def p2_gp(gpsimd):
                gpsimd.wait_ge(sem["s_ct"], 16 * 18)
                gpsimd.wait_ge(sem["s_qk"], 64)
                gpsimd.wait_ge(sem["s_conv"], 16 * 7)
                gpsimd.collective_compute(
                    "AllGather", ALU.bypass, replica_groups=rg,
                    ins=[ctb_d.ap().opt()], outs=[cgat_d.ap().opt()],
                ).then_inc(sem["s_cc"])

            if ph >= 3.9:
                block.gpsimd(p2_gp)

            def p1_vec(vector):
                vector.memset(eps_col[:], EPS)
                vector.memset(onesf[:], 1.0).then_inc(sem["s_const"], 1)
                for rt in range(NRT):
                    vector.wait_ge(sem["s_x"], 16 * min(rt + 3, NRT))
                    if rt == NRT - 1:
                        vector.wait_ge(sem["s_qkst"], 16)
                    vector.reduce_sum(st_sum[:, rt:rt + 1],
                                      x_full[:, rt * D:(rt + 1) * D],
                                      axis=AX.X).then_inc(sem["s_sum"], 1)
                vector.wait_ge(sem["s_sum"], NRT)
                vector.wait_ge(sem["s_sq"], NRT)
                vfence(vector, vector.tensor_scalar_mul(st_mu[:], st_sum[:],
                                                        1.0 / D))
                vfence(vector, vector.tensor_scalar_mul(st_t0[:], st_sq[:],
                                                        1.0 / D))
                vfence(vector, vector.tensor_mul(st_var[:], st_mu[:],
                                                 st_mu[:]))
                vector.tensor_sub(st_var[:], st_t0[:], st_var[:]).then_inc(
                    sem["s_var"], 1)
                for rt in range(NRT):
                    if rt == 0:
                        vector.wait_ge(sem["s_rstd"], 1)
                    if rt >= 2:
                        vector.wait_ge(sem["s_tp"], KC * (rt - 1))
                    vector.tensor_scalar(
                        xn_tm[rt % 2][:], x_full[:, rt * D:(rt + 1) * D],
                        st_mu[:, rt:rt + 1], st_rs[:, rt:rt + 1],
                        op0=ALU.subtract, op1=ALU.mult).then_inc(sem["s_ap"],
                                                                 1)

            block.vector(p1_vec)

            def p1_scl(scalar):
                for rt in range(NRT):
                    scalar.wait_ge(sem["s_x"], 16 * min(rt + 3, NRT))
                    if rt == NRT - 1:
                        scalar.wait_ge(sem["s_qkst"], 16)
                    scalar.activation(sq_dump[:], x_full[:, rt * D:(rt + 1) * D],
                                      AF.Square,
                                      accum_out=st_sq[:, rt:rt + 1]).then_inc(
                        sem["s_sq"], 1)
                scalar.wait_ge(sem["s_var"], 1)
                afence(scalar, scalar.activation(st_t0[:], st_var[:], AF.Ln,
                                                 bias=eps_col[:, 0:1]))
                scalar.activation(st_rs[:], st_t0[:], AF.Exp,
                                  scale=-0.5).then_inc(sem["s_rstd"], 1)
                for rt in range(NRT):
                    for k in range(KC):
                        g = KC * rt + k
                        scalar.wait_ge(sem["s_tp"], g + 1)
                        scalar.copy(
                            xn_fm[:, k * TLOC + rt * P:
                                  k * TLOC + (rt + 1) * P],
                            ps_t[g % 2][:, 0:P]).then_inc(sem["s_ev"], 1)

            block.scalar(p1_scl)

            def p1_tns(tensor):
                tensor.wait_ge(sem["s_const"], 32)
                for rt in range(NRT):
                    tensor.wait_ge(sem["s_ap"], rt + 1)
                    for k in range(KC):
                        g = KC * rt + k
                        if g >= 2:
                            tensor.wait_ge(sem["s_ev"], g - 1)
                        tensor.transpose(ps_t[g % 2][:, 0:P],
                                         xn_tm[rt % 2][:, k * P:(k + 1) * P],
                                         ident_sb[:]).then_inc(sem["s_tp"], 1)

            block.tensor(p1_tns)

            def p2_vec(vector):
                if ph >= 2.15:
                    for n in range(4):
                        vector.wait_ge(sem["s_p2mm"], n + 1)
                        vector.tensor_scalar_mul(
                            xq_aug[0:RANK, n * 512:(n + 1) * 512],
                            ps_q[n % 2][:], INV32).then_inc(sem["s_xqe"], 1)
                    for n in range(4):
                        vector.wait_ge(sem["s_p2mm"], 4 + n + 1)
                        vector.tensor_copy(
                            xkc[0:RANK, n * 512:(n + 1) * 512],
                            ps_q[n % 2][:]).then_inc(sem["s_xke"], 1)
                    vector.memset(xkc[RANK:RA, :], 1.0).then_inc(sem["s_xke"],
                                                                 1)
                if ph >= 2.25:
                    for n in range(4):
                        vector.wait_ge(sem["s_nk"], n + 1)
                        vector.reduce_max(mk4[0:1, n:n + 1],
                                          ps_n[n % 2][0:1, :],
                                          axis=AX.X).then_inc(sem["s_nkr"], 1)
                    vector.wait_ge(sem["s_nkr"], 4)
                    vector.reduce_max(lmax_bf[0:1, 0:1], mk4[0:1, :],
                                      axis=AX.X).then_inc(sem["s_nkr"], 1)

            block.vector(p2_vec)

            def p2_tns(tensor):
                tensor.wait_ge(sem["s_ev"], NRT * KC)
                tensor.wait_ge(sem["s_qk"], 48)
                for proj, wsb in ((0, qt_sb), (1, kt_sb)):
                    for n in range(4):
                        gi = proj * 4 + n
                        if proj == 0 and n >= 2:
                            tensor.wait_ge(sem["s_xqe"], n - 1)
                        if proj == 1 and n == 0:
                            tensor.wait_ge(sem["s_xqe"], 3)
                        if proj == 1 and n == 1:
                            tensor.wait_ge(sem["s_xqe"], 4)
                        if proj == 1 and n >= 2:
                            tensor.wait_ge(sem["s_xke"], n - 1)
                        for k in range(KC):
                            mmq = tensor.matmul(
                                ps_q[gi % 2][:],
                                wsb[:, k * RANK:(k + 1) * RANK],
                                xn_fm[:, k * TLOC + n * 512:
                                      k * TLOC + (n + 1) * 512],
                                start=(k == 0), stop=(k == KC - 1))
                        mmq.then_inc(sem["s_p2mm"], 1)
                if ph < 2.25:
                    return
                tensor.wait_ge(sem["s_sq2xk"], 1)
                for n in range(4):
                    if n >= 2:
                        tensor.wait_ge(sem["s_nkr"], n - 1)
                    tensor.matmul(ps_n[n % 2][:], ones_sb[0:RANK, 0:1],
                                  sq2[:, n * 512:(n + 1) * 512],
                                  start=True, stop=True).then_inc(sem["s_nk"],
                                                                  1)
                tensor.wait_ge(sem["s_sq2xq"], 1)
                tensor.wait_ge(sem["s_nkr"], 4)
                for n in range(4):
                    if n >= 2:
                        tensor.wait_ge(sem["s_qnl"], n - 1)
                    tensor.matmul(ps_n[n % 2][:], ones_sb[0:RANK, 0:1],
                                  sq2[:, n * 512:(n + 1) * 512],
                                  start=True, stop=True).then_inc(sem["s_nq"],
                                                                  1)
                if ph < 2.35:
                    return
                tensor.wait_ge(sem["s_qk"], 64)
                tensor.wait_ge(sem["s_conv"], 16)
                for rc in range(NRT):
                    for fh in range(2):
                        idx = 2 * rc + fh
                        if idx >= 2:
                            tensor.wait_ge(sem["s_vev"], idx - 1)
                        for k in range(KC):
                            mmv = tensor.matmul(
                                ps_v[idx % 2][:],
                                xn_fm[:, k * TLOC + rc * P:
                                      k * TLOC + (rc + 1) * P],
                                vo_sb[:, k * D + fh * 512:
                                      k * D + (fh + 1) * 512],
                                start=(k == 0), stop=(k == KC - 1))
                        mmv.then_inc(sem["s_vmm"], 1)

            if ph >= 2.15:
                block.tensor(p2_tns)

            def p2_scl(scalar):
                scalar.wait_ge(sem["s_xke"], 4)
                scalar.activation(sq2[:], xkc[0:RANK, :], AF.Square).then_inc(
                    sem["s_sq2xk"], 1)
                scalar.wait_ge(sem["s_nk"], 4)
                scalar.wait_ge(sem["s_xqe"], 4)
                scalar.activation(sq2[:], xq_aug[0:RANK, :],
                                  AF.Square).then_inc(sem["s_sq2xq"], 1)
                for n in range(4):
                    scalar.wait_ge(sem["s_nq"], n + 1)
                    scalar.activation(tmp_row[0:1, :], ps_n[n % 2][0:1, :],
                                      AF.Ln).then_inc(sem["s_qnl"], 1)
                    scalar.wait_ge(sem["s_qnl"], n + 1)
                    scalar.activation(qn_bf[0:1, n * 512:(n + 1) * 512],
                                      tmp_row[0:1, :], AF.Exp,
                                      scale=0.5).then_inc(sem["s_qnx"], 1)
                if ph < 2.35:
                    return
                for rc in range(NRT):
                    for fh in range(2):
                        idx = 2 * rc + fh
                        scalar.wait_ge(sem["s_vmm"], idx + 1)
                        if rc >= 4 and fh == 0:
                            scalar.wait_ge(sem["s_ct"], 16 * (rc - 3))
                        scalar.copy(xvc[rc % 4][:, fh * 512:(fh + 1) * 512],
                                    ps_v[idx % 2][:]).then_inc(sem["s_vev"], 1)

            if ph >= 2.25:
                block.scalar(p2_scl)

        with ExitStack() as S2:
            xk_all = S2.enter_context(nc.sbuf_tensor(
                "xk_all", [RA, NCORE * TLOC], BF16, side="right"))
            wo_sb = S2.enter_context(nc.sbuf_tensor(
                "wo_sb", [P, KC * D], BF16, side="right"))
            slb = [S2.enter_context(nc.sbuf_tensor(f"slb{i}", [P, D], BF16))
                   for i in range(NSLOT)]
            c1_f = S2.enter_context(nc.sbuf_tensor("c1_f", [P, KC * POS], F32))
            c1_b = S2.enter_context(nc.sbuf_tensor("c1_b", [P, KC * POS],
                                                   BF16))
            fig = [[S2.enter_context(nc.sbuf_tensor(f"fig{s}_{g}", [P, 512],
                                                    F32))
                    for g in range(3)] for s in range(2)]
            zps = [[S2.enter_context(nc.psum_tensor(f"zps{s}_{g}", [P, 512],
                                                    F32))
                    for g in range(3)] for s in range(2)]
            ps_r = [S2.enter_context(nc.psum_tensor(f"ps_r{i}", [1, 512],
                                                    F32)) for i in range(2)]
            zdbg = S2.enter_context(nc.sbuf_tensor(
                "zdbg", [P, 3 * 512], F32)) if (dbg and ph <= 4) else None

            def p3_sync(sync):
                sync.wait_ge(sem["s_ap"], NRT)
                sync.wait_ge(sem["s_tp"], NRT * KC)
                sync.wait_ge(sem["s_sum"], NRT)
                sync.wait_ge(sem["s_sq"], NRT)
                sync.wait_ge(sem["s_p2mm"], 8)
                for i, (nm, o) in enumerate(slice_src):
                    if i >= 48 and i % 2 == 1:
                        continue
                    if i == NSLOT:
                        sync.wait_ge(sem["s_cc"], 1)
                    sync.wait_ge(sem["s_conv"], 16 * (CIDX[nm] + 2))
                    slice_owner_wait(sync, i)
                    sync.dma_start(out=slb[i % NSLOT][:],
                                   in_=wbf_d[nm][o]).then_inc(sem["s_sl"], 16)

            if ph >= 3.4:
                block.sync(p3_sync)

            def p3_vec(vector):
                gate_names = ("f", "i", "c")

                def rf_evac(idx):
                    g, o = idx // KC, idx % KC
                    nm = gate_names[g]
                    vector.wait_ge(sem["s_mv"], idx + 1)
                    vector.wait_ge(sem["s_wb"], 16 * (g + 1))
                    vector.tensor_add(
                        rf_bf[nm][0:1, o * P:(o + 1) * P],
                        ps_r[idx % 2][0:1, 0:P],
                        b_bf[nm][0:1, o * P:(o + 1) * P]).then_inc(
                        sem["s_rfe"], 1)

                def c1_chunk(o, pc):
                    idx = 2 * o + pc
                    sset = idx % 2
                    vector.wait_ge(sem["s_gact"], 3 * (idx + 1))
                    base = o * POS + pc * 512
                    vector.tensor_scalar_mul(fig[sset][0][:], fig[sset][0][:],
                                             inc_f[:, o:o + 1])
                    vfence(vector, vector.tensor_mul(
                        fig[sset][1][:], fig[sset][1][:], fig[sset][2][:]))
                    vfence(vector, vector.tensor_add(
                        c1_f[:, base:base + 512],
                        fig[sset][0][:], fig[sset][1][:]))
                    vector.tensor_copy(c1_b[:, base:base + 512],
                                       c1_f[:, base:base + 512]).then_inc(
                        sem["s_cdve"], 1)

                def c2_chunk(o, pc):
                    idx = 16 + 2 * o + pc
                    sset = idx % 2
                    vector.wait_ge(sem["s_gact"], 3 * (idx + 1))
                    base = o * POS + pc * 512
                    vector.tensor_mul(fig[sset][0][:], fig[sset][0][:],
                                      c1_f[:, base:base + 512])
                    vfence(vector, vector.tensor_mul(
                        fig[sset][1][:], fig[sset][1][:], fig[sset][2][:]))
                    vector.tensor_add(c2_bf[:, base:base + 512],
                                      fig[sset][0][:],
                                      fig[sset][1][:]).then_inc(sem["s_cdve"],
                                                                1)

                for j in range(24):
                    rf_evac(j)
                for o in range(KC):
                    if o >= 2:
                        for pc in range(2):
                            c1_chunk(o - 2, pc)
                for o in (KC - 2, KC - 1):
                    for pc in range(2):
                        c1_chunk(o, pc)
                if ph < 3.7:
                    return
                for o in range(KC):
                    if o >= 2:
                        for pc in range(2):
                            c2_chunk(o - 2, pc)
                for o in (KC - 2, KC - 1):
                    for pc in range(2):
                        c2_chunk(o, pc)

            if ph >= 3.4:
                block.vector(p3_vec)

            def p3_tns(tensor):
                tensor.wait_ge(sem["s_qk"], 48)
                if ph >= 3.9:
                    tensor.wait_ge(sem["s_cc"], 1)
                for g in range(3):
                    for o in range(KC):
                        idx = g * KC + o
                        wait_slices(tensor, idx)
                        if idx >= 2:
                            tensor.wait_ge(sem["s_rfe"], idx - 1)
                        sl = slb[idx % NSLOT]
                        for k in range(KC):
                            mmr = tensor.matmul(
                                ps_r[idx % 2][0:1, 0:P],
                                inc_b[:, k:k + 1],
                                sl[:, k * P:(k + 1) * P],
                                start=(k == 0), stop=(k == KC - 1))
                        mmr.then_inc(sem["s_mv"], 1)
                tensor.wait_ge(sem["s_rfe"], 24)
                for o in range(KC):
                    wait_slices(tensor, 24 + 3 * o + 2)
                    for pc in range(2):
                        idx = 2 * o + pc
                        sset = idx % 2
                        if idx >= 2:
                            tensor.wait_ge(sem["s_gact"], 3 * (idx - 1))
                        for g, nm in enumerate(("f", "i", "c")):
                            sl = slb[(24 + 3 * o + g) % NSLOT]
                            for k in range(KC):
                                tensor.matmul(
                                    zps[sset][g][:],
                                    sl[:, k * P:(k + 1) * P],
                                    xn_fm[:, k * TLOC + pc * 512:
                                          k * TLOC + pc * 512 + 512],
                                    start=(k == 0), stop=False)
                            tensor.matmul(
                                zps[sset][g][:],
                                rf_bf[nm][0:1, o * P:(o + 1) * P],
                                ones_sb[0:1, :],
                                start=False, stop=True).then_inc(sem["s_zmm"],
                                                                 1)
                if ph < 3.7:
                    return
                tensor.wait_ge(sem["s_cdve"], 16)
                for o in range(KC):
                    wait_slices(tensor, 48 + 6 * o + 5)
                    for pc in range(2):
                        idx = 16 + 2 * o + pc
                        sset = idx % 2
                        tensor.wait_ge(sem["s_gact"], 3 * (idx - 1))
                        for g, nm in enumerate(("f", "i", "c")):
                            slw = slb[(48 + 6 * o + g) % NSLOT]
                            slu = slb[(48 + 6 * o + 3 + g) % NSLOT]
                            for k in range(KC):
                                tensor.matmul(
                                    zps[sset][g][:],
                                    slw[:, k * P:(k + 1) * P],
                                    xn_fm[:, k * TLOC + POS + pc * 512:
                                          k * TLOC + POS + pc * 512 + 512],
                                    start=(k == 0), stop=False)
                            for k in range(KC):
                                tensor.matmul(
                                    zps[sset][g][:],
                                    slu[:, k * P:(k + 1) * P],
                                    c1_b[:, k * POS + pc * 512:
                                         k * POS + pc * 512 + 512],
                                    start=False, stop=False)
                            tensor.matmul(
                                zps[sset][g][:],
                                b_bf[nm][0:1, o * P:(o + 1) * P],
                                ones_sb[0:1, :],
                                start=False, stop=True).then_inc(sem["s_zmm"],
                                                                 1)

            if ph >= 3.4:
                block.tensor(p3_tns)

            def p3_scl(scalar):
                for phase in range(2 if ph >= 3.7 else 1):
                    for o in range(KC):
                        for pc in range(2):
                            idx = phase * 16 + 2 * o + pc
                            sset = idx % 2
                            scalar.wait_ge(sem["s_zmm"], 3 * (idx + 1))
                            if idx >= 2:
                                scalar.wait_ge(sem["s_cdve"], idx - 1)
                            if zdbg is not None and phase == 0 and idx == 0:
                                for gg in range(3):
                                    scalar.copy(
                                        zdbg[:, gg * 512:(gg + 1) * 512],
                                        zps[sset][gg][:])
                            scalar.activation(fig[sset][0][:], zps[sset][0][:],
                                              AF.Sigmoid).then_inc(
                                sem["s_gact"], 1)
                            scalar.activation(fig[sset][1][:], zps[sset][1][:],
                                              AF.Sigmoid).then_inc(
                                sem["s_gact"], 1)
                            scalar.activation(fig[sset][2][:], zps[sset][2][:],
                                              AF.Tanh).then_inc(
                                sem["s_gact"], 1)

            if ph >= 3.4:
                block.scalar(p3_scl)

            if dbg and 3.4 <= ph <= 4:
                def pdbg_ffn(gpsimd):
                    gpsimd.wait_ge(sem["s_cdve"], 32 if ph >= 3.7 else 16)
                    gpsimd.dma_start(out=dbg_d["c1"],
                                     in_=c1_f[:, :]).then_inc(sem["s_st"], 16)
                    gpsimd.dma_start(out=dbg_d["c2"],
                                     in_=c2_bf[:, :]).then_inc(sem["s_st"], 16)
                    gpsimd.dma_start(out=dbg_d["xn"],
                                     in_=xn_fm[:, :]).then_inc(sem["s_st"], 16)
                    gpsimd.dma_start(out=dbg_d["stats"][:, 0:NRT],
                                     in_=st_mu[:, :]).then_inc(sem["s_st"], 16)
                    gpsimd.dma_start(out=dbg_d["stats"][:, NRT:2 * NRT],
                                     in_=st_rs[:, :]).then_inc(sem["s_st"], 16)
                    gpsimd.dma_start(out=dbg_d["xq"],
                                     in_=xq_aug[0:RA, :]).then_inc(sem["s_st"],
                                                                   16)
                    gpsimd.wait_ge(sem["s_st"], 16 * 6)

                block.gpsimd(pdbg_ffn)

        with ExitStack() as S3:
            uo_sb = S3.enter_context(nc.sbuf_tensor("uo_sb", [P, KC * D],
                                                    BF16))
            xv_buf = [S3.enter_context(nc.sbuf_tensor(f"xv_buf{i}",
                                                      [P, VCH * D], BF16))
                      for i in range(2)]
            p_sb = [S3.enter_context(nc.sbuf_tensor(f"p_sb{i}", [P, QBS],
                                                    BF16)) for i in range(3)]
            rl_sb = S3.enter_context(nc.sbuf_tensor("rl_sb", [P, 2], F32))
            h_sb = S3.enter_context(nc.sbuf_tensor("h_sb", [P, 2 * D], F32))
            xres = S3.enter_context(nc.sbuf_tensor("xres", [P, 2 * D], F32))
            out_b = S3.enter_context(nc.sbuf_tensor("out_b", [P, 2 * D], F32))
            o_ps = [[S3.enter_context(nc.psum_tensor(f"o_ps{q}_{f}", [P, 512],
                                                     F32))
                     for f in range(2)] for q in range(2)]
            ps_s = [S3.enter_context(nc.psum_tensor(f"ps_s{i}", [P, 512],
                                                    F32)) for i in range(2)]
            l_ps = S3.enter_context(nc.psum_tensor("l_ps", [P, 512], F32))
            ps_h = S3.enter_context(nc.psum_tensor("ps_h", [P, 512], F32))

            def p4_gp(gpsimd):
                gpsimd.wait_ge(sem["s_cc"], 1)
                for r in range(NCORE):
                    gpsimd.dma_start(
                        out=xk_all[0:RA, r * TLOC:(r + 1) * TLOC],
                        in_=cgat_d.ap()[r * CTR + TLOC + 2:
                                        (r + 1) * CTR, :].rearrange(
                            "(a b) c -> a (b c)", b=2),
                    ).then_inc(sem["s_kg"], 16)
                for r in range(NCORE):
                    gpsimd.dma_start(
                        out=mk8[0:1, r:r + 1],
                        in_=cgat_d.ap()[r * CTR + TLOC:r * CTR + TLOC + 1,
                                        0:1],
                    ).then_inc(sem["s_kg"], 16)
                gpsimd.dma_start(out=wo_sb[:], in_=wo_d.rearrange(
                    "p k c -> p (k c)")).then_inc(sem["s_wol"], 16)
                for i, (nm, o) in enumerate(slice_src):
                    if i % 2 == 0 or i < 48:
                        continue
                    slice_owner_wait(gpsimd, i)
                    gpsimd.dma_start(out=slb[i % NSLOT][:],
                                     in_=wbf_d[nm][o]).then_inc(sem["s_slg"],
                                                                16)
                gpsimd.wait_ge(sem["s_zmm"], 96)
                gpsimd.wait_ge(sem["s_cdve"], 32)
                gpsimd.dma_start(out=uo_sb[:], in_=uo_d.rearrange(
                    "p k c -> p (k c)")).then_inc(sem["s_wol"], 16)

            if ph >= 3.9:
                block.gpsimd(p4_gp)

            def p4_vec2(vector):
                vector.wait_ge(sem["s_kg"], 16 * 16)
                vector.wait_ge(sem["s_wol"], 16)
                vector.reduce_max(mkv2[0:1, 0:1], mk8[0:1, :],
                                  axis=AX.X).then_inc(sem["s_mk8r"], 1)
                vector.wait_ge(sem["s_mkv"], 1)
                vector.wait_ge(sem["s_qnx"], 4)
                vector.tensor_scalar(
                    xq_aug[RANK:RA, :], qn_bf[0:1, :], mkv[0:1, 0:1], -1.0,
                    op0=ALU.mult, op1=ALU.mult).then_inc(sem["s_mhat"], 1)

            if ph >= 3.9:
                block.vector(p4_vec2)

            def p4_scl(scalar):
                scalar.wait_ge(sem["s_mk8r"], 1)
                afence(scalar, scalar.activation(tmp_row[0:1, 0:1],
                                                 mkv2[0:1, 0:1], AF.Ln))
                scalar.activation(mkv[0:1, 0:1], tmp_row[0:1, 0:1], AF.Exp,
                                  scale=0.5).then_inc(sem["s_mkv"], 1)

            if ph >= 3.9:
                block.scalar(p4_scl)

            def p5_sync(sync):
                sync.wait_ge(sem["s_cc"], 1)
                sync.wait_ge(sem["s_zmm"], 96)
                sync.wait_ge(sem["s_cdve"], 32)
                for e in range(NEP):
                    b = e // NQB
                    for cv in range(NVC):
                        gc = e * NVC + cv
                        if gc >= 2:
                            sync.wait_ge(sem["s_vfree"], gc - 1)
                        row0 = (cv // 2) * CTR + b * POS + (cv % 2) * 512
                        sync.dma_start(
                            out=xv_buf[gc % 2].ap().rearrange(
                                "p (t f) -> p t f", f=D),
                            in_=cgat_d.ap()[row0:row0 + VCH * P, :].rearrange(
                                "(t p) f -> p t f", p=P),
                        ).then_inc(sem["s_vs"], 16)

            if ph >= 5:
                block.sync(p5_sync)

            sched = [(e, kt) for e in range(NEP) for kt in range(NKT)]

            def p5_tns(tensor):
                tensor.wait_ge(sem["s_kg"], 16 * 16)
                tensor.wait_ge(sem["s_mhat"], 1)
                tensor.wait_ge(sem["s_const"], 33)
                tensor.wait_ge(sem["s_wol"], 16)
                tensor.wait_ge(sem["s_wb"], 64)
                tensor.wait_ge(sem["s_gact"], 96)
                tensor.wait_ge(sem["s_cdve"], 32)

                def emit_smm(g):
                    e2, kt2 = sched[g]
                    b2, qb2 = e2 // NQB, e2 % NQB
                    q0 = b2 * POS + qb2 * QBS
                    r, t = kt2 // 8, kt2 % 8
                    kcol = r * TLOC + b2 * POS + t * P
                    tensor.matmul(ps_s[g % 2][:, 0:QBS],
                                  xk_all[0:RA, kcol:kcol + P],
                                  xq_aug[0:RA, q0:q0 + QBS],
                                  start=True, stop=True).then_inc(sem["s_smm"],
                                                                  1)

                emit_smm(0)
                emit_smm(1)
                for gb in range(0, len(sched), 2):
                    eb, ktb = sched[gb]
                    cvb = ktb // VCH
                    if ktb % VCH == 0:
                        tensor.wait_ge(sem["s_vs"],
                                       16 * (eb * NVC + cvb + 1))
                    tensor.wait_ge(sem["s_exp"], gb + 2)
                    if gb + 2 < len(sched):
                        emit_smm(gb + 2)
                    if gb + 3 < len(sched):
                        emit_smm(gb + 3)
                    for g in (gb, gb + 1):
                        e, kt = sched[g]
                        b, qb = e // NQB, e % NQB
                        cv = kt // VCH
                        for qs in range(2):
                            if kt == 0 and e >= 1:
                                tensor.wait_ge(sem["s_eadd"],
                                               4 * (e - 1) + 2 * (qs + 1))
                            pt = p_sb[g % 3][:, qs * P:(qs + 1) * P]
                            for fh in range(2):
                                mo = tensor.matmul(
                                    o_ps[qs][fh][:], pt,
                                    xv_buf[(e * NVC + cv) % 2][
                                        :, (kt % VCH) * D + fh * 512:
                                        (kt % VCH) * D + fh * 512 + 512],
                                    start=(kt == 0), stop=(kt == NKT - 1),
                                    skip_group_check=True)
                                if (qs == 1 and fh == 1
                                        and kt % VCH == VCH - 1):
                                    mo.then_inc(sem["s_vfree"], 1)
                        mm = tensor.matmul(
                            l_ps[0:1, 0:QBS], ones_sb[:, 0:1],
                            p_sb[g % 3][:, 0:QBS],
                            start=(kt == 0), stop=(kt == NKT - 1),
                            skip_group_check=True)
                        if kt == NKT - 1:
                            mm.then_inc(sem["s_att"], 1)
                        else:
                            mm.then_inc(sem["s_pfree"], 1)
                        if kt != NKT - 1:
                            continue
                        q0 = b * POS + qb * QBS
                        if e == 0:
                            tensor.wait_ge(sem["s_wol"], 32)
                            tensor.wait_ge(sem["s_xr"], 16)
                        for qs in range(2):
                            l0 = q0 + qs * P
                            j0 = qb * QBS + qs * P
                            for fh in range(2):
                                u = 4 * e + 2 * qs + fh
                                tensor.wait_ge(sem["s_tanh"], u)
                                for k in range(KC):
                                    mh = tensor.matmul(
                                        ps_h[:],
                                        xn_fm[:, k * TLOC + l0:
                                              k * TLOC + l0 + P],
                                        wo_sb[:, k * D + fh * 512:
                                              k * D + fh * 512 + 512],
                                        start=(k == 0), stop=False)
                                    if qs == 0 and fh == 0 and k == 0:
                                        mh.then_inc(sem["s_pfree"], 1)
                                for k in range(KC):
                                    tensor.matmul(
                                        ps_h[:],
                                        c2_bf[:, k * POS + j0:
                                              k * POS + j0 + P],
                                        uo_sb[:, k * D + fh * 512:
                                              k * D + fh * 512 + 512],
                                        start=False, stop=False)
                                tensor.matmul(
                                    ps_h[:], ones_sb[0:1, 0:P],
                                    b_bf["o"][0:1, fh * 512:(fh + 1) * 512],
                                    start=False, stop=True).then_inc(
                                    sem["s_hmm"], 1)
                                if qs == 0 and fh == 0:
                                    tensor.wait_ge(sem["s_lrec"], e + 1)
                                    tensor.transpose(l_ps[0:P, 300:301],
                                                     tmp_row[0:1, 0:P],
                                                     onesf[0:1, 0:1])
                                    tensor.transpose(
                                        l_ps[0:P, 301:302],
                                        tmp_row[0:1, P:2 * P],
                                        onesf[0:1, 0:1]).then_inc(
                                        sem["s_ltp"], 1)

            if ph >= 5:
                block.tensor(p5_tns)

            def p5_scl(scalar):
                for e in range(NEP):
                    for kt in range(NKT):
                        g = e * NKT + kt
                        scalar.wait_ge(sem["s_smm"], g + 1)
                        if g >= 3:
                            scalar.wait_ge(sem["s_pfree"], g - 2)
                        scalar.activation(p_sb[g % 3][:],
                                          ps_s[g % 2][:, 0:QBS],
                                          AF.Exp).then_inc(sem["s_exp"], 1)
                    for u in range(4):
                        qs, fh = u // 2, u % 2
                        scalar.wait_ge(sem["s_hmm"], 4 * e + u + 1)
                        if e >= 1 and u == 0:
                            scalar.wait_ge(sem["s_eadd"], 4 * e)
                        scalar.activation(
                            h_sb[:, qs * D + fh * 512:qs * D + fh * 512 + 512],
                            ps_h[:], AF.Tanh).then_inc(sem["s_tanh"], 1)

            if ph >= 5:
                block.scalar(p5_scl)

            def p5_vec(vector):
                for e in range(NEP):
                    vector.wait_ge(sem["s_att"], e + 1)
                    vector.reciprocal(tmp_row[0:1, 0:QBS],
                                      l_ps[0:1, 0:QBS]).then_inc(
                        sem["s_lrec"], 1)
                    vector.wait_ge(sem["s_ltp"], e + 1)
                    vfence(vector, vector.tensor_copy(rl_sb[:, 0:2],
                                                      l_ps[:, 300:302]))
                    for qs in range(2):
                        for fh in range(2):
                            u = 4 * e + 2 * qs + fh
                            vector.wait_ge(sem["s_tanh"], u + 1)
                            vector.scalar_tensor_tensor(
                                h_sb[:, qs * D + fh * 512:
                                     qs * D + fh * 512 + 512],
                                o_ps[qs][fh][:], rl_sb[:, qs:qs + 1],
                                h_sb[:, qs * D + fh * 512:
                                     qs * D + fh * 512 + 512],
                                op0=ALU.mult, op1=ALU.add).then_inc(
                                sem["s_eadd"], 1)

            if ph >= 5:
                block.vector(p5_vec)

            def p5_gp(gpsimd):
                gpsimd.wait_ge(sem["s_cdve"], 32)
                gpsimd.dma_start(
                    out=xres.ap().rearrange("p (t f) -> p t f", f=D),
                    in_=x_d[0:QBS, :].rearrange("(t p) f -> p t f",
                                                p=P)).then_inc(sem["s_xr"],
                                                               16)
                for e in range(NEP):
                    b, qb = e // NQB, e % NQB
                    for qs in range(2):
                        gpsimd.wait_ge(sem["s_eadd"], 4 * e + 2 * qs + 2)
                        gpsimd.wait_ge(sem["s_xr"], 16 * (e + 1))
                        if e >= 1 and qs == 0:
                            gpsimd.wait_ge(sem["s_st"], 32 * e)
                        gpsimd.tensor_add(
                            out_b[:, qs * D:(qs + 1) * D],
                            h_sb[:, qs * D:(qs + 1) * D],
                            xres[:, qs * D:(qs + 1) * D]).then_inc(
                            sem["s_pool"], 1)
                        gpsimd.wait_ge(sem["s_pool"], 2 * e + qs + 1)
                        l0 = b * POS + qb * QBS + qs * P
                        gpsimd.dma_start(
                            out=out_d[l0:l0 + P, :],
                            in_=out_b[:, qs * D:(qs + 1) * D]).then_inc(
                            sem["s_st"], 16)
                    if e + 1 < NEP:
                        b2, qb2 = (e + 1) // NQB, (e + 1) % NQB
                        x0 = b2 * POS + qb2 * QBS
                        gpsimd.dma_start(
                            out=xres.ap().rearrange("p (t f) -> p t f", f=D),
                            in_=x_d[x0:x0 + QBS, :].rearrange(
                                "(t p) f -> p t f", p=P)).then_inc(sem["s_xr"],
                                                                   16)
                gpsimd.wait_ge(sem["s_st"], 16 * 16)

            if ph >= 5:
                block.gpsimd(p5_gp)

            if dbg and ph >= 5:
                def pdbg5(gpsimd):
                    gpsimd.wait_ge(sem["s_st"], 16 * 16)
                    gpsimd.dma_start(out=dbg_d["xn"], in_=xn_fm[:, :]).then_inc(sem["s_st"], 16)
                    gpsimd.dma_start(out=dbg_d["xq"], in_=xq_aug[0:RA, :]).then_inc(sem["s_st"], 16)
                    gpsimd.dma_start(out=dbg_d["c2"], in_=c2_bf[:, :]).then_inc(sem["s_st"], 16)
                    gpsimd.dma_start(out=dbg_d["stats"][:, 0:NRT], in_=st_mu[:, :]).then_inc(sem["s_st"], 16)
                    gpsimd.dma_start(out=dbg_d["stats"][:, NRT:2 * NRT], in_=st_rs[:, :]).then_inc(sem["s_st"], 16)
                    gpsimd.dma_start(out=dbg_d["wo"], in_=wo_sb[:, :]).then_inc(sem["s_st"], 16)
                    gpsimd.dma_start(out=dbg_d["uo"], in_=uo_sb[:, :]).then_inc(sem["s_st"], 16)
                    gpsimd.dma_start(out=dbg_d["kall"], in_=xk_all[0:RA, :]).then_inc(sem["s_st"], 16)
                    gpsimd.dma_start(out=dbg_d["mkv"], in_=mkv[0:1, :]).then_inc(sem["s_st"], 16)
                    gpsimd.dma_start(out=dbg_d["hsb"], in_=h_sb[:, :]).then_inc(sem["s_st"], 16)
                    gpsimd.dma_start(out=dbg_d["kctb"][1:CROWS, :], in_=ctb_d.ap()[TLOC + 2:CTR, :].rearrange("(a b) c -> a (b c)", b=2)).then_inc(sem["s_st"], 16)
                    gpsimd.dma_start(out=dbg_d["kctb"][0:1, 0:1], in_=ctb_d.ap()[TLOC:TLOC + 1, 0:1]).then_inc(sem["s_st"], 16)
                    gpsimd.wait_ge(sem["s_st"], 16 * 28)

                block.gpsimd(pdbg5)

    return nc


_NC_CACHE = None


def kernel(**inputs):
    global _NC_CACHE
    x = np.asarray(inputs["L0_attn_x"], dtype=np.float32)
    Q = np.asarray(inputs["L0_attn_Q"], dtype=np.float32)
    K = np.asarray(inputs["L0_attn_K"], dtype=np.float32)
    VO = np.asarray(inputs["L0_attn_VO"], dtype=np.float32)
    assert x.shape == (2, 8192, 1024)

    def full_prep(w):
        return np.ascontiguousarray(
            w.reshape(KC, P, w.shape[1]).transpose(1, 0, 2))

    def slice_prep(wt):
        return np.ascontiguousarray(
            wt.reshape(KC, P, KC, P).transpose(2, 1, 0, 3))

    feed = {
        "qt": full_prep(Q.T),
        "kt": full_prep(K.T),
        "vo": full_prep(VO),
        "wo": full_prep(np.ascontiguousarray(
            np.asarray(inputs["L0_ffn_wo"], np.float32).T)),
        "uo": full_prep(np.ascontiguousarray(
            np.asarray(inputs["L0_ffn_uo"], np.float32).T)),
        "bf": np.asarray(inputs["L0_ffn_bf"], np.float32).reshape(1, D),
        "bi": np.asarray(inputs["L0_ffn_bi"], np.float32).reshape(1, D),
        "bc": np.asarray(inputs["L0_ffn_bc"], np.float32).reshape(1, D),
        "bo": np.asarray(inputs["L0_ffn_bo"], np.float32).reshape(1, D),
        "in_c": np.ascontiguousarray(
            np.asarray(inputs["L0_ffn_in_c"], np.float32).reshape(KC, P).T),
    }
    for nm in ("wf", "wi", "wc", "uf", "ui", "uc"):
        feed[nm] = slice_prep(
            np.ascontiguousarray(np.asarray(inputs[f"L0_ffn_{nm}"],
                                            np.float32).T))

    if _NC_CACHE is None:
        _NC_CACHE = build_nc()
    nc = _NC_CACHE

    in_maps = []
    for c in range(NCORE):
        sl = slice(c * POS, (c + 1) * POS)
        xs = np.ascontiguousarray(
            np.concatenate([x[0, sl], x[1, sl]], axis=0))
        m = dict(feed)
        m["x"] = xs
        in_maps.append(m)

    res = run_bass_kernel_spmd(nc, in_maps, core_ids=list(range(NCORE)))

    y = np.empty((2, 8192, 1024), dtype=np.float32)
    for c in range(NCORE):
        o = np.asarray(res.results[c]["out"])
        y[0, c * POS:(c + 1) * POS] = o[0:POS]
        y[1, c * POS:(c + 1) * POS] = o[POS:TLOC]
    return y


if __name__ == "__main__":
    nc = build_nc()
    print("built ok; instructions:", nc.n_instructions())


# version 20
# speedup vs baseline: 1.0186x; 1.0186x over previous
import numpy as np
from contextlib import ExitStack

import concourse.bass as bass
import concourse.mybir as mybir
from concourse.bass_utils import run_bass_kernel_spmd
import ml_dtypes

F32 = mybir.dt.float32
BF16 = mybir.dt.bfloat16
AF = mybir.ActivationFunctionType
ALU = mybir.AluOpType
AX = mybir.AxisListType

NCORE = 8
P = 128
D = 1024
KC = 8
TLOC = 2048
NRT = 16
POS = 1024
NB = 2
RANK = 64
RA = 65
CROWS = 66
QBS = 256
NSLOT = 12
NQB = POS // QBS
NEP = NB * NQB
NKT = 64
VCH = 4
NVC = NKT // VCH
EPS = 1e-5
INV32 = 1.0 / 32.0

CONV_W = ("uf", "ui", "uc", "wf", "wi", "wc")
CIDX = {nm: i for i, nm in enumerate(CONV_W)}


def build_nc(ph=5, dbg=False):
    nc = bass.Bass("TRN2", target_bir_lowering=False)

    x_d = nc.dram_tensor("x", [TLOC, D], F32, kind="ExternalInput").ap()
    qt_d = nc.dram_tensor("qt", [P, KC, RANK], F32, kind="ExternalInput").ap()
    kt_d = nc.dram_tensor("kt", [P, KC, RANK], F32, kind="ExternalInput").ap()
    vo_d = nc.dram_tensor("vo", [P, KC, D], F32, kind="ExternalInput").ap()
    wo_d = nc.dram_tensor("wo", [P, KC, D], F32, kind="ExternalInput").ap()
    uo_d = nc.dram_tensor("uo", [P, KC, D], F32, kind="ExternalInput").ap()
    ws_d = {}
    for nm in ("wf", "wi", "wc", "uf", "ui", "uc"):
        ws_d[nm] = nc.dram_tensor(nm, [KC, P, KC, P], F32,
                                  kind="ExternalInput").ap()
    b_d = {}
    for nm in ("bf", "bi", "bc", "bo"):
        b_d[nm] = nc.dram_tensor(nm, [1, D], F32, kind="ExternalInput").ap()
    inc_d = nc.dram_tensor("in_c", [P, KC], F32, kind="ExternalInput").ap()
    out_d = nc.dram_tensor("out", [TLOC, D], F32, kind="ExternalOutput").ap()

    wbf_d = {nm: nc.dram_tensor(f"wbf_{nm}", [KC, P, KC * P], BF16,
                                kind="Internal").ap()
             for nm in CONV_W}

    CTR = TLOC + 2 * CROWS
    ctb_d = nc.dram_tensor("ctb", [CTR, D], BF16, kind="Internal")
    cgat_d = nc.dram_tensor("cgat", [NCORE * CTR, D], BF16, kind="Internal")

    ident_d = nc.inline_tensor(np.eye(P, dtype=ml_dtypes.bfloat16),
                               "ident").ap()
    ones_d = nc.inline_tensor(np.ones((P, 512), dtype=ml_dtypes.bfloat16),
                              "onesc").ap()

    rg = [list(range(NCORE))]

    dbg_d = {}
    if dbg:
        dbg_d["xn"] = nc.dram_tensor("dbg_xn", [P, KC * TLOC], BF16,
                                     kind="ExternalOutput").ap()
        dbg_d["xq"] = nc.dram_tensor("dbg_xq", [RA, TLOC], BF16,
                                     kind="ExternalOutput").ap()
        dbg_d["c2"] = nc.dram_tensor("dbg_c2", [P, KC * POS], BF16,
                                     kind="ExternalOutput").ap()
        dbg_d["c1"] = nc.dram_tensor("dbg_c1", [P, KC * POS], F32,
                                     kind="ExternalOutput").ap()
        dbg_d["stats"] = nc.dram_tensor("dbg_stats", [P, 2 * NRT], F32,
                                        kind="ExternalOutput").ap()
        dbg_d["wo"] = nc.dram_tensor("dbg_wo", [P, KC * D], BF16,
                                     kind="ExternalOutput").ap()
        dbg_d["uo"] = nc.dram_tensor("dbg_uo", [P, KC * D], BF16,
                                     kind="ExternalOutput").ap()
        dbg_d["kall"] = nc.dram_tensor("dbg_kall", [RA, NCORE * TLOC], BF16,
                                       kind="ExternalOutput").ap()
        dbg_d["mkv"] = nc.dram_tensor("dbg_mkv", [1, 1], F32,
                                      kind="ExternalOutput").ap()
        dbg_d["hsb"] = nc.dram_tensor("dbg_hsb", [P, 2 * D], F32,
                                      kind="ExternalOutput").ap()
        dbg_d["kctb"] = nc.dram_tensor("dbg_kctb", [CROWS, TLOC], BF16,
                                       kind="ExternalOutput").ap()
        dbg_d["xkc"] = nc.dram_tensor("dbg_xkc", [RA, TLOC], BF16,
                                      kind="ExternalOutput").ap()

    with ExitStack() as S0:
        ident_sb = S0.enter_context(nc.sbuf_tensor("ident_sb", [P, P], BF16))
        ones_sb = S0.enter_context(nc.sbuf_tensor("ones_sb", [P, 512], BF16))
        xn_fm = S0.enter_context(nc.sbuf_tensor("xn_fm", [P, KC * TLOC], BF16))
        c2_bf = S0.enter_context(nc.sbuf_tensor("c2_bf", [P, KC * POS], BF16))
        xq_aug = S0.enter_context(nc.sbuf_tensor("xq_aug", [RA, TLOC], BF16))
        qn_bf = S0.enter_context(nc.sbuf_tensor("qn_bf", [1, TLOC], BF16))
        inc_f = S0.enter_context(nc.sbuf_tensor("inc_f", [P, KC], F32))
        inc_b = S0.enter_context(nc.sbuf_tensor("inc_b", [P, KC], BF16))
        b_bf = {nm: S0.enter_context(nc.sbuf_tensor(f"b_{nm}", [1, D], BF16))
                for nm in ("f", "i", "c", "o")}
        rf_bf = {nm: S0.enter_context(nc.sbuf_tensor(f"r_{nm}", [1, D], BF16))
                 for nm in ("f", "i", "c")}
        st_sum = S0.enter_context(nc.sbuf_tensor("st_sum", [P, NRT], F32))
        st_sq = S0.enter_context(nc.sbuf_tensor("st_sq", [P, NRT], F32))
        st_mu = S0.enter_context(nc.sbuf_tensor("st_mu", [P, NRT], F32))
        st_t0 = S0.enter_context(nc.sbuf_tensor("st_t0", [P, NRT], F32))
        st_var = S0.enter_context(nc.sbuf_tensor("st_var", [P, NRT], F32))
        st_rs = S0.enter_context(nc.sbuf_tensor("st_rs", [P, NRT], F32))
        tmp_row = S0.enter_context(nc.sbuf_tensor("tmp_row", [1, 512], F32))
        mk4 = S0.enter_context(nc.sbuf_tensor("mk4", [1, 4], F32))
        lmax_bf = S0.enter_context(nc.sbuf_tensor("lmax_bf", [1, 1], BF16))
        mk8 = S0.enter_context(nc.sbuf_tensor("mk8", [1, NCORE], BF16))
        mkv = S0.enter_context(nc.sbuf_tensor("mkv", [1, 1], F32))
        mkv2 = S0.enter_context(nc.sbuf_tensor("mkv2", [1, 1], F32))
        eps_col = S0.enter_context(nc.sbuf_tensor("eps_col", [P, 1], F32))

        sem_names = [
            "s_const", "s_x", "s_xb", "s_sum", "s_sq", "s_var", "s_rstd",
            "s_ap", "s_tp", "s_ev",
            "s_qkst", "s_qk", "s_conv", "s_wb", "s_slg",
            "s_p2mm", "s_xqe", "s_xke", "s_sq2xk", "s_nk", "s_nkr",
            "s_sq2xq", "s_nq", "s_qnl", "s_qnx",
            "s_vmm", "s_vev", "s_ct", "s_cc", "s_kg", "s_mk8r", "s_mkv",
            "s_mhat",
            "s_sl", "s_mv", "s_rfe",
            "s_zmm", "s_gact", "s_cdve",
            "s_wol",
            "s_smm", "s_exp", "s_pfree", "s_vs", "s_vfree",
            "s_att", "s_hmm", "s_tanh", "s_eadd", "s_pool", "s_xr", "s_st",
            "s_vch", "s_ach",
        ]
        sem = {nm: S0.enter_context(nc.semaphore(nm)) for nm in sem_names}
        vch_n = [0]
        ach_n = [0]

        def vfence(vector, inst):
            vch_n[0] += 1
            inst.then_inc(sem["s_vch"], 1)
            vector.wait_ge(sem["s_vch"], vch_n[0])

        def afence(scalar, inst):
            ach_n[0] += 1
            inst.then_inc(sem["s_ach"], 1)
            scalar.wait_ge(sem["s_ach"], ach_n[0])

        block = S0.enter_context(nc.Block())

        slice_src = []
        for nm in ("uf", "ui", "uc"):
            for o in range(KC):
                slice_src.append((nm, o))
        for o in range(KC):
            for nm in ("wf", "wi", "wc"):
                slice_src.append((nm, o))
        for o in range(KC):
            for nm in ("wf", "wi", "wc", "uf", "ui", "uc"):
                slice_src.append((nm, o))
        NSL = len(slice_src)

        def slice_owner_wait(eng, i):
            j = i - NSLOT
            if j < 0:
                return
            if j < 24:
                eng.wait_ge(sem["s_mv"], j + 1)
            elif j < 48:
                o_j = (j - 24) // 3
                eng.wait_ge(sem["s_gact"], 3 * (2 * o_j + 2))
            else:
                o_j = (j - 48) // 6
                eng.wait_ge(sem["s_gact"], 3 * (16 + 2 * o_j + 2))

        def wait_slices(eng, imax):
            if imax < 48:
                eng.wait_ge(sem["s_sl"], 16 * (imax + 1))
            else:
                eng.wait_ge(sem["s_sl"], 16 * (48 + (imax - 48) // 2 + 1))
                eng.wait_ge(sem["s_slg"], 16 * ((imax - 47) // 2))

        with ExitStack() as S1:
            x_full = S1.enter_context(nc.sbuf_tensor("x_full",
                                                     [P, NRT * D], F32))
            xn_tm = [S1.enter_context(nc.sbuf_tensor(f"xn_tm{i}", [P, D],
                                                     BF16)) for i in range(2)]
            sq_dump = S1.enter_context(nc.sbuf_tensor("sq_dump", [P, D], BF16))
            qt_sb = S1.enter_context(nc.sbuf_tensor("qt_sb", [P, KC * RANK],
                                                    BF16))
            kt_sb = S1.enter_context(nc.sbuf_tensor("kt_sb", [P, KC * RANK],
                                                    BF16))
            vo_sb = S1.enter_context(nc.sbuf_tensor("vo_sb", [P, KC * D],
                                                    BF16))
            xkc = S1.enter_context(nc.sbuf_tensor("xkc", [RA, TLOC], BF16))
            sq2 = S1.enter_context(nc.sbuf_tensor("sq2", [RANK, TLOC], BF16))
            xvc = [S1.enter_context(nc.sbuf_tensor(f"xvc{i}", [P, D], BF16))
                   for i in range(6)]
            ps_t = [S1.enter_context(nc.psum_tensor(f"ps_t{i}", [P, D], BF16))
                    for i in range(2)]
            ps_q = [S1.enter_context(nc.psum_tensor(f"ps_q{i}", [RANK, 512],
                                                    F32)) for i in range(2)]
            ps_v = [S1.enter_context(nc.psum_tensor(f"ps_v{i}", [P, 512], F32))
                    for i in range(2)]
            ps_n = [S1.enter_context(nc.psum_tensor(f"ps_n{i}", [1, 512], F32))
                    for i in range(2)]

            def p12_sync(sync):
                sync.dma_start(out=ident_sb[:], in_=ident_d).then_inc(
                    sem["s_const"], 16)
                sync.dma_start(out=ones_sb[:], in_=ones_d).then_inc(
                    sem["s_const"], 16)
                for rt in range(NRT):
                    sync.dma_start(out=x_full[:, rt * D:(rt + 1) * D],
                                   in_=x_d[rt * P:(rt + 1) * P, :]).then_inc(
                        sem["s_x"], 16)
                sync.dma_start(out=inc_f[:], in_=inc_d).then_inc(
                    sem["s_qkst"], 16)
                for rc in range(NRT):
                    sync.wait_ge(sem["s_vev"], 2 * rc + 2)
                    sync.dma_start(out=ctb_d.ap()[rc * P:(rc + 1) * P, :],
                                   in_=xvc[rc % 6][:]).then_inc(sem["s_ct"],
                                                                16)
                sync.wait_ge(sem["s_xke"], 5)
                sync.dma_start(out=ctb_d.ap()[TLOC + 2:CTR, :].rearrange(
                    "(a b) c -> a (b c)", b=2),
                    in_=xkc[0:RA, :]).then_inc(sem["s_ct"], 16)
                sync.wait_ge(sem["s_nkr"], 5)
                sync.dma_start(out=ctb_d.ap()[TLOC:TLOC + 1, 0:1],
                               in_=lmax_bf[0:1, 0:1]).then_inc(sem["s_ct"], 16)
                if dbg:
                    sync.dma_start(out=dbg_d["xkc"],
                                   in_=xkc[0:RA, :]).then_inc(sem["s_const"],
                                                              16)

            block.sync(p12_sync)

            def p1_gp(gpsimd):
                for nm, dst in (("bf", b_bf["f"]), ("bi", b_bf["i"]),
                                ("bc", b_bf["c"]), ("bo", b_bf["o"])):
                    gpsimd.dma_start(out=dst[0:1, :], in_=b_d[nm]).then_inc(
                        sem["s_wb"], 16)
                gpsimd.dma_start(out=qt_sb[:], in_=qt_d.rearrange(
                    "p k c -> p (k c)")).then_inc(sem["s_qk"], 16)
                gpsimd.dma_start(out=kt_sb[:], in_=kt_d.rearrange(
                    "p k c -> p (k c)")).then_inc(sem["s_qk"], 16)
                gpsimd.dma_start(out=inc_b[:], in_=inc_d).then_inc(
                    sem["s_qk"], 16)
                gpsimd.wait_ge(sem["s_x"], 16 * NRT)
                gpsimd.dma_start(out=vo_sb[:], in_=vo_d.rearrange(
                    "p k c -> p (k c)")).then_inc(sem["s_qk"], 16)
                for nm in CONV_W:
                    gpsimd.dma_start(
                        out=wbf_d[nm].rearrange("o p f -> (o p) f"),
                        in_=ws_d[nm].rearrange("o p k c -> (o p) (k c)"),
                    ).then_inc(sem["s_conv"], 16)
                gpsimd.dma_start(out=sq_dump[0:1, 0:2],
                                 in_=x_d[0:1, 0:2]).then_inc(sem["s_conv"], 16)

            block.gpsimd(p1_gp)

            def p2_gp(gpsimd):
                gpsimd.wait_ge(sem["s_ct"], 16 * 18)
                gpsimd.wait_ge(sem["s_qk"], 64)
                gpsimd.wait_ge(sem["s_conv"], 16 * 7)
                gpsimd.collective_compute(
                    "AllGather", ALU.bypass, replica_groups=rg,
                    ins=[ctb_d.ap().opt()], outs=[cgat_d.ap().opt()],
                ).then_inc(sem["s_cc"])

            if ph >= 3.9:
                block.gpsimd(p2_gp)

            def p1_vec(vector):
                vector.memset(eps_col[:], EPS)
                for rt in range(NRT):
                    vector.wait_ge(sem["s_x"], 16 * min(rt + 3, NRT))
                    if rt == NRT - 1:
                        vector.wait_ge(sem["s_qkst"], 16)
                    vector.reduce_sum(st_sum[:, rt:rt + 1],
                                      x_full[:, rt * D:(rt + 1) * D],
                                      axis=AX.X).then_inc(sem["s_sum"], 1)
                vector.wait_ge(sem["s_sum"], NRT)
                vector.wait_ge(sem["s_sq"], NRT)
                vfence(vector, vector.tensor_scalar_mul(st_mu[:], st_sum[:],
                                                        1.0 / D))
                vfence(vector, vector.tensor_scalar_mul(st_t0[:], st_sq[:],
                                                        1.0 / D))
                vfence(vector, vector.tensor_mul(st_var[:], st_mu[:],
                                                 st_mu[:]))
                vector.tensor_sub(st_var[:], st_t0[:], st_var[:]).then_inc(
                    sem["s_var"], 1)
                for rt in range(NRT):
                    if rt == 0:
                        vector.wait_ge(sem["s_rstd"], 1)
                    if rt >= 2:
                        vector.wait_ge(sem["s_tp"], KC * (rt - 1))
                    vector.tensor_scalar(
                        xn_tm[rt % 2][:], x_full[:, rt * D:(rt + 1) * D],
                        st_mu[:, rt:rt + 1], st_rs[:, rt:rt + 1],
                        op0=ALU.subtract, op1=ALU.mult).then_inc(sem["s_ap"],
                                                                 1)

            block.vector(p1_vec)

            def p1_scl(scalar):
                for rt in range(NRT):
                    scalar.wait_ge(sem["s_x"], 16 * min(rt + 3, NRT))
                    if rt == NRT - 1:
                        scalar.wait_ge(sem["s_qkst"], 16)
                    scalar.activation(sq_dump[:], x_full[:, rt * D:(rt + 1) * D],
                                      AF.Square,
                                      accum_out=st_sq[:, rt:rt + 1]).then_inc(
                        sem["s_sq"], 1)
                scalar.wait_ge(sem["s_var"], 1)
                afence(scalar, scalar.activation(st_t0[:], st_var[:], AF.Ln,
                                                 bias=eps_col[:, 0:1]))
                scalar.activation(st_rs[:], st_t0[:], AF.Exp,
                                  scale=-0.5).then_inc(sem["s_rstd"], 1)
                for rt in range(NRT):
                    for k in range(KC):
                        g = KC * rt + k
                        scalar.wait_ge(sem["s_tp"], g + 1)
                        scalar.copy(
                            xn_fm[:, k * TLOC + rt * P:
                                  k * TLOC + (rt + 1) * P],
                            ps_t[g % 2][:, 0:P]).then_inc(sem["s_ev"], 1)

            block.scalar(p1_scl)

            def p1_tns(tensor):
                tensor.wait_ge(sem["s_const"], 32)
                for rt in range(NRT):
                    tensor.wait_ge(sem["s_ap"], rt + 1)
                    for k in range(KC):
                        g = KC * rt + k
                        if g >= 2:
                            tensor.wait_ge(sem["s_ev"], g - 1)
                        tensor.transpose(ps_t[g % 2][:, 0:P],
                                         xn_tm[rt % 2][:, k * P:(k + 1) * P],
                                         ident_sb[:]).then_inc(sem["s_tp"], 1)

            block.tensor(p1_tns)

            def p2_vec(vector):
                if ph >= 2.15:
                    for n in range(4):
                        vector.wait_ge(sem["s_p2mm"], n + 1)
                        vector.tensor_scalar_mul(
                            xq_aug[0:RANK, n * 512:(n + 1) * 512],
                            ps_q[n % 2][:], INV32).then_inc(sem["s_xqe"], 1)
                    for n in range(4):
                        vector.wait_ge(sem["s_p2mm"], 4 + n + 1)
                        vector.tensor_copy(
                            xkc[0:RANK, n * 512:(n + 1) * 512],
                            ps_q[n % 2][:]).then_inc(sem["s_xke"], 1)
                    vector.memset(xkc[RANK:RA, :], 1.0).then_inc(sem["s_xke"],
                                                                 1)
                if ph >= 2.25:
                    for n in range(4):
                        vector.wait_ge(sem["s_nk"], n + 1)
                        vector.reduce_max(mk4[0:1, n:n + 1],
                                          ps_n[n % 2][0:1, :],
                                          axis=AX.X).then_inc(sem["s_nkr"], 1)
                    vector.wait_ge(sem["s_nkr"], 4)
                    vector.reduce_max(lmax_bf[0:1, 0:1], mk4[0:1, :],
                                      axis=AX.X).then_inc(sem["s_nkr"], 1)

            block.vector(p2_vec)

            def p2_tns(tensor):
                tensor.wait_ge(sem["s_ev"], NRT * KC)
                tensor.wait_ge(sem["s_qk"], 48)
                for proj, wsb in ((0, qt_sb), (1, kt_sb)):
                    for n in range(4):
                        gi = proj * 4 + n
                        if proj == 0 and n >= 2:
                            tensor.wait_ge(sem["s_xqe"], n - 1)
                        if proj == 1 and n == 0:
                            tensor.wait_ge(sem["s_xqe"], 3)
                        if proj == 1 and n == 1:
                            tensor.wait_ge(sem["s_xqe"], 4)
                        if proj == 1 and n >= 2:
                            tensor.wait_ge(sem["s_xke"], n - 1)
                        for k in range(KC):
                            mmq = tensor.matmul(
                                ps_q[gi % 2][:],
                                wsb[:, k * RANK:(k + 1) * RANK],
                                xn_fm[:, k * TLOC + n * 512:
                                      k * TLOC + (n + 1) * 512],
                                start=(k == 0), stop=(k == KC - 1))
                        mmq.then_inc(sem["s_p2mm"], 1)
                if ph < 2.25:
                    return
                tensor.wait_ge(sem["s_sq2xk"], 1)
                for n in range(4):
                    if n >= 2:
                        tensor.wait_ge(sem["s_nkr"], n - 1)
                    tensor.matmul(ps_n[n % 2][:], ones_sb[0:RANK, 0:1],
                                  sq2[:, n * 512:(n + 1) * 512],
                                  start=True, stop=True).then_inc(sem["s_nk"],
                                                                  1)
                tensor.wait_ge(sem["s_sq2xq"], 1)
                tensor.wait_ge(sem["s_nkr"], 4)
                for n in range(4):
                    if n >= 2:
                        tensor.wait_ge(sem["s_qnl"], n - 1)
                    tensor.matmul(ps_n[n % 2][:], ones_sb[0:RANK, 0:1],
                                  sq2[:, n * 512:(n + 1) * 512],
                                  start=True, stop=True).then_inc(sem["s_nq"],
                                                                  1)
                if ph < 2.35:
                    return
                tensor.wait_ge(sem["s_qk"], 64)
                tensor.wait_ge(sem["s_conv"], 16)
                for rc in range(NRT):
                    for fh in range(2):
                        idx = 2 * rc + fh
                        if idx >= 2:
                            tensor.wait_ge(sem["s_vev"], idx - 1)
                        for k in range(KC):
                            mmv = tensor.matmul(
                                ps_v[idx % 2][:],
                                xn_fm[:, k * TLOC + rc * P:
                                      k * TLOC + (rc + 1) * P],
                                vo_sb[:, k * D + fh * 512:
                                      k * D + (fh + 1) * 512],
                                start=(k == 0), stop=(k == KC - 1))
                        mmv.then_inc(sem["s_vmm"], 1)

            if ph >= 2.15:
                block.tensor(p2_tns)

            def p2_scl(scalar):
                scalar.wait_ge(sem["s_xke"], 4)
                scalar.activation(sq2[:], xkc[0:RANK, :], AF.Square).then_inc(
                    sem["s_sq2xk"], 1)
                scalar.wait_ge(sem["s_nk"], 4)
                scalar.wait_ge(sem["s_xqe"], 4)
                scalar.activation(sq2[:], xq_aug[0:RANK, :],
                                  AF.Square).then_inc(sem["s_sq2xq"], 1)
                for n in range(4):
                    scalar.wait_ge(sem["s_nq"], n + 1)
                    scalar.activation(tmp_row[0:1, :], ps_n[n % 2][0:1, :],
                                      AF.Ln).then_inc(sem["s_qnl"], 1)
                    scalar.wait_ge(sem["s_qnl"], n + 1)
                    scalar.activation(qn_bf[0:1, n * 512:(n + 1) * 512],
                                      tmp_row[0:1, :], AF.Exp,
                                      scale=0.5).then_inc(sem["s_qnx"], 1)
                if ph < 2.35:
                    return
                for rc in range(NRT):
                    for fh in range(2):
                        idx = 2 * rc + fh
                        scalar.wait_ge(sem["s_vmm"], idx + 1)
                        if rc >= 6 and fh == 0:
                            scalar.wait_ge(sem["s_ct"], 16 * (rc - 5))
                        scalar.copy(xvc[rc % 6][:, fh * 512:(fh + 1) * 512],
                                    ps_v[idx % 2][:]).then_inc(sem["s_vev"], 1)

            if ph >= 2.25:
                block.scalar(p2_scl)

        with ExitStack() as S2:
            xk_all = S2.enter_context(nc.sbuf_tensor(
                "xk_all", [RA, NCORE * TLOC], BF16, side="right"))
            wo_sb = S2.enter_context(nc.sbuf_tensor(
                "wo_sb", [P, KC * D], BF16, side="right"))
            slb = [S2.enter_context(nc.sbuf_tensor(f"slb{i}", [P, D], BF16))
                   for i in range(NSLOT)]
            c1_f = S2.enter_context(nc.sbuf_tensor("c1_f", [P, KC * POS], F32))
            c1_b = S2.enter_context(nc.sbuf_tensor("c1_b", [P, KC * POS],
                                                   BF16))
            fig = [[S2.enter_context(nc.sbuf_tensor(f"fig{s}_{g}", [P, 512],
                                                    F32))
                    for g in range(3)] for s in range(2)]
            zps = [[S2.enter_context(nc.psum_tensor(f"zps{s}_{g}", [P, 512],
                                                    F32))
                    for g in range(3)] for s in range(2)]
            ps_r = [S2.enter_context(nc.psum_tensor(f"ps_r{i}", [1, 512],
                                                    F32)) for i in range(2)]
            zdbg = S2.enter_context(nc.sbuf_tensor(
                "zdbg", [P, 3 * 512], F32)) if (dbg and ph <= 4) else None

            def p3_sync(sync):
                sync.wait_ge(sem["s_ap"], NRT)
                sync.wait_ge(sem["s_tp"], NRT * KC)
                sync.wait_ge(sem["s_sum"], NRT)
                sync.wait_ge(sem["s_sq"], NRT)
                sync.wait_ge(sem["s_p2mm"], 8)
                for i, (nm, o) in enumerate(slice_src):
                    if i >= 48 and i % 2 == 1:
                        continue
                    if i == NSLOT:
                        sync.wait_ge(sem["s_cc"], 1)
                    sync.wait_ge(sem["s_conv"], 16 * (CIDX[nm] + 2))
                    slice_owner_wait(sync, i)
                    sync.dma_start(out=slb[i % NSLOT][:],
                                   in_=wbf_d[nm][o]).then_inc(sem["s_sl"], 16)

            if ph >= 3.4:
                block.sync(p3_sync)

            def p3_vec(vector):
                gate_names = ("f", "i", "c")

                def rf_evac(idx):
                    g, o = idx // KC, idx % KC
                    nm = gate_names[g]
                    vector.wait_ge(sem["s_mv"], idx + 1)
                    vector.wait_ge(sem["s_wb"], 16 * (g + 1))
                    vector.tensor_add(
                        rf_bf[nm][0:1, o * P:(o + 1) * P],
                        ps_r[idx % 2][0:1, 0:P],
                        b_bf[nm][0:1, o * P:(o + 1) * P]).then_inc(
                        sem["s_rfe"], 1)

                def c1_chunk(o, pc):
                    idx = 2 * o + pc
                    sset = idx % 2
                    vector.wait_ge(sem["s_gact"], 3 * (idx + 1))
                    base = o * POS + pc * 512
                    vector.tensor_scalar_mul(fig[sset][0][:], fig[sset][0][:],
                                             inc_f[:, o:o + 1])
                    vfence(vector, vector.tensor_mul(
                        fig[sset][1][:], fig[sset][1][:], fig[sset][2][:]))
                    vfence(vector, vector.tensor_add(
                        c1_f[:, base:base + 512],
                        fig[sset][0][:], fig[sset][1][:]))
                    vector.tensor_copy(c1_b[:, base:base + 512],
                                       c1_f[:, base:base + 512]).then_inc(
                        sem["s_cdve"], 1)

                def c2_chunk(o, pc):
                    idx = 16 + 2 * o + pc
                    sset = idx % 2
                    vector.wait_ge(sem["s_gact"], 3 * (idx + 1))
                    base = o * POS + pc * 512
                    vector.tensor_mul(fig[sset][0][:], fig[sset][0][:],
                                      c1_f[:, base:base + 512])
                    vfence(vector, vector.tensor_mul(
                        fig[sset][1][:], fig[sset][1][:], fig[sset][2][:]))
                    vector.tensor_add(c2_bf[:, base:base + 512],
                                      fig[sset][0][:],
                                      fig[sset][1][:]).then_inc(sem["s_cdve"],
                                                                1)

                for j in range(24):
                    rf_evac(j)
                for o in range(KC):
                    if o >= 2:
                        for pc in range(2):
                            c1_chunk(o - 2, pc)
                for o in (KC - 2, KC - 1):
                    for pc in range(2):
                        c1_chunk(o, pc)
                if ph < 3.7:
                    return
                for o in range(KC):
                    if o >= 2:
                        for pc in range(2):
                            c2_chunk(o - 2, pc)
                for o in (KC - 2, KC - 1):
                    for pc in range(2):
                        c2_chunk(o, pc)

            if ph >= 3.4:
                block.vector(p3_vec)

            def p3_tns(tensor):
                tensor.wait_ge(sem["s_qk"], 48)
                if ph >= 3.9:
                    tensor.wait_ge(sem["s_cc"], 1)
                for g in range(3):
                    for o in range(KC):
                        idx = g * KC + o
                        wait_slices(tensor, idx)
                        if idx >= 2:
                            tensor.wait_ge(sem["s_rfe"], idx - 1)
                        sl = slb[idx % NSLOT]
                        for k in range(KC):
                            mmr = tensor.matmul(
                                ps_r[idx % 2][0:1, 0:P],
                                inc_b[:, k:k + 1],
                                sl[:, k * P:(k + 1) * P],
                                start=(k == 0), stop=(k == KC - 1))
                        mmr.then_inc(sem["s_mv"], 1)
                tensor.wait_ge(sem["s_rfe"], 24)
                for o in range(KC):
                    wait_slices(tensor, 24 + 3 * o + 2)
                    for pc in range(2):
                        idx = 2 * o + pc
                        sset = idx % 2
                        if idx >= 2:
                            tensor.wait_ge(sem["s_gact"], 3 * (idx - 1))
                        for g, nm in enumerate(("f", "i", "c")):
                            sl = slb[(24 + 3 * o + g) % NSLOT]
                            for k in range(KC):
                                tensor.matmul(
                                    zps[sset][g][:],
                                    sl[:, k * P:(k + 1) * P],
                                    xn_fm[:, k * TLOC + pc * 512:
                                          k * TLOC + pc * 512 + 512],
                                    start=(k == 0), stop=False)
                            tensor.matmul(
                                zps[sset][g][:],
                                rf_bf[nm][0:1, o * P:(o + 1) * P],
                                ones_sb[0:1, :],
                                start=False, stop=True).then_inc(sem["s_zmm"],
                                                                 1)
                if ph < 3.7:
                    return
                tensor.wait_ge(sem["s_cdve"], 16)
                for o in range(KC):
                    wait_slices(tensor, 48 + 6 * o + 5)
                    for pc in range(2):
                        idx = 16 + 2 * o + pc
                        sset = idx % 2
                        tensor.wait_ge(sem["s_gact"], 3 * (idx - 1))
                        for g, nm in enumerate(("f", "i", "c")):
                            slw = slb[(48 + 6 * o + g) % NSLOT]
                            slu = slb[(48 + 6 * o + 3 + g) % NSLOT]
                            for k in range(KC):
                                tensor.matmul(
                                    zps[sset][g][:],
                                    slw[:, k * P:(k + 1) * P],
                                    xn_fm[:, k * TLOC + POS + pc * 512:
                                          k * TLOC + POS + pc * 512 + 512],
                                    start=(k == 0), stop=False)
                            for k in range(KC):
                                tensor.matmul(
                                    zps[sset][g][:],
                                    slu[:, k * P:(k + 1) * P],
                                    c1_b[:, k * POS + pc * 512:
                                         k * POS + pc * 512 + 512],
                                    start=False, stop=False)
                            tensor.matmul(
                                zps[sset][g][:],
                                b_bf[nm][0:1, o * P:(o + 1) * P],
                                ones_sb[0:1, :],
                                start=False, stop=True).then_inc(sem["s_zmm"],
                                                                 1)

            if ph >= 3.4:
                block.tensor(p3_tns)

            def p3_scl(scalar):
                for phase in range(2 if ph >= 3.7 else 1):
                    for o in range(KC):
                        for pc in range(2):
                            idx = phase * 16 + 2 * o + pc
                            sset = idx % 2
                            scalar.wait_ge(sem["s_zmm"], 3 * (idx + 1))
                            if idx >= 2:
                                scalar.wait_ge(sem["s_cdve"], idx - 1)
                            if zdbg is not None and phase == 0 and idx == 0:
                                for gg in range(3):
                                    scalar.copy(
                                        zdbg[:, gg * 512:(gg + 1) * 512],
                                        zps[sset][gg][:])
                            scalar.activation(fig[sset][0][:], zps[sset][0][:],
                                              AF.Sigmoid).then_inc(
                                sem["s_gact"], 1)
                            scalar.activation(fig[sset][1][:], zps[sset][1][:],
                                              AF.Sigmoid).then_inc(
                                sem["s_gact"], 1)
                            scalar.activation(fig[sset][2][:], zps[sset][2][:],
                                              AF.Tanh).then_inc(
                                sem["s_gact"], 1)

            if ph >= 3.4:
                block.scalar(p3_scl)

            if dbg and 3.4 <= ph <= 4:
                def pdbg_ffn(gpsimd):
                    gpsimd.wait_ge(sem["s_cdve"], 32 if ph >= 3.7 else 16)
                    gpsimd.dma_start(out=dbg_d["c1"],
                                     in_=c1_f[:, :]).then_inc(sem["s_st"], 16)
                    gpsimd.dma_start(out=dbg_d["c2"],
                                     in_=c2_bf[:, :]).then_inc(sem["s_st"], 16)
                    gpsimd.dma_start(out=dbg_d["xn"],
                                     in_=xn_fm[:, :]).then_inc(sem["s_st"], 16)
                    gpsimd.dma_start(out=dbg_d["stats"][:, 0:NRT],
                                     in_=st_mu[:, :]).then_inc(sem["s_st"], 16)
                    gpsimd.dma_start(out=dbg_d["stats"][:, NRT:2 * NRT],
                                     in_=st_rs[:, :]).then_inc(sem["s_st"], 16)
                    gpsimd.dma_start(out=dbg_d["xq"],
                                     in_=xq_aug[0:RA, :]).then_inc(sem["s_st"],
                                                                   16)
                    gpsimd.wait_ge(sem["s_st"], 16 * 6)

                block.gpsimd(pdbg_ffn)

        with ExitStack() as S3:
            uo_sb = S3.enter_context(nc.sbuf_tensor("uo_sb", [P, KC * D],
                                                    BF16))
            xv_buf = [S3.enter_context(nc.sbuf_tensor(f"xv_buf{i}",
                                                      [P, VCH * D], BF16))
                      for i in range(2)]
            p_sb = [S3.enter_context(nc.sbuf_tensor(f"p_sb{i}", [P, QBS],
                                                    BF16)) for i in range(3)]
            rl_sb = S3.enter_context(nc.sbuf_tensor("rl_sb", [P, 2], F32))
            h_sb = S3.enter_context(nc.sbuf_tensor("h_sb", [P, 2 * D], F32))
            xres = S3.enter_context(nc.sbuf_tensor("xres", [P, 2 * D], F32))
            out_b = S3.enter_context(nc.sbuf_tensor("out_b", [P, 2 * D], F32))
            o_ps = [[S3.enter_context(nc.psum_tensor(f"o_ps{q}_{f}", [P, 512],
                                                     F32))
                     for f in range(2)] for q in range(2)]
            ps_s = [S3.enter_context(nc.psum_tensor(f"ps_s{i}", [P, 512],
                                                    F32)) for i in range(2)]
            l_ps = S3.enter_context(nc.psum_tensor("l_ps", [P, 512], F32))
            ps_h = S3.enter_context(nc.psum_tensor("ps_h", [P, 512], F32))

            def p4_gp(gpsimd):
                gpsimd.wait_ge(sem["s_cc"], 1)
                for r in range(NCORE):
                    gpsimd.dma_start(
                        out=xk_all[0:RA, r * TLOC:(r + 1) * TLOC],
                        in_=cgat_d.ap()[r * CTR + TLOC + 2:
                                        (r + 1) * CTR, :].rearrange(
                            "(a b) c -> a (b c)", b=2),
                    ).then_inc(sem["s_kg"], 16)
                for r in range(NCORE):
                    gpsimd.dma_start(
                        out=mk8[0:1, r:r + 1],
                        in_=cgat_d.ap()[r * CTR + TLOC:r * CTR + TLOC + 1,
                                        0:1],
                    ).then_inc(sem["s_kg"], 16)
                gpsimd.dma_start(out=wo_sb[:], in_=wo_d.rearrange(
                    "p k c -> p (k c)")).then_inc(sem["s_wol"], 16)
                for i, (nm, o) in enumerate(slice_src):
                    if i % 2 == 0 or i < 48:
                        continue
                    slice_owner_wait(gpsimd, i)
                    gpsimd.dma_start(out=slb[i % NSLOT][:],
                                     in_=wbf_d[nm][o]).then_inc(sem["s_slg"],
                                                                16)
                gpsimd.wait_ge(sem["s_zmm"], 96)
                gpsimd.wait_ge(sem["s_cdve"], 32)
                gpsimd.dma_start(out=uo_sb[:], in_=uo_d.rearrange(
                    "p k c -> p (k c)")).then_inc(sem["s_wol"], 16)

            if ph >= 3.9:
                block.gpsimd(p4_gp)

            def p4_vec2(vector):
                vector.wait_ge(sem["s_kg"], 16 * 16)
                vector.wait_ge(sem["s_wol"], 16)
                vector.reduce_max(mkv2[0:1, 0:1], mk8[0:1, :],
                                  axis=AX.X).then_inc(sem["s_mk8r"], 1)
                vector.wait_ge(sem["s_mkv"], 1)
                vector.wait_ge(sem["s_qnx"], 4)
                vector.tensor_scalar(
                    xq_aug[RANK:RA, :], qn_bf[0:1, :], mkv[0:1, 0:1], -1.0,
                    op0=ALU.mult, op1=ALU.mult).then_inc(sem["s_mhat"], 1)

            if ph >= 3.9:
                block.vector(p4_vec2)

            def p4_scl(scalar):
                scalar.wait_ge(sem["s_mk8r"], 1)
                afence(scalar, scalar.activation(tmp_row[0:1, 0:1],
                                                 mkv2[0:1, 0:1], AF.Ln))
                scalar.activation(mkv[0:1, 0:1], tmp_row[0:1, 0:1], AF.Exp,
                                  scale=0.5).then_inc(sem["s_mkv"], 1)

            if ph >= 3.9:
                block.scalar(p4_scl)

            def p5_sync(sync):
                sync.wait_ge(sem["s_cc"], 1)
                sync.wait_ge(sem["s_zmm"], 96)
                sync.wait_ge(sem["s_cdve"], 32)
                for e in range(NEP):
                    b = e // NQB
                    for cv in range(NVC):
                        gc = e * NVC + cv
                        if gc >= 2:
                            sync.wait_ge(sem["s_vfree"], gc - 1)
                        row0 = (cv // 2) * CTR + b * POS + (cv % 2) * 512
                        sync.dma_start(
                            out=xv_buf[gc % 2].ap().rearrange(
                                "p (t f) -> p t f", f=D),
                            in_=cgat_d.ap()[row0:row0 + VCH * P, :].rearrange(
                                "(t p) f -> p t f", p=P),
                        ).then_inc(sem["s_vs"], 16)

            if ph >= 5:
                block.sync(p5_sync)

            sched = [(e, kt) for e in range(NEP) for kt in range(NKT)]

            def p5_tns(tensor):
                tensor.wait_ge(sem["s_kg"], 16 * 16)
                tensor.wait_ge(sem["s_mhat"], 1)
                tensor.wait_ge(sem["s_wol"], 16)
                tensor.wait_ge(sem["s_wb"], 64)
                tensor.wait_ge(sem["s_gact"], 96)
                tensor.wait_ge(sem["s_cdve"], 32)

                def emit_smm(g):
                    e2, kt2 = sched[g]
                    b2, qb2 = e2 // NQB, e2 % NQB
                    q0 = b2 * POS + qb2 * QBS
                    r, t = kt2 // 8, kt2 % 8
                    kcol = r * TLOC + b2 * POS + t * P
                    tensor.matmul(ps_s[g % 2][:, 0:QBS],
                                  xk_all[0:RA, kcol:kcol + P],
                                  xq_aug[0:RA, q0:q0 + QBS],
                                  start=True, stop=True).then_inc(sem["s_smm"],
                                                                  1)

                emit_smm(0)
                emit_smm(1)
                for gb in range(0, len(sched), 2):
                    eb, ktb = sched[gb]
                    cvb = ktb // VCH
                    if ktb % VCH == 0:
                        tensor.wait_ge(sem["s_vs"],
                                       16 * (eb * NVC + cvb + 1))
                    tensor.wait_ge(sem["s_exp"], gb + 2)
                    if gb + 2 < len(sched):
                        emit_smm(gb + 2)
                    if gb + 3 < len(sched):
                        emit_smm(gb + 3)
                    for g in (gb, gb + 1):
                        e, kt = sched[g]
                        b, qb = e // NQB, e % NQB
                        cv = kt // VCH
                        for qs in range(2):
                            if kt == 0 and e >= 1:
                                tensor.wait_ge(sem["s_eadd"],
                                               4 * (e - 1) + 2 * (qs + 1))
                            pt = p_sb[g % 3][:, qs * P:(qs + 1) * P]
                            for fh in range(2):
                                mo = tensor.matmul(
                                    o_ps[qs][fh][:], pt,
                                    xv_buf[(e * NVC + cv) % 2][
                                        :, (kt % VCH) * D + fh * 512:
                                        (kt % VCH) * D + fh * 512 + 512],
                                    start=(kt == 0), stop=(kt == NKT - 1),
                                    skip_group_check=True)
                                if (qs == 1 and fh == 1
                                        and kt % VCH == VCH - 1):
                                    mo.then_inc(sem["s_vfree"], 1)
                            mm = tensor.matmul(
                                l_ps[:, qs:qs + 1], pt, ones_sb[:, 0:1],
                                start=(kt == 0), stop=(kt == NKT - 1),
                                skip_group_check=True)
                            if qs == 1:
                                if kt == NKT - 1:
                                    mm.then_inc(sem["s_att"], 1)
                                else:
                                    mm.then_inc(sem["s_pfree"], 1)
                        if kt != NKT - 1:
                            continue
                        q0 = b * POS + qb * QBS
                        if e == 0:
                            tensor.wait_ge(sem["s_wol"], 32)
                            tensor.wait_ge(sem["s_xr"], 16)
                        for qs in range(2):
                            l0 = q0 + qs * P
                            j0 = qb * QBS + qs * P
                            for fh in range(2):
                                u = 4 * e + 2 * qs + fh
                                tensor.wait_ge(sem["s_tanh"], u)
                                for k in range(KC):
                                    mh = tensor.matmul(
                                        ps_h[:],
                                        xn_fm[:, k * TLOC + l0:
                                              k * TLOC + l0 + P],
                                        wo_sb[:, k * D + fh * 512:
                                              k * D + fh * 512 + 512],
                                        start=(k == 0), stop=False)
                                    if qs == 0 and fh == 0 and k == 0:
                                        mh.then_inc(sem["s_pfree"], 1)
                                for k in range(KC):
                                    tensor.matmul(
                                        ps_h[:],
                                        c2_bf[:, k * POS + j0:
                                              k * POS + j0 + P],
                                        uo_sb[:, k * D + fh * 512:
                                              k * D + fh * 512 + 512],
                                        start=False, stop=False)
                                tensor.matmul(
                                    ps_h[:], ones_sb[0:1, 0:P],
                                    b_bf["o"][0:1, fh * 512:(fh + 1) * 512],
                                    start=False, stop=True).then_inc(
                                    sem["s_hmm"], 1)

            if ph >= 5:
                block.tensor(p5_tns)

            def p5_scl(scalar):
                for e in range(NEP):
                    for kt in range(NKT):
                        g = e * NKT + kt
                        scalar.wait_ge(sem["s_smm"], g + 1)
                        if g >= 3:
                            scalar.wait_ge(sem["s_pfree"], g - 2)
                        scalar.activation(p_sb[g % 3][:],
                                          ps_s[g % 2][:, 0:QBS],
                                          AF.Exp).then_inc(sem["s_exp"], 1)
                    for u in range(4):
                        qs, fh = u // 2, u % 2
                        scalar.wait_ge(sem["s_hmm"], 4 * e + u + 1)
                        if e >= 1 and u == 0:
                            scalar.wait_ge(sem["s_eadd"], 4 * e)
                        scalar.activation(
                            h_sb[:, qs * D + fh * 512:qs * D + fh * 512 + 512],
                            ps_h[:], AF.Tanh).then_inc(sem["s_tanh"], 1)

            if ph >= 5:
                block.scalar(p5_scl)

            def p5_vec(vector):
                for e in range(NEP):
                    vector.wait_ge(sem["s_att"], e + 1)
                    vfence(vector, vector.reciprocal(rl_sb[:], l_ps[:, 0:2]))
                    for qs in range(2):
                        for fh in range(2):
                            u = 4 * e + 2 * qs + fh
                            vector.wait_ge(sem["s_tanh"], u + 1)
                            vector.scalar_tensor_tensor(
                                h_sb[:, qs * D + fh * 512:
                                     qs * D + fh * 512 + 512],
                                o_ps[qs][fh][:], rl_sb[:, qs:qs + 1],
                                h_sb[:, qs * D + fh * 512:
                                     qs * D + fh * 512 + 512],
                                op0=ALU.mult, op1=ALU.add).then_inc(
                                sem["s_eadd"], 1)

            if ph >= 5:
                block.vector(p5_vec)

            def p5_gp(gpsimd):
                gpsimd.wait_ge(sem["s_cdve"], 32)
                gpsimd.dma_start(
                    out=xres.ap().rearrange("p (t f) -> p t f", f=D),
                    in_=x_d[0:QBS, :].rearrange("(t p) f -> p t f",
                                                p=P)).then_inc(sem["s_xr"],
                                                               16)
                for e in range(NEP):
                    b, qb = e // NQB, e % NQB
                    for qs in range(2):
                        gpsimd.wait_ge(sem["s_eadd"], 4 * e + 2 * qs + 2)
                        gpsimd.wait_ge(sem["s_xr"], 16 * (e + 1))
                        if e >= 1 and qs == 0:
                            gpsimd.wait_ge(sem["s_st"], 32 * e)
                        gpsimd.tensor_add(
                            out_b[:, qs * D:(qs + 1) * D],
                            h_sb[:, qs * D:(qs + 1) * D],
                            xres[:, qs * D:(qs + 1) * D]).then_inc(
                            sem["s_pool"], 1)
                        gpsimd.wait_ge(sem["s_pool"], 2 * e + qs + 1)
                        l0 = b * POS + qb * QBS + qs * P
                        gpsimd.dma_start(
                            out=out_d[l0:l0 + P, :],
                            in_=out_b[:, qs * D:(qs + 1) * D]).then_inc(
                            sem["s_st"], 16)
                    if e + 1 < NEP:
                        b2, qb2 = (e + 1) // NQB, (e + 1) % NQB
                        x0 = b2 * POS + qb2 * QBS
                        gpsimd.dma_start(
                            out=xres.ap().rearrange("p (t f) -> p t f", f=D),
                            in_=x_d[x0:x0 + QBS, :].rearrange(
                                "(t p) f -> p t f", p=P)).then_inc(sem["s_xr"],
                                                                   16)
                gpsimd.wait_ge(sem["s_st"], 16 * 16)

            if ph >= 5:
                block.gpsimd(p5_gp)

            if dbg and ph >= 5:
                def pdbg5(gpsimd):
                    gpsimd.wait_ge(sem["s_st"], 16 * 16)
                    gpsimd.dma_start(out=dbg_d["xn"], in_=xn_fm[:, :]).then_inc(sem["s_st"], 16)
                    gpsimd.dma_start(out=dbg_d["xq"], in_=xq_aug[0:RA, :]).then_inc(sem["s_st"], 16)
                    gpsimd.dma_start(out=dbg_d["c2"], in_=c2_bf[:, :]).then_inc(sem["s_st"], 16)
                    gpsimd.dma_start(out=dbg_d["stats"][:, 0:NRT], in_=st_mu[:, :]).then_inc(sem["s_st"], 16)
                    gpsimd.dma_start(out=dbg_d["stats"][:, NRT:2 * NRT], in_=st_rs[:, :]).then_inc(sem["s_st"], 16)
                    gpsimd.dma_start(out=dbg_d["wo"], in_=wo_sb[:, :]).then_inc(sem["s_st"], 16)
                    gpsimd.dma_start(out=dbg_d["uo"], in_=uo_sb[:, :]).then_inc(sem["s_st"], 16)
                    gpsimd.dma_start(out=dbg_d["kall"], in_=xk_all[0:RA, :]).then_inc(sem["s_st"], 16)
                    gpsimd.dma_start(out=dbg_d["mkv"], in_=mkv[0:1, :]).then_inc(sem["s_st"], 16)
                    gpsimd.dma_start(out=dbg_d["hsb"], in_=h_sb[:, :]).then_inc(sem["s_st"], 16)
                    gpsimd.dma_start(out=dbg_d["kctb"][1:CROWS, :], in_=ctb_d.ap()[TLOC + 2:CTR, :].rearrange("(a b) c -> a (b c)", b=2)).then_inc(sem["s_st"], 16)
                    gpsimd.dma_start(out=dbg_d["kctb"][0:1, 0:1], in_=ctb_d.ap()[TLOC:TLOC + 1, 0:1]).then_inc(sem["s_st"], 16)
                    gpsimd.wait_ge(sem["s_st"], 16 * 28)

                block.gpsimd(pdbg5)

    return nc


_NC_CACHE = None


def kernel(**inputs):
    global _NC_CACHE
    x = np.asarray(inputs["L0_attn_x"], dtype=np.float32)
    Q = np.asarray(inputs["L0_attn_Q"], dtype=np.float32)
    K = np.asarray(inputs["L0_attn_K"], dtype=np.float32)
    VO = np.asarray(inputs["L0_attn_VO"], dtype=np.float32)
    assert x.shape == (2, 8192, 1024)

    def full_prep(w):
        return np.ascontiguousarray(
            w.reshape(KC, P, w.shape[1]).transpose(1, 0, 2))

    def slice_prep(wt):
        return np.ascontiguousarray(
            wt.reshape(KC, P, KC, P).transpose(2, 1, 0, 3))

    feed = {
        "qt": full_prep(Q.T),
        "kt": full_prep(K.T),
        "vo": full_prep(VO),
        "wo": full_prep(np.ascontiguousarray(
            np.asarray(inputs["L0_ffn_wo"], np.float32).T)),
        "uo": full_prep(np.ascontiguousarray(
            np.asarray(inputs["L0_ffn_uo"], np.float32).T)),
        "bf": np.asarray(inputs["L0_ffn_bf"], np.float32).reshape(1, D),
        "bi": np.asarray(inputs["L0_ffn_bi"], np.float32).reshape(1, D),
        "bc": np.asarray(inputs["L0_ffn_bc"], np.float32).reshape(1, D),
        "bo": np.asarray(inputs["L0_ffn_bo"], np.float32).reshape(1, D),
        "in_c": np.ascontiguousarray(
            np.asarray(inputs["L0_ffn_in_c"], np.float32).reshape(KC, P).T),
    }
    for nm in ("wf", "wi", "wc", "uf", "ui", "uc"):
        feed[nm] = slice_prep(
            np.ascontiguousarray(np.asarray(inputs[f"L0_ffn_{nm}"],
                                            np.float32).T))

    if _NC_CACHE is None:
        _NC_CACHE = build_nc()
    nc = _NC_CACHE

    in_maps = []
    for c in range(NCORE):
        sl = slice(c * POS, (c + 1) * POS)
        xs = np.ascontiguousarray(
            np.concatenate([x[0, sl], x[1, sl]], axis=0))
        m = dict(feed)
        m["x"] = xs
        in_maps.append(m)

    res = run_bass_kernel_spmd(nc, in_maps, core_ids=list(range(NCORE)))

    y = np.empty((2, 8192, 1024), dtype=np.float32)
    for c in range(NCORE):
        o = np.asarray(res.results[c]["out"])
        y[0, c * POS:(c + 1) * POS] = o[0:POS]
        y[1, c * POS:(c + 1) * POS] = o[POS:TLOC]
    return y


if __name__ == "__main__":
    nc = build_nc()
    print("built ok; instructions:", nc.n_instructions())


# version 21
# speedup vs baseline: 1.0336x; 1.0147x over previous
import numpy as np
from contextlib import ExitStack

import concourse.bass as bass
import concourse.mybir as mybir
from concourse.bass_utils import run_bass_kernel_spmd
import ml_dtypes

F32 = mybir.dt.float32
BF16 = mybir.dt.bfloat16
AF = mybir.ActivationFunctionType
ALU = mybir.AluOpType
AX = mybir.AxisListType

NCORE = 8
P = 128
D = 1024
KC = 8
TLOC = 2048
NRT = 16
POS = 1024
NB = 2
RANK = 64
RA = 65
CROWS = 66
QBS = 256
NSLOT = 12
NQB = POS // QBS
NEP = NB * NQB
NKT = 64
VCH = 4
NVC = NKT // VCH
EPS = 1e-5
INV32 = 1.0 / 32.0

CONV_W = ("uf", "ui", "uc", "wf", "wi", "wc")
CIDX = {nm: i for i, nm in enumerate(CONV_W)}


def build_nc(ph=5, dbg=False):
    nc = bass.Bass("TRN2", target_bir_lowering=False)

    x_d = nc.dram_tensor("x", [TLOC, D], F32, kind="ExternalInput").ap()
    qt_d = nc.dram_tensor("qt", [P, KC, RANK], F32, kind="ExternalInput").ap()
    kt_d = nc.dram_tensor("kt", [P, KC, RANK], F32, kind="ExternalInput").ap()
    vo_d = nc.dram_tensor("vo", [P, KC, D], F32, kind="ExternalInput").ap()
    wo_d = nc.dram_tensor("wo", [P, KC, D], F32, kind="ExternalInput").ap()
    uo_d = nc.dram_tensor("uo", [P, KC, D], F32, kind="ExternalInput").ap()
    ws_d = {}
    for nm in ("wf", "wi", "wc", "uf", "ui", "uc"):
        ws_d[nm] = nc.dram_tensor(nm, [KC, P, KC, P], F32,
                                  kind="ExternalInput").ap()
    b_d = {}
    for nm in ("bf", "bi", "bc", "bo"):
        b_d[nm] = nc.dram_tensor(nm, [1, D], F32, kind="ExternalInput").ap()
    inc_d = nc.dram_tensor("in_c", [P, KC], F32, kind="ExternalInput").ap()
    out_d = nc.dram_tensor("out", [TLOC, D], F32, kind="ExternalOutput").ap()

    wbf_d = {nm: nc.dram_tensor(f"wbf_{nm}", [KC, P, KC * P], BF16,
                                kind="Internal").ap()
             for nm in CONV_W}

    CTR = TLOC + 2 * CROWS
    ctb_d = nc.dram_tensor("ctb", [CTR, D], BF16, kind="Internal")
    cgat_d = nc.dram_tensor("cgat", [NCORE * CTR, D], BF16, kind="Internal")

    ident_d = nc.inline_tensor(np.eye(P, dtype=ml_dtypes.bfloat16),
                               "ident").ap()
    ones_d = nc.inline_tensor(np.ones((P, 512), dtype=ml_dtypes.bfloat16),
                              "onesc").ap()

    rg = [list(range(NCORE))]

    dbg_d = {}
    if dbg:
        dbg_d["xn"] = nc.dram_tensor("dbg_xn", [P, KC * TLOC], BF16,
                                     kind="ExternalOutput").ap()
        dbg_d["xq"] = nc.dram_tensor("dbg_xq", [RA, TLOC], BF16,
                                     kind="ExternalOutput").ap()
        dbg_d["c2"] = nc.dram_tensor("dbg_c2", [P, KC * POS], BF16,
                                     kind="ExternalOutput").ap()
        dbg_d["c1"] = nc.dram_tensor("dbg_c1", [P, KC * POS], F32,
                                     kind="ExternalOutput").ap()
        dbg_d["stats"] = nc.dram_tensor("dbg_stats", [P, 2 * NRT], F32,
                                        kind="ExternalOutput").ap()
        dbg_d["wo"] = nc.dram_tensor("dbg_wo", [P, KC * D], BF16,
                                     kind="ExternalOutput").ap()
        dbg_d["uo"] = nc.dram_tensor("dbg_uo", [P, KC * D], BF16,
                                     kind="ExternalOutput").ap()
        dbg_d["kall"] = nc.dram_tensor("dbg_kall", [RA, NCORE * TLOC], BF16,
                                       kind="ExternalOutput").ap()
        dbg_d["mkv"] = nc.dram_tensor("dbg_mkv", [1, 1], F32,
                                      kind="ExternalOutput").ap()
        dbg_d["hsb"] = nc.dram_tensor("dbg_hsb", [P, 2 * D], F32,
                                      kind="ExternalOutput").ap()
        dbg_d["kctb"] = nc.dram_tensor("dbg_kctb", [CROWS, TLOC], BF16,
                                       kind="ExternalOutput").ap()
        dbg_d["xkc"] = nc.dram_tensor("dbg_xkc", [RA, TLOC], BF16,
                                      kind="ExternalOutput").ap()

    with ExitStack() as S0:
        ident_sb = S0.enter_context(nc.sbuf_tensor("ident_sb", [P, P], BF16))
        ones_sb = S0.enter_context(nc.sbuf_tensor("ones_sb", [P, 512], BF16))
        xn_fm = S0.enter_context(nc.sbuf_tensor("xn_fm", [P, KC * TLOC], BF16))
        c2_bf = S0.enter_context(nc.sbuf_tensor("c2_bf", [P, KC * POS], BF16))
        xq_aug = S0.enter_context(nc.sbuf_tensor("xq_aug", [RA, TLOC], BF16))
        qn_bf = S0.enter_context(nc.sbuf_tensor("qn_bf", [1, TLOC], BF16))
        inc_f = S0.enter_context(nc.sbuf_tensor("inc_f", [P, KC], F32))
        inc_b = S0.enter_context(nc.sbuf_tensor("inc_b", [P, KC], BF16))
        b_bf = {nm: S0.enter_context(nc.sbuf_tensor(f"b_{nm}", [1, D], BF16))
                for nm in ("f", "i", "c", "o")}
        rf_bf = {nm: S0.enter_context(nc.sbuf_tensor(f"r_{nm}", [1, D], BF16))
                 for nm in ("f", "i", "c")}
        st_sum = S0.enter_context(nc.sbuf_tensor("st_sum", [P, NRT], F32))
        st_sq = S0.enter_context(nc.sbuf_tensor("st_sq", [P, NRT], F32))
        st_mu = S0.enter_context(nc.sbuf_tensor("st_mu", [P, NRT], F32))
        st_t0 = S0.enter_context(nc.sbuf_tensor("st_t0", [P, NRT], F32))
        st_var = S0.enter_context(nc.sbuf_tensor("st_var", [P, NRT], F32))
        st_rs = S0.enter_context(nc.sbuf_tensor("st_rs", [P, NRT], F32))
        tmp_row = S0.enter_context(nc.sbuf_tensor("tmp_row", [1, 512], F32))
        mk4 = S0.enter_context(nc.sbuf_tensor("mk4", [1, 4], F32))
        lmax_bf = S0.enter_context(nc.sbuf_tensor("lmax_bf", [1, 1], BF16))
        mk8 = S0.enter_context(nc.sbuf_tensor("mk8", [1, NCORE], BF16))
        mkv = S0.enter_context(nc.sbuf_tensor("mkv", [1, 1], F32))
        mkv2 = S0.enter_context(nc.sbuf_tensor("mkv2", [1, 1], F32))
        eps_col = S0.enter_context(nc.sbuf_tensor("eps_col", [P, 1], F32))

        sem_names = [
            "s_const", "s_x", "s_xb", "s_sum", "s_sq", "s_var", "s_rstd",
            "s_ap", "s_tp", "s_ev",
            "s_qkst", "s_qk", "s_conv", "s_wb", "s_slg",
            "s_p2mm", "s_xqe", "s_xke", "s_sq2xk", "s_nk", "s_nkr",
            "s_sq2xq", "s_nq", "s_qnl", "s_qnx",
            "s_vmm", "s_vev", "s_ct", "s_cc", "s_kg", "s_mk8r", "s_mkv",
            "s_mhat",
            "s_sl", "s_mv", "s_rfe",
            "s_zmm", "s_gact", "s_cdve",
            "s_wol",
            "s_smm", "s_exp", "s_pfree", "s_vs", "s_vfree",
            "s_att", "s_hmm", "s_tanh", "s_eadd", "s_pool", "s_xr", "s_st",
            "s_vch", "s_ach",
        ]
        sem = {nm: S0.enter_context(nc.semaphore(nm)) for nm in sem_names}
        vch_n = [0]
        ach_n = [0]

        def vfence(vector, inst):
            vch_n[0] += 1
            inst.then_inc(sem["s_vch"], 1)
            vector.wait_ge(sem["s_vch"], vch_n[0])

        def afence(scalar, inst):
            ach_n[0] += 1
            inst.then_inc(sem["s_ach"], 1)
            scalar.wait_ge(sem["s_ach"], ach_n[0])

        block = S0.enter_context(nc.Block())

        slice_src = []
        for nm in ("uf", "ui", "uc"):
            for o in range(KC):
                slice_src.append((nm, o))
        for o in range(KC):
            for nm in ("wf", "wi", "wc"):
                slice_src.append((nm, o))
        for o in range(KC):
            for nm in ("wf", "wi", "wc", "uf", "ui", "uc"):
                slice_src.append((nm, o))
        NSL = len(slice_src)

        def slice_owner_wait(eng, i):
            j = i - NSLOT
            if j < 0:
                return
            if j < 24:
                eng.wait_ge(sem["s_mv"], j + 1)
            elif j < 48:
                o_j = (j - 24) // 3
                eng.wait_ge(sem["s_gact"], 3 * (2 * o_j + 2))
            else:
                o_j = (j - 48) // 6
                eng.wait_ge(sem["s_gact"], 3 * (16 + 2 * o_j + 2))

        def wait_slices(eng, imax):
            if imax < 48:
                eng.wait_ge(sem["s_sl"], 16 * (imax + 1))
            else:
                eng.wait_ge(sem["s_sl"], 16 * (48 + (imax - 48) // 2 + 1))
                eng.wait_ge(sem["s_slg"], 16 * ((imax - 47) // 2))

        with ExitStack() as S1:
            x_full = S1.enter_context(nc.sbuf_tensor("x_full",
                                                     [P, NRT * D], F32))
            xn_tm = [S1.enter_context(nc.sbuf_tensor(f"xn_tm{i}", [P, D],
                                                     BF16)) for i in range(2)]
            sq_dump = S1.enter_context(nc.sbuf_tensor("sq_dump", [P, D], BF16))
            qt_sb = S1.enter_context(nc.sbuf_tensor("qt_sb", [P, KC * RANK],
                                                    BF16))
            kt_sb = S1.enter_context(nc.sbuf_tensor("kt_sb", [P, KC * RANK],
                                                    BF16))
            vo_sb = S1.enter_context(nc.sbuf_tensor("vo_sb", [P, KC * D],
                                                    BF16))
            xkc = S1.enter_context(nc.sbuf_tensor("xkc", [RA, TLOC], BF16))
            sq2 = S1.enter_context(nc.sbuf_tensor("sq2", [RANK, TLOC], BF16))
            xvc = [S1.enter_context(nc.sbuf_tensor(f"xvc{i}", [P, D], BF16))
                   for i in range(8)]
            ps_t = [S1.enter_context(nc.psum_tensor(f"ps_t{i}", [P, D], BF16))
                    for i in range(2)]
            ps_q = [S1.enter_context(nc.psum_tensor(f"ps_q{i}", [RANK, 512],
                                                    F32)) for i in range(2)]
            ps_v = [S1.enter_context(nc.psum_tensor(f"ps_v{i}", [P, 512], F32))
                    for i in range(2)]
            ps_n = [S1.enter_context(nc.psum_tensor(f"ps_n{i}", [1, 512], F32))
                    for i in range(2)]

            def p12_sync(sync):
                sync.dma_start(out=ident_sb[:], in_=ident_d).then_inc(
                    sem["s_const"], 16)
                sync.dma_start(out=ones_sb[:], in_=ones_d).then_inc(
                    sem["s_const"], 16)
                for rt in range(NRT):
                    sync.dma_start(out=x_full[:, rt * D:(rt + 1) * D],
                                   in_=x_d[rt * P:(rt + 1) * P, :]).then_inc(
                        sem["s_x"], 16)
                sync.dma_start(out=inc_f[:], in_=inc_d).then_inc(
                    sem["s_qkst"], 16)
                for rc in range(NRT):
                    sync.wait_ge(sem["s_vev"], 2 * rc + 2)
                    sync.dma_start(out=ctb_d.ap()[rc * P:(rc + 1) * P, :],
                                   in_=xvc[rc % 8][:]).then_inc(sem["s_ct"],
                                                                16)
                sync.wait_ge(sem["s_xke"], 5)
                sync.dma_start(out=ctb_d.ap()[TLOC + 2:CTR, :].rearrange(
                    "(a b) c -> a (b c)", b=2),
                    in_=xkc[0:RA, :]).then_inc(sem["s_ct"], 16)
                sync.wait_ge(sem["s_nkr"], 5)
                sync.dma_start(out=ctb_d.ap()[TLOC:TLOC + 1, 0:1],
                               in_=lmax_bf[0:1, 0:1]).then_inc(sem["s_ct"], 16)
                if dbg:
                    sync.dma_start(out=dbg_d["xkc"],
                                   in_=xkc[0:RA, :]).then_inc(sem["s_const"],
                                                              16)

            block.sync(p12_sync)

            def p1_gp(gpsimd):
                for nm, dst in (("bf", b_bf["f"]), ("bi", b_bf["i"]),
                                ("bc", b_bf["c"]), ("bo", b_bf["o"])):
                    gpsimd.dma_start(out=dst[0:1, :], in_=b_d[nm]).then_inc(
                        sem["s_wb"], 16)
                gpsimd.dma_start(out=qt_sb[:], in_=qt_d.rearrange(
                    "p k c -> p (k c)")).then_inc(sem["s_qk"], 16)
                gpsimd.dma_start(out=kt_sb[:], in_=kt_d.rearrange(
                    "p k c -> p (k c)")).then_inc(sem["s_qk"], 16)
                gpsimd.dma_start(out=inc_b[:], in_=inc_d).then_inc(
                    sem["s_qk"], 16)
                gpsimd.wait_ge(sem["s_x"], 16 * NRT)
                gpsimd.dma_start(out=vo_sb[:], in_=vo_d.rearrange(
                    "p k c -> p (k c)")).then_inc(sem["s_qk"], 16)
                for nm in CONV_W:
                    gpsimd.dma_start(
                        out=wbf_d[nm].rearrange("o p f -> (o p) f"),
                        in_=ws_d[nm].rearrange("o p k c -> (o p) (k c)"),
                    ).then_inc(sem["s_conv"], 16)
                gpsimd.dma_start(out=sq_dump[0:1, 0:2],
                                 in_=x_d[0:1, 0:2]).then_inc(sem["s_conv"], 16)

            block.gpsimd(p1_gp)

            def p2_gp(gpsimd):
                gpsimd.wait_ge(sem["s_ct"], 16 * 18)
                gpsimd.wait_ge(sem["s_qk"], 64)
                gpsimd.wait_ge(sem["s_conv"], 16 * 7)
                gpsimd.collective_compute(
                    "AllGather", ALU.bypass, replica_groups=rg,
                    ins=[ctb_d.ap().opt()], outs=[cgat_d.ap().opt()],
                ).then_inc(sem["s_cc"])

            if ph >= 3.9:
                block.gpsimd(p2_gp)

            def p1_vec(vector):
                vector.memset(eps_col[:], EPS)
                for rt in range(NRT):
                    vector.wait_ge(sem["s_x"], 16 * min(rt + 3, NRT))
                    if rt == NRT - 1:
                        vector.wait_ge(sem["s_qkst"], 16)
                    vector.reduce_sum(st_sum[:, rt:rt + 1],
                                      x_full[:, rt * D:(rt + 1) * D],
                                      axis=AX.X).then_inc(sem["s_sum"], 1)
                vector.wait_ge(sem["s_sum"], NRT)
                vector.wait_ge(sem["s_sq"], NRT)
                vfence(vector, vector.tensor_scalar_mul(st_mu[:], st_sum[:],
                                                        1.0 / D))
                vfence(vector, vector.tensor_scalar_mul(st_t0[:], st_sq[:],
                                                        1.0 / D))
                vfence(vector, vector.tensor_mul(st_var[:], st_mu[:],
                                                 st_mu[:]))
                vector.tensor_sub(st_var[:], st_t0[:], st_var[:]).then_inc(
                    sem["s_var"], 1)
                for rt in range(NRT):
                    if rt == 0:
                        vector.wait_ge(sem["s_rstd"], 1)
                    if rt >= 2:
                        vector.wait_ge(sem["s_tp"], KC * (rt - 1))
                    vector.tensor_scalar(
                        xn_tm[rt % 2][:], x_full[:, rt * D:(rt + 1) * D],
                        st_mu[:, rt:rt + 1], st_rs[:, rt:rt + 1],
                        op0=ALU.subtract, op1=ALU.mult).then_inc(sem["s_ap"],
                                                                 1)

            block.vector(p1_vec)

            def p1_scl(scalar):
                for rt in range(NRT):
                    scalar.wait_ge(sem["s_x"], 16 * min(rt + 3, NRT))
                    if rt == NRT - 1:
                        scalar.wait_ge(sem["s_qkst"], 16)
                    scalar.activation(sq_dump[:], x_full[:, rt * D:(rt + 1) * D],
                                      AF.Square,
                                      accum_out=st_sq[:, rt:rt + 1]).then_inc(
                        sem["s_sq"], 1)
                scalar.wait_ge(sem["s_var"], 1)
                afence(scalar, scalar.activation(st_t0[:], st_var[:], AF.Ln,
                                                 bias=eps_col[:, 0:1]))
                scalar.activation(st_rs[:], st_t0[:], AF.Exp,
                                  scale=-0.5).then_inc(sem["s_rstd"], 1)
                for rt in range(NRT):
                    for k in range(KC):
                        g = KC * rt + k
                        scalar.wait_ge(sem["s_tp"], g + 1)
                        scalar.copy(
                            xn_fm[:, k * TLOC + rt * P:
                                  k * TLOC + (rt + 1) * P],
                            ps_t[g % 2][:, 0:P]).then_inc(sem["s_ev"], 1)

            block.scalar(p1_scl)

            def p1_tns(tensor):
                tensor.wait_ge(sem["s_const"], 32)
                for rt in range(NRT):
                    tensor.wait_ge(sem["s_ap"], rt + 1)
                    for k in range(KC):
                        g = KC * rt + k
                        if g >= 2:
                            tensor.wait_ge(sem["s_ev"], g - 1)
                        tensor.transpose(ps_t[g % 2][:, 0:P],
                                         xn_tm[rt % 2][:, k * P:(k + 1) * P],
                                         ident_sb[:]).then_inc(sem["s_tp"], 1)

            block.tensor(p1_tns)

            def p2_vec(vector):
                if ph >= 2.15:
                    for n in range(4):
                        vector.wait_ge(sem["s_p2mm"], n + 1)
                        vector.tensor_scalar_mul(
                            xq_aug[0:RANK, n * 512:(n + 1) * 512],
                            ps_q[n % 2][:], INV32).then_inc(sem["s_xqe"], 1)
                    for n in range(4):
                        vector.wait_ge(sem["s_p2mm"], 4 + n + 1)
                        vector.tensor_copy(
                            xkc[0:RANK, n * 512:(n + 1) * 512],
                            ps_q[n % 2][:]).then_inc(sem["s_xke"], 1)
                    vector.memset(xkc[RANK:RA, :], 1.0).then_inc(sem["s_xke"],
                                                                 1)
                if ph >= 2.25:
                    for n in range(4):
                        vector.wait_ge(sem["s_nk"], n + 1)
                        vector.reduce_max(mk4[0:1, n:n + 1],
                                          ps_n[n % 2][0:1, :],
                                          axis=AX.X).then_inc(sem["s_nkr"], 1)
                    vector.wait_ge(sem["s_nkr"], 4)
                    vector.reduce_max(lmax_bf[0:1, 0:1], mk4[0:1, :],
                                      axis=AX.X).then_inc(sem["s_nkr"], 1)

            block.vector(p2_vec)

            def p2_tns(tensor):
                tensor.wait_ge(sem["s_ev"], NRT * KC)
                tensor.wait_ge(sem["s_qk"], 48)
                for proj, wsb in ((0, qt_sb), (1, kt_sb)):
                    for n in range(4):
                        gi = proj * 4 + n
                        if proj == 0 and n >= 2:
                            tensor.wait_ge(sem["s_xqe"], n - 1)
                        if proj == 1 and n == 0:
                            tensor.wait_ge(sem["s_xqe"], 3)
                        if proj == 1 and n == 1:
                            tensor.wait_ge(sem["s_xqe"], 4)
                        if proj == 1 and n >= 2:
                            tensor.wait_ge(sem["s_xke"], n - 1)
                        for k in range(KC):
                            mmq = tensor.matmul(
                                ps_q[gi % 2][:],
                                wsb[:, k * RANK:(k + 1) * RANK],
                                xn_fm[:, k * TLOC + n * 512:
                                      k * TLOC + (n + 1) * 512],
                                start=(k == 0), stop=(k == KC - 1))
                        mmq.then_inc(sem["s_p2mm"], 1)
                if ph < 2.25:
                    return
                tensor.wait_ge(sem["s_sq2xk"], 1)
                for n in range(4):
                    if n >= 2:
                        tensor.wait_ge(sem["s_nkr"], n - 1)
                    tensor.matmul(ps_n[n % 2][:], ones_sb[0:RANK, 0:1],
                                  sq2[:, n * 512:(n + 1) * 512],
                                  start=True, stop=True).then_inc(sem["s_nk"],
                                                                  1)
                tensor.wait_ge(sem["s_sq2xq"], 1)
                tensor.wait_ge(sem["s_nkr"], 4)
                for n in range(4):
                    if n >= 2:
                        tensor.wait_ge(sem["s_qnl"], n - 1)
                    tensor.matmul(ps_n[n % 2][:], ones_sb[0:RANK, 0:1],
                                  sq2[:, n * 512:(n + 1) * 512],
                                  start=True, stop=True).then_inc(sem["s_nq"],
                                                                  1)
                if ph < 2.35:
                    return
                tensor.wait_ge(sem["s_qk"], 64)
                tensor.wait_ge(sem["s_conv"], 16)
                for rc in range(NRT):
                    for fh in range(2):
                        idx = 2 * rc + fh
                        if idx >= 2:
                            tensor.wait_ge(sem["s_vev"], idx - 1)
                        for k in range(KC):
                            mmv = tensor.matmul(
                                ps_v[idx % 2][:],
                                xn_fm[:, k * TLOC + rc * P:
                                      k * TLOC + (rc + 1) * P],
                                vo_sb[:, k * D + fh * 512:
                                      k * D + (fh + 1) * 512],
                                start=(k == 0), stop=(k == KC - 1))
                        mmv.then_inc(sem["s_vmm"], 1)

            if ph >= 2.15:
                block.tensor(p2_tns)

            def p2_scl(scalar):
                scalar.wait_ge(sem["s_xke"], 4)
                scalar.activation(sq2[:], xkc[0:RANK, :], AF.Square).then_inc(
                    sem["s_sq2xk"], 1)
                scalar.wait_ge(sem["s_nk"], 4)
                scalar.wait_ge(sem["s_xqe"], 4)
                scalar.activation(sq2[:], xq_aug[0:RANK, :],
                                  AF.Square).then_inc(sem["s_sq2xq"], 1)
                for n in range(4):
                    scalar.wait_ge(sem["s_nq"], n + 1)
                    scalar.activation(tmp_row[0:1, :], ps_n[n % 2][0:1, :],
                                      AF.Ln).then_inc(sem["s_qnl"], 1)
                    scalar.wait_ge(sem["s_qnl"], n + 1)
                    scalar.activation(qn_bf[0:1, n * 512:(n + 1) * 512],
                                      tmp_row[0:1, :], AF.Exp,
                                      scale=0.5).then_inc(sem["s_qnx"], 1)
                if ph < 2.35:
                    return
                for rc in range(NRT):
                    for fh in range(2):
                        idx = 2 * rc + fh
                        scalar.wait_ge(sem["s_vmm"], idx + 1)
                        if rc >= 8 and fh == 0:
                            scalar.wait_ge(sem["s_ct"], 16 * (rc - 7))
                        scalar.copy(xvc[rc % 8][:, fh * 512:(fh + 1) * 512],
                                    ps_v[idx % 2][:]).then_inc(sem["s_vev"], 1)

            if ph >= 2.25:
                block.scalar(p2_scl)

        with ExitStack() as S2:
            xk_all = S2.enter_context(nc.sbuf_tensor(
                "xk_all", [RA, NCORE * TLOC], BF16, side="right"))
            wo_sb = S2.enter_context(nc.sbuf_tensor(
                "wo_sb", [P, KC * D], BF16, side="right"))
            slb = [S2.enter_context(nc.sbuf_tensor(f"slb{i}", [P, D], BF16))
                   for i in range(NSLOT)]
            c1_f = S2.enter_context(nc.sbuf_tensor("c1_f", [P, KC * POS], F32))
            c1_b = S2.enter_context(nc.sbuf_tensor("c1_b", [P, KC * POS],
                                                   BF16))
            fig = [[S2.enter_context(nc.sbuf_tensor(f"fig{s}_{g}", [P, 512],
                                                    F32))
                    for g in range(3)] for s in range(2)]
            zps = [[S2.enter_context(nc.psum_tensor(f"zps{s}_{g}", [P, 512],
                                                    F32))
                    for g in range(3)] for s in range(2)]
            ps_r = [S2.enter_context(nc.psum_tensor(f"ps_r{i}", [1, 512],
                                                    F32)) for i in range(2)]
            zdbg = S2.enter_context(nc.sbuf_tensor(
                "zdbg", [P, 3 * 512], F32)) if (dbg and ph <= 4) else None

            def p3_sync(sync):
                sync.wait_ge(sem["s_ap"], NRT)
                sync.wait_ge(sem["s_tp"], NRT * KC)
                sync.wait_ge(sem["s_sum"], NRT)
                sync.wait_ge(sem["s_sq"], NRT)
                sync.wait_ge(sem["s_p2mm"], 8)
                for i, (nm, o) in enumerate(slice_src):
                    if i >= 48 and i % 2 == 1:
                        continue
                    if i == NSLOT:
                        sync.wait_ge(sem["s_cc"], 1)
                    sync.wait_ge(sem["s_conv"], 16 * (CIDX[nm] + 2))
                    slice_owner_wait(sync, i)
                    sync.dma_start(out=slb[i % NSLOT][:],
                                   in_=wbf_d[nm][o]).then_inc(sem["s_sl"], 16)

            if ph >= 3.4:
                block.sync(p3_sync)

            def p3_vec(vector):
                gate_names = ("f", "i", "c")

                def rf_evac(idx):
                    g, o = idx // KC, idx % KC
                    nm = gate_names[g]
                    vector.wait_ge(sem["s_mv"], idx + 1)
                    vector.wait_ge(sem["s_wb"], 16 * (g + 1))
                    vector.tensor_add(
                        rf_bf[nm][0:1, o * P:(o + 1) * P],
                        ps_r[idx % 2][0:1, 0:P],
                        b_bf[nm][0:1, o * P:(o + 1) * P]).then_inc(
                        sem["s_rfe"], 1)

                def c1_chunk(o, pc):
                    idx = 2 * o + pc
                    sset = idx % 2
                    vector.wait_ge(sem["s_gact"], 3 * (idx + 1))
                    base = o * POS + pc * 512
                    vector.tensor_scalar_mul(fig[sset][0][:], fig[sset][0][:],
                                             inc_f[:, o:o + 1])
                    vfence(vector, vector.tensor_mul(
                        fig[sset][1][:], fig[sset][1][:], fig[sset][2][:]))
                    vfence(vector, vector.tensor_add(
                        c1_f[:, base:base + 512],
                        fig[sset][0][:], fig[sset][1][:]))
                    vector.tensor_copy(c1_b[:, base:base + 512],
                                       c1_f[:, base:base + 512]).then_inc(
                        sem["s_cdve"], 1)

                def c2_chunk(o, pc):
                    idx = 16 + 2 * o + pc
                    sset = idx % 2
                    vector.wait_ge(sem["s_gact"], 3 * (idx + 1))
                    base = o * POS + pc * 512
                    vector.tensor_mul(fig[sset][0][:], fig[sset][0][:],
                                      c1_f[:, base:base + 512])
                    vfence(vector, vector.tensor_mul(
                        fig[sset][1][:], fig[sset][1][:], fig[sset][2][:]))
                    vector.tensor_add(c2_bf[:, base:base + 512],
                                      fig[sset][0][:],
                                      fig[sset][1][:]).then_inc(sem["s_cdve"],
                                                                1)

                for j in range(24):
                    rf_evac(j)
                for o in range(KC):
                    if o >= 2:
                        for pc in range(2):
                            c1_chunk(o - 2, pc)
                for o in (KC - 2, KC - 1):
                    for pc in range(2):
                        c1_chunk(o, pc)
                if ph < 3.7:
                    return
                for o in range(KC):
                    if o >= 2:
                        for pc in range(2):
                            c2_chunk(o - 2, pc)
                for o in (KC - 2, KC - 1):
                    for pc in range(2):
                        c2_chunk(o, pc)

            if ph >= 3.4:
                block.vector(p3_vec)

            def p3_tns(tensor):
                tensor.wait_ge(sem["s_qk"], 48)
                if ph >= 3.9:
                    tensor.wait_ge(sem["s_cc"], 1)
                for g in range(3):
                    for o in range(KC):
                        idx = g * KC + o
                        wait_slices(tensor, idx)
                        if idx >= 2:
                            tensor.wait_ge(sem["s_rfe"], idx - 1)
                        sl = slb[idx % NSLOT]
                        for k in range(KC):
                            mmr = tensor.matmul(
                                ps_r[idx % 2][0:1, 0:P],
                                inc_b[:, k:k + 1],
                                sl[:, k * P:(k + 1) * P],
                                start=(k == 0), stop=(k == KC - 1))
                        mmr.then_inc(sem["s_mv"], 1)
                tensor.wait_ge(sem["s_rfe"], 24)
                for o in range(KC):
                    wait_slices(tensor, 24 + 3 * o + 2)
                    for pc in range(2):
                        idx = 2 * o + pc
                        sset = idx % 2
                        if idx >= 2:
                            tensor.wait_ge(sem["s_gact"], 3 * (idx - 1))
                        for g, nm in enumerate(("f", "i", "c")):
                            sl = slb[(24 + 3 * o + g) % NSLOT]
                            for k in range(KC):
                                tensor.matmul(
                                    zps[sset][g][:],
                                    sl[:, k * P:(k + 1) * P],
                                    xn_fm[:, k * TLOC + pc * 512:
                                          k * TLOC + pc * 512 + 512],
                                    start=(k == 0), stop=False)
                            tensor.matmul(
                                zps[sset][g][:],
                                rf_bf[nm][0:1, o * P:(o + 1) * P],
                                ones_sb[0:1, :],
                                start=False, stop=True).then_inc(sem["s_zmm"],
                                                                 1)
                if ph < 3.7:
                    return
                tensor.wait_ge(sem["s_cdve"], 16)
                for o in range(KC):
                    wait_slices(tensor, 48 + 6 * o + 5)
                    for pc in range(2):
                        idx = 16 + 2 * o + pc
                        sset = idx % 2
                        tensor.wait_ge(sem["s_gact"], 3 * (idx - 1))
                        for g, nm in enumerate(("f", "i", "c")):
                            slw = slb[(48 + 6 * o + g) % NSLOT]
                            slu = slb[(48 + 6 * o + 3 + g) % NSLOT]
                            for k in range(KC):
                                tensor.matmul(
                                    zps[sset][g][:],
                                    slw[:, k * P:(k + 1) * P],
                                    xn_fm[:, k * TLOC + POS + pc * 512:
                                          k * TLOC + POS + pc * 512 + 512],
                                    start=(k == 0), stop=False)
                            for k in range(KC):
                                tensor.matmul(
                                    zps[sset][g][:],
                                    slu[:, k * P:(k + 1) * P],
                                    c1_b[:, k * POS + pc * 512:
                                         k * POS + pc * 512 + 512],
                                    start=False, stop=False)
                            tensor.matmul(
                                zps[sset][g][:],
                                b_bf[nm][0:1, o * P:(o + 1) * P],
                                ones_sb[0:1, :],
                                start=False, stop=True).then_inc(sem["s_zmm"],
                                                                 1)

            if ph >= 3.4:
                block.tensor(p3_tns)

            def p3_scl(scalar):
                for phase in range(2 if ph >= 3.7 else 1):
                    for o in range(KC):
                        for pc in range(2):
                            idx = phase * 16 + 2 * o + pc
                            sset = idx % 2
                            scalar.wait_ge(sem["s_zmm"], 3 * (idx + 1))
                            if idx >= 2:
                                scalar.wait_ge(sem["s_cdve"], idx - 1)
                            if zdbg is not None and phase == 0 and idx == 0:
                                for gg in range(3):
                                    scalar.copy(
                                        zdbg[:, gg * 512:(gg + 1) * 512],
                                        zps[sset][gg][:])
                            scalar.activation(fig[sset][0][:], zps[sset][0][:],
                                              AF.Sigmoid).then_inc(
                                sem["s_gact"], 1)
                            scalar.activation(fig[sset][1][:], zps[sset][1][:],
                                              AF.Sigmoid).then_inc(
                                sem["s_gact"], 1)
                            scalar.activation(fig[sset][2][:], zps[sset][2][:],
                                              AF.Tanh).then_inc(
                                sem["s_gact"], 1)

            if ph >= 3.4:
                block.scalar(p3_scl)

            if dbg and 3.4 <= ph <= 4:
                def pdbg_ffn(gpsimd):
                    gpsimd.wait_ge(sem["s_cdve"], 32 if ph >= 3.7 else 16)
                    gpsimd.dma_start(out=dbg_d["c1"],
                                     in_=c1_f[:, :]).then_inc(sem["s_st"], 16)
                    gpsimd.dma_start(out=dbg_d["c2"],
                                     in_=c2_bf[:, :]).then_inc(sem["s_st"], 16)
                    gpsimd.dma_start(out=dbg_d["xn"],
                                     in_=xn_fm[:, :]).then_inc(sem["s_st"], 16)
                    gpsimd.dma_start(out=dbg_d["stats"][:, 0:NRT],
                                     in_=st_mu[:, :]).then_inc(sem["s_st"], 16)
                    gpsimd.dma_start(out=dbg_d["stats"][:, NRT:2 * NRT],
                                     in_=st_rs[:, :]).then_inc(sem["s_st"], 16)
                    gpsimd.dma_start(out=dbg_d["xq"],
                                     in_=xq_aug[0:RA, :]).then_inc(sem["s_st"],
                                                                   16)
                    gpsimd.wait_ge(sem["s_st"], 16 * 6)

                block.gpsimd(pdbg_ffn)

        with ExitStack() as S3:
            uo_sb = S3.enter_context(nc.sbuf_tensor("uo_sb", [P, KC * D],
                                                    BF16))
            xv_buf = [S3.enter_context(nc.sbuf_tensor(f"xv_buf{i}",
                                                      [P, VCH * D], BF16))
                      for i in range(2)]
            p_sb = [S3.enter_context(nc.sbuf_tensor(f"p_sb{i}", [P, QBS],
                                                    BF16)) for i in range(3)]
            rl_sb = S3.enter_context(nc.sbuf_tensor("rl_sb", [P, 2], F32))
            h_sb = S3.enter_context(nc.sbuf_tensor("h_sb", [P, 2 * D], F32))
            xres = S3.enter_context(nc.sbuf_tensor("xres", [P, 2 * D], F32))
            out_b = S3.enter_context(nc.sbuf_tensor("out_b", [P, 2 * D], F32))
            o_ps = [[S3.enter_context(nc.psum_tensor(f"o_ps{q}_{f}", [P, 512],
                                                     F32))
                     for f in range(2)] for q in range(2)]
            ps_s = [S3.enter_context(nc.psum_tensor(f"ps_s{i}", [P, 512],
                                                    F32)) for i in range(2)]
            l_ps = S3.enter_context(nc.psum_tensor("l_ps", [P, 512], F32))
            ps_h = S3.enter_context(nc.psum_tensor("ps_h", [P, 512], F32))

            def p4_gp(gpsimd):
                gpsimd.wait_ge(sem["s_cc"], 1)
                for r in range(NCORE):
                    gpsimd.dma_start(
                        out=xk_all[0:RA, r * TLOC:(r + 1) * TLOC],
                        in_=cgat_d.ap()[r * CTR + TLOC + 2:
                                        (r + 1) * CTR, :].rearrange(
                            "(a b) c -> a (b c)", b=2),
                    ).then_inc(sem["s_kg"], 16)
                for r in range(NCORE):
                    gpsimd.dma_start(
                        out=mk8[0:1, r:r + 1],
                        in_=cgat_d.ap()[r * CTR + TLOC:r * CTR + TLOC + 1,
                                        0:1],
                    ).then_inc(sem["s_kg"], 16)
                gpsimd.dma_start(out=wo_sb[:], in_=wo_d.rearrange(
                    "p k c -> p (k c)")).then_inc(sem["s_wol"], 16)
                for i, (nm, o) in enumerate(slice_src):
                    if i % 2 == 0 or i < 48:
                        continue
                    slice_owner_wait(gpsimd, i)
                    gpsimd.dma_start(out=slb[i % NSLOT][:],
                                     in_=wbf_d[nm][o]).then_inc(sem["s_slg"],
                                                                16)
                gpsimd.wait_ge(sem["s_zmm"], 96)
                gpsimd.wait_ge(sem["s_cdve"], 32)
                gpsimd.dma_start(out=uo_sb[:], in_=uo_d.rearrange(
                    "p k c -> p (k c)")).then_inc(sem["s_wol"], 16)

            if ph >= 3.9:
                block.gpsimd(p4_gp)

            def p4_vec2(vector):
                vector.wait_ge(sem["s_kg"], 16 * 16)
                vector.wait_ge(sem["s_wol"], 16)
                vector.reduce_max(mkv2[0:1, 0:1], mk8[0:1, :],
                                  axis=AX.X).then_inc(sem["s_mk8r"], 1)
                vector.wait_ge(sem["s_mkv"], 1)
                vector.wait_ge(sem["s_qnx"], 4)
                vector.tensor_scalar(
                    xq_aug[RANK:RA, :], qn_bf[0:1, :], mkv[0:1, 0:1], -1.0,
                    op0=ALU.mult, op1=ALU.mult).then_inc(sem["s_mhat"], 1)

            if ph >= 3.9:
                block.vector(p4_vec2)

            def p4_scl(scalar):
                scalar.wait_ge(sem["s_mk8r"], 1)
                afence(scalar, scalar.activation(tmp_row[0:1, 0:1],
                                                 mkv2[0:1, 0:1], AF.Ln))
                scalar.activation(mkv[0:1, 0:1], tmp_row[0:1, 0:1], AF.Exp,
                                  scale=0.5).then_inc(sem["s_mkv"], 1)

            if ph >= 3.9:
                block.scalar(p4_scl)

            def p5_sync(sync):
                sync.wait_ge(sem["s_cc"], 1)
                sync.wait_ge(sem["s_zmm"], 96)
                sync.wait_ge(sem["s_cdve"], 32)
                for e in range(NEP):
                    b = e // NQB
                    for cv in range(NVC):
                        gc = e * NVC + cv
                        if gc >= 2:
                            sync.wait_ge(sem["s_vfree"], gc - 1)
                        row0 = (cv // 2) * CTR + b * POS + (cv % 2) * 512
                        sync.dma_start(
                            out=xv_buf[gc % 2].ap().rearrange(
                                "p (t f) -> p t f", f=D),
                            in_=cgat_d.ap()[row0:row0 + VCH * P, :].rearrange(
                                "(t p) f -> p t f", p=P),
                        ).then_inc(sem["s_vs"], 16)

            if ph >= 5:
                block.sync(p5_sync)

            sched = [(e, kt) for e in range(NEP) for kt in range(NKT)]

            def p5_tns(tensor):
                tensor.wait_ge(sem["s_kg"], 16 * 16)
                tensor.wait_ge(sem["s_mhat"], 1)
                tensor.wait_ge(sem["s_wol"], 16)
                tensor.wait_ge(sem["s_wb"], 64)
                tensor.wait_ge(sem["s_gact"], 96)
                tensor.wait_ge(sem["s_cdve"], 32)

                def emit_smm(g):
                    e2, kt2 = sched[g]
                    b2, qb2 = e2 // NQB, e2 % NQB
                    q0 = b2 * POS + qb2 * QBS
                    r, t = kt2 // 8, kt2 % 8
                    kcol = r * TLOC + b2 * POS + t * P
                    tensor.matmul(ps_s[g % 2][:, 0:QBS],
                                  xk_all[0:RA, kcol:kcol + P],
                                  xq_aug[0:RA, q0:q0 + QBS],
                                  start=True, stop=True).then_inc(sem["s_smm"],
                                                                  1)

                emit_smm(0)
                emit_smm(1)
                for gb in range(0, len(sched), 2):
                    eb, ktb = sched[gb]
                    cvb = ktb // VCH
                    if ktb % VCH == 0:
                        tensor.wait_ge(sem["s_vs"],
                                       16 * (eb * NVC + cvb + 1))
                    tensor.wait_ge(sem["s_exp"], gb + 2)
                    if gb + 2 < len(sched):
                        emit_smm(gb + 2)
                    if gb + 3 < len(sched):
                        emit_smm(gb + 3)
                    for g in (gb, gb + 1):
                        e, kt = sched[g]
                        b, qb = e // NQB, e % NQB
                        cv = kt // VCH
                        for qs in range(2):
                            if kt == 0 and e >= 1:
                                tensor.wait_ge(sem["s_eadd"],
                                               4 * (e - 1) + 2 * (qs + 1))
                            pt = p_sb[g % 3][:, qs * P:(qs + 1) * P]
                            for fh in range(2):
                                mo = tensor.matmul(
                                    o_ps[qs][fh][:], pt,
                                    xv_buf[(e * NVC + cv) % 2][
                                        :, (kt % VCH) * D + fh * 512:
                                        (kt % VCH) * D + fh * 512 + 512],
                                    start=(kt == 0), stop=(kt == NKT - 1),
                                    skip_group_check=True)
                                if (qs == 1 and fh == 1
                                        and kt % VCH == VCH - 1):
                                    mo.then_inc(sem["s_vfree"], 1)
                            mm = tensor.matmul(
                                l_ps[:, qs:qs + 1], pt, ones_sb[:, 0:1],
                                start=(kt == 0), stop=(kt == NKT - 1),
                                skip_group_check=True)
                            if qs == 1:
                                if kt == NKT - 1:
                                    mm.then_inc(sem["s_att"], 1)
                                else:
                                    mm.then_inc(sem["s_pfree"], 1)
                        if kt != NKT - 1:
                            continue
                        q0 = b * POS + qb * QBS
                        if e == 0:
                            tensor.wait_ge(sem["s_wol"], 32)
                            tensor.wait_ge(sem["s_xr"], 16)
                        for qs in range(2):
                            l0 = q0 + qs * P
                            j0 = qb * QBS + qs * P
                            for fh in range(2):
                                u = 4 * e + 2 * qs + fh
                                tensor.wait_ge(sem["s_tanh"], u)
                                for k in range(KC):
                                    mh = tensor.matmul(
                                        ps_h[:],
                                        xn_fm[:, k * TLOC + l0:
                                              k * TLOC + l0 + P],
                                        wo_sb[:, k * D + fh * 512:
                                              k * D + fh * 512 + 512],
                                        start=(k == 0), stop=False)
                                    if qs == 0 and fh == 0 and k == 0:
                                        mh.then_inc(sem["s_pfree"], 1)
                                for k in range(KC):
                                    tensor.matmul(
                                        ps_h[:],
                                        c2_bf[:, k * POS + j0:
                                              k * POS + j0 + P],
                                        uo_sb[:, k * D + fh * 512:
                                              k * D + fh * 512 + 512],
                                        start=False, stop=False)
                                tensor.matmul(
                                    ps_h[:], ones_sb[0:1, 0:P],
                                    b_bf["o"][0:1, fh * 512:(fh + 1) * 512],
                                    start=False, stop=True).then_inc(
                                    sem["s_hmm"], 1)

            if ph >= 5:
                block.tensor(p5_tns)

            def p5_scl(scalar):
                for e in range(NEP):
                    for kt in range(NKT):
                        g = e * NKT + kt
                        scalar.wait_ge(sem["s_smm"], g + 1)
                        if g >= 3:
                            scalar.wait_ge(sem["s_pfree"], g - 2)
                        scalar.activation(p_sb[g % 3][:],
                                          ps_s[g % 2][:, 0:QBS],
                                          AF.Exp).then_inc(sem["s_exp"], 1)
                    for u in range(4):
                        qs, fh = u // 2, u % 2
                        scalar.wait_ge(sem["s_hmm"], 4 * e + u + 1)
                        if e >= 1 and u == 0:
                            scalar.wait_ge(sem["s_eadd"], 4 * e)
                        scalar.activation(
                            h_sb[:, qs * D + fh * 512:qs * D + fh * 512 + 512],
                            ps_h[:], AF.Tanh).then_inc(sem["s_tanh"], 1)

            if ph >= 5:
                block.scalar(p5_scl)

            def p5_vec(vector):
                for e in range(NEP):
                    vector.wait_ge(sem["s_att"], e + 1)
                    vfence(vector, vector.reciprocal(rl_sb[:], l_ps[:, 0:2]))
                    for qs in range(2):
                        for fh in range(2):
                            u = 4 * e + 2 * qs + fh
                            vector.wait_ge(sem["s_tanh"], u + 1)
                            vector.scalar_tensor_tensor(
                                h_sb[:, qs * D + fh * 512:
                                     qs * D + fh * 512 + 512],
                                o_ps[qs][fh][:], rl_sb[:, qs:qs + 1],
                                h_sb[:, qs * D + fh * 512:
                                     qs * D + fh * 512 + 512],
                                op0=ALU.mult, op1=ALU.add).then_inc(
                                sem["s_eadd"], 1)

            if ph >= 5:
                block.vector(p5_vec)

            def p5_gp(gpsimd):
                gpsimd.wait_ge(sem["s_cdve"], 32)
                gpsimd.dma_start(
                    out=xres.ap().rearrange("p (t f) -> p t f", f=D),
                    in_=x_d[0:QBS, :].rearrange("(t p) f -> p t f",
                                                p=P)).then_inc(sem["s_xr"],
                                                               16)
                for e in range(NEP):
                    b, qb = e // NQB, e % NQB
                    for qs in range(2):
                        gpsimd.wait_ge(sem["s_eadd"], 4 * e + 2 * qs + 2)
                        gpsimd.wait_ge(sem["s_xr"], 16 * (e + 1))
                        if e >= 1 and qs == 0:
                            gpsimd.wait_ge(sem["s_st"], 32 * e)
                        gpsimd.tensor_add(
                            out_b[:, qs * D:(qs + 1) * D],
                            h_sb[:, qs * D:(qs + 1) * D],
                            xres[:, qs * D:(qs + 1) * D]).then_inc(
                            sem["s_pool"], 1)
                        gpsimd.wait_ge(sem["s_pool"], 2 * e + qs + 1)
                        l0 = b * POS + qb * QBS + qs * P
                        gpsimd.dma_start(
                            out=out_d[l0:l0 + P, :],
                            in_=out_b[:, qs * D:(qs + 1) * D]).then_inc(
                            sem["s_st"], 16)
                    if e + 1 < NEP:
                        b2, qb2 = (e + 1) // NQB, (e + 1) % NQB
                        x0 = b2 * POS + qb2 * QBS
                        gpsimd.dma_start(
                            out=xres.ap().rearrange("p (t f) -> p t f", f=D),
                            in_=x_d[x0:x0 + QBS, :].rearrange(
                                "(t p) f -> p t f", p=P)).then_inc(sem["s_xr"],
                                                                   16)
                gpsimd.wait_ge(sem["s_st"], 16 * 16)

            if ph >= 5:
                block.gpsimd(p5_gp)

            if dbg and ph >= 5:
                def pdbg5(gpsimd):
                    gpsimd.wait_ge(sem["s_st"], 16 * 16)
                    gpsimd.dma_start(out=dbg_d["xn"], in_=xn_fm[:, :]).then_inc(sem["s_st"], 16)
                    gpsimd.dma_start(out=dbg_d["xq"], in_=xq_aug[0:RA, :]).then_inc(sem["s_st"], 16)
                    gpsimd.dma_start(out=dbg_d["c2"], in_=c2_bf[:, :]).then_inc(sem["s_st"], 16)
                    gpsimd.dma_start(out=dbg_d["stats"][:, 0:NRT], in_=st_mu[:, :]).then_inc(sem["s_st"], 16)
                    gpsimd.dma_start(out=dbg_d["stats"][:, NRT:2 * NRT], in_=st_rs[:, :]).then_inc(sem["s_st"], 16)
                    gpsimd.dma_start(out=dbg_d["wo"], in_=wo_sb[:, :]).then_inc(sem["s_st"], 16)
                    gpsimd.dma_start(out=dbg_d["uo"], in_=uo_sb[:, :]).then_inc(sem["s_st"], 16)
                    gpsimd.dma_start(out=dbg_d["kall"], in_=xk_all[0:RA, :]).then_inc(sem["s_st"], 16)
                    gpsimd.dma_start(out=dbg_d["mkv"], in_=mkv[0:1, :]).then_inc(sem["s_st"], 16)
                    gpsimd.dma_start(out=dbg_d["hsb"], in_=h_sb[:, :]).then_inc(sem["s_st"], 16)
                    gpsimd.dma_start(out=dbg_d["kctb"][1:CROWS, :], in_=ctb_d.ap()[TLOC + 2:CTR, :].rearrange("(a b) c -> a (b c)", b=2)).then_inc(sem["s_st"], 16)
                    gpsimd.dma_start(out=dbg_d["kctb"][0:1, 0:1], in_=ctb_d.ap()[TLOC:TLOC + 1, 0:1]).then_inc(sem["s_st"], 16)
                    gpsimd.wait_ge(sem["s_st"], 16 * 28)

                block.gpsimd(pdbg5)

    return nc


_NC_CACHE = None


def kernel(**inputs):
    global _NC_CACHE
    x = np.asarray(inputs["L0_attn_x"], dtype=np.float32)
    Q = np.asarray(inputs["L0_attn_Q"], dtype=np.float32)
    K = np.asarray(inputs["L0_attn_K"], dtype=np.float32)
    VO = np.asarray(inputs["L0_attn_VO"], dtype=np.float32)
    assert x.shape == (2, 8192, 1024)

    def full_prep(w):
        return np.ascontiguousarray(
            w.reshape(KC, P, w.shape[1]).transpose(1, 0, 2))

    def slice_prep(wt):
        return np.ascontiguousarray(
            wt.reshape(KC, P, KC, P).transpose(2, 1, 0, 3))

    feed = {
        "qt": full_prep(Q.T),
        "kt": full_prep(K.T),
        "vo": full_prep(VO),
        "wo": full_prep(np.ascontiguousarray(
            np.asarray(inputs["L0_ffn_wo"], np.float32).T)),
        "uo": full_prep(np.ascontiguousarray(
            np.asarray(inputs["L0_ffn_uo"], np.float32).T)),
        "bf": np.asarray(inputs["L0_ffn_bf"], np.float32).reshape(1, D),
        "bi": np.asarray(inputs["L0_ffn_bi"], np.float32).reshape(1, D),
        "bc": np.asarray(inputs["L0_ffn_bc"], np.float32).reshape(1, D),
        "bo": np.asarray(inputs["L0_ffn_bo"], np.float32).reshape(1, D),
        "in_c": np.ascontiguousarray(
            np.asarray(inputs["L0_ffn_in_c"], np.float32).reshape(KC, P).T),
    }
    for nm in ("wf", "wi", "wc", "uf", "ui", "uc"):
        feed[nm] = slice_prep(
            np.ascontiguousarray(np.asarray(inputs[f"L0_ffn_{nm}"],
                                            np.float32).T))

    if _NC_CACHE is None:
        _NC_CACHE = build_nc()
    nc = _NC_CACHE

    in_maps = []
    for c in range(NCORE):
        sl = slice(c * POS, (c + 1) * POS)
        xs = np.ascontiguousarray(
            np.concatenate([x[0, sl], x[1, sl]], axis=0))
        m = dict(feed)
        m["x"] = xs
        in_maps.append(m)

    res = run_bass_kernel_spmd(nc, in_maps, core_ids=list(range(NCORE)))

    y = np.empty((2, 8192, 1024), dtype=np.float32)
    for c in range(NCORE):
        o = np.asarray(res.results[c]["out"])
        y[0, c * POS:(c + 1) * POS] = o[0:POS]
        y[1, c * POS:(c + 1) * POS] = o[POS:TLOC]
    return y


if __name__ == "__main__":
    nc = build_nc()
    print("built ok; instructions:", nc.n_instructions())


# version 22
# speedup vs baseline: 1.0992x; 1.0634x over previous
import numpy as np
from contextlib import ExitStack

import concourse.bass as bass
import concourse.mybir as mybir
from concourse.bass_utils import run_bass_kernel_spmd
import ml_dtypes

F32 = mybir.dt.float32
BF16 = mybir.dt.bfloat16
AF = mybir.ActivationFunctionType
ALU = mybir.AluOpType
AX = mybir.AxisListType

NCORE = 8
P = 128
D = 1024
KC = 8
TLOC = 2048
NRT = 16
POS = 1024
NB = 2
RANK = 64
RA = 65
CROWS = 66
QBS = 256
NSLOT = 12
NQB = POS // QBS
NEP = NB * NQB
NKT = 64
VCH = 4
NVC = NKT // VCH
EPS = 1e-5
INV32 = 1.0 / 32.0

CONV_W = ("uf", "ui", "uc", "wf", "wi", "wc")
CIDX = {nm: i for i, nm in enumerate(CONV_W)}


def build_nc(ph=5, dbg=False):
    nc = bass.Bass("TRN2", target_bir_lowering=False)

    x_d = nc.dram_tensor("x", [TLOC, D], F32, kind="ExternalInput").ap()
    qt_d = nc.dram_tensor("qt", [P, KC, RANK], F32, kind="ExternalInput").ap()
    kt_d = nc.dram_tensor("kt", [P, KC, RANK], F32, kind="ExternalInput").ap()
    vo_d = nc.dram_tensor("vo", [P, KC, D], F32, kind="ExternalInput").ap()
    wo_d = nc.dram_tensor("wo", [P, KC, D], F32, kind="ExternalInput").ap()
    uo_d = nc.dram_tensor("uo", [P, KC, D], F32, kind="ExternalInput").ap()
    ws_d = {}
    for nm in ("wf", "wi", "wc", "uf", "ui", "uc"):
        ws_d[nm] = nc.dram_tensor(nm, [KC, P, KC, P], F32,
                                  kind="ExternalInput").ap()
    b_d = {}
    for nm in ("bf", "bi", "bc", "bo"):
        b_d[nm] = nc.dram_tensor(nm, [1, D], F32, kind="ExternalInput").ap()
    inc_d = nc.dram_tensor("in_c", [P, KC], F32, kind="ExternalInput").ap()
    out_d = nc.dram_tensor("out", [TLOC, D], F32, kind="ExternalOutput").ap()

    wbf_d = {nm: nc.dram_tensor(f"wbf_{nm}", [KC, P, KC * P], BF16,
                                kind="Internal").ap()
             for nm in CONV_W}

    CTR = TLOC + 2 * CROWS
    ctb_d = nc.dram_tensor("ctb", [CTR, D], BF16, kind="Internal")
    cgat_d = nc.dram_tensor("cgat", [NCORE * CTR, D], BF16, kind="Internal")

    ident_d = nc.inline_tensor(np.eye(P, dtype=ml_dtypes.bfloat16),
                               "ident").ap()
    ones_d = nc.inline_tensor(np.ones((P, 512), dtype=ml_dtypes.bfloat16),
                              "onesc").ap()

    rg = [list(range(NCORE))]

    dbg_d = {}
    if dbg:
        dbg_d["xn"] = nc.dram_tensor("dbg_xn", [P, KC * TLOC], BF16,
                                     kind="ExternalOutput").ap()
        dbg_d["xq"] = nc.dram_tensor("dbg_xq", [RA, TLOC], BF16,
                                     kind="ExternalOutput").ap()
        dbg_d["c2"] = nc.dram_tensor("dbg_c2", [P, KC * POS], BF16,
                                     kind="ExternalOutput").ap()
        dbg_d["c1"] = nc.dram_tensor("dbg_c1", [P, KC * POS], F32,
                                     kind="ExternalOutput").ap()
        dbg_d["stats"] = nc.dram_tensor("dbg_stats", [P, 2 * NRT], F32,
                                        kind="ExternalOutput").ap()
        dbg_d["wo"] = nc.dram_tensor("dbg_wo", [P, KC * D], BF16,
                                     kind="ExternalOutput").ap()
        dbg_d["uo"] = nc.dram_tensor("dbg_uo", [P, KC * D], BF16,
                                     kind="ExternalOutput").ap()
        dbg_d["kall"] = nc.dram_tensor("dbg_kall", [RA, NCORE * TLOC], BF16,
                                       kind="ExternalOutput").ap()
        dbg_d["mkv"] = nc.dram_tensor("dbg_mkv", [1, 1], F32,
                                      kind="ExternalOutput").ap()
        dbg_d["hsb"] = nc.dram_tensor("dbg_hsb", [P, 2 * D], F32,
                                      kind="ExternalOutput").ap()
        dbg_d["kctb"] = nc.dram_tensor("dbg_kctb", [CROWS, TLOC], BF16,
                                       kind="ExternalOutput").ap()
        dbg_d["xkc"] = nc.dram_tensor("dbg_xkc", [RA, TLOC], BF16,
                                      kind="ExternalOutput").ap()

    with ExitStack() as S0:
        ident_sb = S0.enter_context(nc.sbuf_tensor("ident_sb", [P, P], BF16))
        ones_sb = S0.enter_context(nc.sbuf_tensor("ones_sb", [P, 512], BF16))
        xn_fm = S0.enter_context(nc.sbuf_tensor("xn_fm", [P, KC * TLOC], BF16))
        c2_bf = S0.enter_context(nc.sbuf_tensor("c2_bf", [P, KC * POS], BF16))
        xq_aug = S0.enter_context(nc.sbuf_tensor("xq_aug", [RA, TLOC], BF16))
        qn_bf = S0.enter_context(nc.sbuf_tensor("qn_bf", [1, TLOC], BF16))
        inc_f = S0.enter_context(nc.sbuf_tensor("inc_f", [P, KC], F32))
        inc_b = S0.enter_context(nc.sbuf_tensor("inc_b", [P, KC], BF16))
        b_bf = {nm: S0.enter_context(nc.sbuf_tensor(f"b_{nm}", [1, D], BF16))
                for nm in ("f", "i", "c", "o")}
        rf_bf = {nm: S0.enter_context(nc.sbuf_tensor(f"r_{nm}", [1, D], BF16))
                 for nm in ("f", "i", "c")}
        st_sum = S0.enter_context(nc.sbuf_tensor("st_sum", [P, NRT], F32))
        st_sq = S0.enter_context(nc.sbuf_tensor("st_sq", [P, NRT], F32))
        st_mu = S0.enter_context(nc.sbuf_tensor("st_mu", [P, NRT], F32))
        st_t0 = S0.enter_context(nc.sbuf_tensor("st_t0", [P, NRT], F32))
        st_var = S0.enter_context(nc.sbuf_tensor("st_var", [P, NRT], F32))
        st_rs = S0.enter_context(nc.sbuf_tensor("st_rs", [P, NRT], F32))
        tmp_row = S0.enter_context(nc.sbuf_tensor("tmp_row", [1, 512], F32))
        mk4 = S0.enter_context(nc.sbuf_tensor("mk4", [1, 4], F32))
        lmax_bf = S0.enter_context(nc.sbuf_tensor("lmax_bf", [1, 1], BF16))
        mk8 = S0.enter_context(nc.sbuf_tensor("mk8", [1, NCORE], BF16))
        mkv = S0.enter_context(nc.sbuf_tensor("mkv", [1, 1], F32))
        mkv2 = S0.enter_context(nc.sbuf_tensor("mkv2", [1, 1], F32))
        eps_col = S0.enter_context(nc.sbuf_tensor("eps_col", [P, 1], F32))

        sem_names = [
            "s_const", "s_x", "s_xb", "s_sum", "s_sq", "s_var", "s_rstd",
            "s_ap", "s_tp", "s_ev",
            "s_qkst", "s_qk", "s_conv", "s_wb", "s_slg",
            "s_p2mm", "s_xqe", "s_xke", "s_sq2xk", "s_nk", "s_nkr",
            "s_sq2xq", "s_nq", "s_qnl", "s_qnx",
            "s_vmm", "s_vev", "s_ct", "s_cc", "s_kg", "s_mk8r", "s_mkv",
            "s_mhat",
            "s_sl", "s_mv", "s_rfe",
            "s_zmm", "s_gact", "s_cdve",
            "s_wol",
            "s_smm", "s_exp", "s_pfree", "s_vs", "s_vfree",
            "s_att", "s_hmm", "s_tanh", "s_eadd", "s_pool", "s_xr", "s_st",
            "s_vch", "s_ach",
        ]
        sem = {nm: S0.enter_context(nc.semaphore(nm)) for nm in sem_names}
        vch_n = [0]
        ach_n = [0]

        def vfence(vector, inst):
            vch_n[0] += 1
            inst.then_inc(sem["s_vch"], 1)
            vector.wait_ge(sem["s_vch"], vch_n[0])

        def afence(scalar, inst):
            ach_n[0] += 1
            inst.then_inc(sem["s_ach"], 1)
            scalar.wait_ge(sem["s_ach"], ach_n[0])

        block = S0.enter_context(nc.Block())

        slice_src = []
        for nm in ("uf", "ui", "uc"):
            for o in range(KC):
                slice_src.append((nm, o))
        for o in range(KC):
            for nm in ("wf", "wi", "wc"):
                slice_src.append((nm, o))
        for o in range(KC):
            for nm in ("wf", "wi", "wc", "uf", "ui", "uc"):
                slice_src.append((nm, o))
        NSL = len(slice_src)

        def slice_owner_wait(eng, i):
            j = i - NSLOT
            if j < 0:
                return
            if j < 24:
                eng.wait_ge(sem["s_mv"], j + 1)
            elif j < 48:
                o_j = (j - 24) // 3
                eng.wait_ge(sem["s_gact"], 3 * (2 * o_j + 2))
            else:
                o_j = (j - 48) // 6
                eng.wait_ge(sem["s_gact"], 3 * (16 + 2 * o_j + 2))

        def wait_slices(eng, imax):
            if imax < 48:
                eng.wait_ge(sem["s_sl"], 16 * (imax + 1))
            else:
                eng.wait_ge(sem["s_sl"], 16 * (48 + (imax - 48) // 2 + 1))
                eng.wait_ge(sem["s_slg"], 16 * ((imax - 47) // 2))

        with ExitStack() as S1:
            x_full = S1.enter_context(nc.sbuf_tensor("x_full",
                                                     [P, NRT * D], F32))
            xn_tm = [S1.enter_context(nc.sbuf_tensor(f"xn_tm{i}", [P, D],
                                                     BF16)) for i in range(2)]
            sq_dump = S1.enter_context(nc.sbuf_tensor("sq_dump", [P, D], BF16))
            qt_sb = S1.enter_context(nc.sbuf_tensor("qt_sb", [P, KC * RANK],
                                                    BF16))
            kt_sb = S1.enter_context(nc.sbuf_tensor("kt_sb", [P, KC * RANK],
                                                    BF16))
            vo_sb = S1.enter_context(nc.sbuf_tensor("vo_sb", [P, KC * D],
                                                    BF16))
            xkc = S1.enter_context(nc.sbuf_tensor("xkc", [RA, TLOC], BF16))
            sq2 = S1.enter_context(nc.sbuf_tensor("sq2", [RANK, TLOC], BF16))
            xvc = [S1.enter_context(nc.sbuf_tensor(f"xvc{i}", [P, D], BF16))
                   for i in range(6)]
            ps_t = [S1.enter_context(nc.psum_tensor(f"ps_t{i}", [P, D], BF16))
                    for i in range(2)]
            ps_q = [S1.enter_context(nc.psum_tensor(f"ps_q{i}", [RANK, 512],
                                                    F32)) for i in range(2)]
            ps_v = [S1.enter_context(nc.psum_tensor(f"ps_v{i}", [P, 512], F32))
                    for i in range(2)]
            ps_n = [S1.enter_context(nc.psum_tensor(f"ps_n{i}", [1, 512], F32))
                    for i in range(2)]

            def p12_sync(sync):
                sync.dma_start(out=ident_sb[:], in_=ident_d).then_inc(
                    sem["s_const"], 16)
                sync.dma_start(out=ones_sb[:], in_=ones_d).then_inc(
                    sem["s_const"], 16)
                for rt in range(NRT):
                    sync.dma_start(out=x_full[:, rt * D:(rt + 1) * D],
                                   in_=x_d[rt * P:(rt + 1) * P, :]).then_inc(
                        sem["s_x"], 16)
                sync.dma_start(out=inc_f[:], in_=inc_d).then_inc(
                    sem["s_qkst"], 16)
                for rc in range(NRT):
                    sync.wait_ge(sem["s_vev"], 2 * rc + 2)
                    sync.dma_start(out=ctb_d.ap()[rc * P:(rc + 1) * P, :],
                                   in_=xvc[rc % 6][:]).then_inc(sem["s_ct"],
                                                                16)
                sync.wait_ge(sem["s_xke"], 5)
                sync.dma_start(out=ctb_d.ap()[TLOC + 2:CTR, :].rearrange(
                    "(a b) c -> a (b c)", b=2),
                    in_=xkc[0:RA, :]).then_inc(sem["s_ct"], 16)
                sync.wait_ge(sem["s_nkr"], 5)
                sync.dma_start(out=ctb_d.ap()[TLOC:TLOC + 1, 0:1],
                               in_=lmax_bf[0:1, 0:1]).then_inc(sem["s_ct"], 16)
                if dbg:
                    sync.dma_start(out=dbg_d["xkc"],
                                   in_=xkc[0:RA, :]).then_inc(sem["s_const"],
                                                              16)

            block.sync(p12_sync)

            def p1_gp(gpsimd):
                for nm, dst in (("bf", b_bf["f"]), ("bi", b_bf["i"]),
                                ("bc", b_bf["c"]), ("bo", b_bf["o"])):
                    gpsimd.dma_start(out=dst[0:1, :], in_=b_d[nm]).then_inc(
                        sem["s_wb"], 16)
                gpsimd.dma_start(out=qt_sb[:], in_=qt_d.rearrange(
                    "p k c -> p (k c)")).then_inc(sem["s_qk"], 16)
                gpsimd.dma_start(out=kt_sb[:], in_=kt_d.rearrange(
                    "p k c -> p (k c)")).then_inc(sem["s_qk"], 16)
                gpsimd.dma_start(out=inc_b[:], in_=inc_d).then_inc(
                    sem["s_qk"], 16)
                gpsimd.wait_ge(sem["s_x"], 16 * NRT)
                gpsimd.dma_start(out=vo_sb[:], in_=vo_d.rearrange(
                    "p k c -> p (k c)")).then_inc(sem["s_qk"], 16)
                for nm in CONV_W:
                    gpsimd.dma_start(
                        out=wbf_d[nm].rearrange("o p f -> (o p) f"),
                        in_=ws_d[nm].rearrange("o p k c -> (o p) (k c)"),
                    ).then_inc(sem["s_conv"], 16)
                gpsimd.dma_start(out=sq_dump[0:1, 0:2],
                                 in_=x_d[0:1, 0:2]).then_inc(sem["s_conv"], 16)

            block.gpsimd(p1_gp)

            def p2_gp(gpsimd):
                gpsimd.wait_ge(sem["s_ct"], 16 * 18)
                gpsimd.wait_ge(sem["s_qk"], 64)
                gpsimd.wait_ge(sem["s_conv"], 16 * 7)
                gpsimd.collective_compute(
                    "AllGather", ALU.bypass, replica_groups=rg,
                    ins=[ctb_d.ap().opt()], outs=[cgat_d.ap().opt()],
                ).then_inc(sem["s_cc"])

            if ph >= 3.9:
                block.gpsimd(p2_gp)

            def p1_vec(vector):
                vector.memset(eps_col[:], EPS)
                for rt in range(NRT):
                    vector.wait_ge(sem["s_x"], 16 * min(rt + 3, NRT))
                    if rt == NRT - 1:
                        vector.wait_ge(sem["s_qkst"], 16)
                    vector.reduce_sum(st_sum[:, rt:rt + 1],
                                      x_full[:, rt * D:(rt + 1) * D],
                                      axis=AX.X).then_inc(sem["s_sum"], 1)
                vector.wait_ge(sem["s_sum"], NRT)
                vector.wait_ge(sem["s_sq"], NRT)
                vfence(vector, vector.tensor_scalar_mul(st_mu[:], st_sum[:],
                                                        1.0 / D))
                vfence(vector, vector.tensor_scalar_mul(st_t0[:], st_sq[:],
                                                        1.0 / D))
                vfence(vector, vector.tensor_mul(st_var[:], st_mu[:],
                                                 st_mu[:]))
                vector.tensor_sub(st_var[:], st_t0[:], st_var[:]).then_inc(
                    sem["s_var"], 1)
                for rt in range(NRT):
                    if rt == 0:
                        vector.wait_ge(sem["s_rstd"], 1)
                    if rt >= 2:
                        vector.wait_ge(sem["s_tp"], KC * (rt - 1))
                    vector.tensor_scalar(
                        xn_tm[rt % 2][:], x_full[:, rt * D:(rt + 1) * D],
                        st_mu[:, rt:rt + 1], st_rs[:, rt:rt + 1],
                        op0=ALU.subtract, op1=ALU.mult).then_inc(sem["s_ap"],
                                                                 1)

            block.vector(p1_vec)

            def p1_scl(scalar):
                for rt in range(NRT):
                    scalar.wait_ge(sem["s_x"], 16 * min(rt + 3, NRT))
                    if rt == NRT - 1:
                        scalar.wait_ge(sem["s_qkst"], 16)
                    scalar.activation(sq_dump[:], x_full[:, rt * D:(rt + 1) * D],
                                      AF.Square,
                                      accum_out=st_sq[:, rt:rt + 1]).then_inc(
                        sem["s_sq"], 1)
                scalar.wait_ge(sem["s_var"], 1)
                afence(scalar, scalar.activation(st_t0[:], st_var[:], AF.Ln,
                                                 bias=eps_col[:, 0:1]))
                scalar.activation(st_rs[:], st_t0[:], AF.Exp,
                                  scale=-0.5).then_inc(sem["s_rstd"], 1)
                for rt in range(NRT):
                    for k in range(KC):
                        g = KC * rt + k
                        scalar.wait_ge(sem["s_tp"], g + 1)
                        scalar.copy(
                            xn_fm[:, k * TLOC + rt * P:
                                  k * TLOC + (rt + 1) * P],
                            ps_t[g % 2][:, 0:P]).then_inc(sem["s_ev"], 1)

            block.scalar(p1_scl)

            def p1_tns(tensor):
                tensor.wait_ge(sem["s_const"], 32)
                for rt in range(NRT):
                    tensor.wait_ge(sem["s_ap"], rt + 1)
                    for k in range(KC):
                        g = KC * rt + k
                        if g >= 2:
                            tensor.wait_ge(sem["s_ev"], g - 1)
                        tensor.transpose(ps_t[g % 2][:, 0:P],
                                         xn_tm[rt % 2][:, k * P:(k + 1) * P],
                                         ident_sb[:]).then_inc(sem["s_tp"], 1)

            block.tensor(p1_tns)

            def p2_vec(vector):
                if ph >= 2.15:
                    for n in range(4):
                        vector.wait_ge(sem["s_p2mm"], n + 1)
                        vector.tensor_scalar_mul(
                            xq_aug[0:RANK, n * 512:(n + 1) * 512],
                            ps_q[n % 2][:], INV32).then_inc(sem["s_xqe"], 1)
                    for n in range(4):
                        vector.wait_ge(sem["s_p2mm"], 4 + n + 1)
                        vector.tensor_copy(
                            xkc[0:RANK, n * 512:(n + 1) * 512],
                            ps_q[n % 2][:]).then_inc(sem["s_xke"], 1)
                    vector.memset(xkc[RANK:RA, :], 1.0).then_inc(sem["s_xke"],
                                                                 1)
                if ph >= 2.25:
                    for n in range(4):
                        vector.wait_ge(sem["s_nk"], n + 1)
                        vector.reduce_max(mk4[0:1, n:n + 1],
                                          ps_n[n % 2][0:1, :],
                                          axis=AX.X).then_inc(sem["s_nkr"], 1)
                    vector.wait_ge(sem["s_nkr"], 4)
                    vector.reduce_max(lmax_bf[0:1, 0:1], mk4[0:1, :],
                                      axis=AX.X).then_inc(sem["s_nkr"], 1)

            block.vector(p2_vec)

            def p2_tns(tensor):
                tensor.wait_ge(sem["s_ev"], NRT * KC)
                tensor.wait_ge(sem["s_qk"], 48)
                for proj, wsb in ((0, qt_sb), (1, kt_sb)):
                    for n in range(4):
                        gi = proj * 4 + n
                        if proj == 0 and n >= 2:
                            tensor.wait_ge(sem["s_xqe"], n - 1)
                        if proj == 1 and n == 0:
                            tensor.wait_ge(sem["s_xqe"], 3)
                        if proj == 1 and n == 1:
                            tensor.wait_ge(sem["s_xqe"], 4)
                        if proj == 1 and n >= 2:
                            tensor.wait_ge(sem["s_xke"], n - 1)
                        for k in range(KC):
                            mmq = tensor.matmul(
                                ps_q[gi % 2][:],
                                wsb[:, k * RANK:(k + 1) * RANK],
                                xn_fm[:, k * TLOC + n * 512:
                                      k * TLOC + (n + 1) * 512],
                                start=(k == 0), stop=(k == KC - 1))
                        mmq.then_inc(sem["s_p2mm"], 1)
                if ph < 2.25:
                    return
                tensor.wait_ge(sem["s_sq2xk"], 1)
                for n in range(4):
                    if n >= 2:
                        tensor.wait_ge(sem["s_nkr"], n - 1)
                    tensor.matmul(ps_n[n % 2][:], ones_sb[0:RANK, 0:1],
                                  sq2[:, n * 512:(n + 1) * 512],
                                  start=True, stop=True).then_inc(sem["s_nk"],
                                                                  1)
                tensor.wait_ge(sem["s_sq2xq"], 1)
                tensor.wait_ge(sem["s_nkr"], 4)
                for n in range(4):
                    if n >= 2:
                        tensor.wait_ge(sem["s_qnl"], n - 1)
                    tensor.matmul(ps_n[n % 2][:], ones_sb[0:RANK, 0:1],
                                  sq2[:, n * 512:(n + 1) * 512],
                                  start=True, stop=True).then_inc(sem["s_nq"],
                                                                  1)
                if ph < 2.35:
                    return
                tensor.wait_ge(sem["s_qk"], 64)
                tensor.wait_ge(sem["s_conv"], 16)
                for rc in range(NRT):
                    for fh in range(2):
                        idx = 2 * rc + fh
                        if idx >= 2:
                            tensor.wait_ge(sem["s_vev"], idx - 1)
                        for k in range(KC):
                            mmv = tensor.matmul(
                                ps_v[idx % 2][:],
                                xn_fm[:, k * TLOC + rc * P:
                                      k * TLOC + (rc + 1) * P],
                                vo_sb[:, k * D + fh * 512:
                                      k * D + (fh + 1) * 512],
                                start=(k == 0), stop=(k == KC - 1))
                        mmv.then_inc(sem["s_vmm"], 1)

            if ph >= 2.15:
                block.tensor(p2_tns)

            def p2_scl(scalar):
                scalar.wait_ge(sem["s_xke"], 4)
                scalar.activation(sq2[:], xkc[0:RANK, :], AF.Square).then_inc(
                    sem["s_sq2xk"], 1)
                scalar.wait_ge(sem["s_nk"], 4)
                scalar.wait_ge(sem["s_xqe"], 4)
                scalar.activation(sq2[:], xq_aug[0:RANK, :],
                                  AF.Square).then_inc(sem["s_sq2xq"], 1)
                for n in range(4):
                    scalar.wait_ge(sem["s_nq"], n + 1)
                    scalar.activation(tmp_row[0:1, :], ps_n[n % 2][0:1, :],
                                      AF.Ln).then_inc(sem["s_qnl"], 1)
                    scalar.wait_ge(sem["s_qnl"], n + 1)
                    scalar.activation(qn_bf[0:1, n * 512:(n + 1) * 512],
                                      tmp_row[0:1, :], AF.Exp,
                                      scale=0.5).then_inc(sem["s_qnx"], 1)
                if ph < 2.35:
                    return
                for rc in range(NRT):
                    for fh in range(2):
                        idx = 2 * rc + fh
                        scalar.wait_ge(sem["s_vmm"], idx + 1)
                        if rc >= 6 and fh == 0:
                            scalar.wait_ge(sem["s_ct"], 16 * (rc - 5))
                        scalar.copy(xvc[rc % 6][:, fh * 512:(fh + 1) * 512],
                                    ps_v[idx % 2][:]).then_inc(sem["s_vev"], 1)

            if ph >= 2.25:
                block.scalar(p2_scl)

        with ExitStack() as S2:
            xk_all = S2.enter_context(nc.sbuf_tensor(
                "xk_all", [RA, NCORE * TLOC], BF16, side="right"))
            wo_sb = S2.enter_context(nc.sbuf_tensor(
                "wo_sb", [P, KC * D], BF16, side="right"))
            slb = [S2.enter_context(nc.sbuf_tensor(f"slb{i}", [P, D], BF16))
                   for i in range(NSLOT)]
            c1_f = S2.enter_context(nc.sbuf_tensor("c1_f", [P, KC * POS], F32))
            c1_b = S2.enter_context(nc.sbuf_tensor("c1_b", [P, KC * POS],
                                                   BF16))
            fig = [[S2.enter_context(nc.sbuf_tensor(f"fig{s}_{g}", [P, 512],
                                                    F32))
                    for g in range(3)] for s in range(2)]
            zps = [[S2.enter_context(nc.psum_tensor(f"zps{s}_{g}", [P, 512],
                                                    F32))
                    for g in range(3)] for s in range(2)]
            ps_r = [S2.enter_context(nc.psum_tensor(f"ps_r{i}", [1, 512],
                                                    F32)) for i in range(2)]
            zdbg = S2.enter_context(nc.sbuf_tensor(
                "zdbg", [P, 3 * 512], F32)) if (dbg and ph <= 4) else None

            def p3_sync(sync):
                sync.wait_ge(sem["s_ap"], NRT)
                sync.wait_ge(sem["s_tp"], NRT * KC)
                sync.wait_ge(sem["s_sum"], NRT)
                sync.wait_ge(sem["s_sq"], NRT)
                sync.wait_ge(sem["s_p2mm"], 8)
                for i, (nm, o) in enumerate(slice_src):
                    if i >= 48 and i % 2 == 1:
                        continue
                    if i == NSLOT:
                        sync.wait_ge(sem["s_cc"], 1)
                    sync.wait_ge(sem["s_conv"], 16 * (CIDX[nm] + 2))
                    slice_owner_wait(sync, i)
                    sync.dma_start(out=slb[i % NSLOT][:],
                                   in_=wbf_d[nm][o]).then_inc(sem["s_sl"], 16)

            if ph >= 3.4:
                block.sync(p3_sync)

            def p3_vec(vector):
                gate_names = ("f", "i", "c")

                def rf_evac(idx):
                    g, o = idx // KC, idx % KC
                    nm = gate_names[g]
                    vector.wait_ge(sem["s_mv"], idx + 1)
                    vector.wait_ge(sem["s_wb"], 16 * (g + 1))
                    vector.tensor_add(
                        rf_bf[nm][0:1, o * P:(o + 1) * P],
                        ps_r[idx % 2][0:1, 0:P],
                        b_bf[nm][0:1, o * P:(o + 1) * P]).then_inc(
                        sem["s_rfe"], 1)

                def c1_chunk(o, pc):
                    idx = 2 * o + pc
                    sset = idx % 2
                    vector.wait_ge(sem["s_gact"], 3 * (idx + 1))
                    base = o * POS + pc * 512
                    vector.tensor_scalar_mul(fig[sset][0][:], fig[sset][0][:],
                                             inc_f[:, o:o + 1])
                    vfence(vector, vector.tensor_mul(
                        fig[sset][1][:], fig[sset][1][:], fig[sset][2][:]))
                    vfence(vector, vector.tensor_add(
                        c1_f[:, base:base + 512],
                        fig[sset][0][:], fig[sset][1][:]))
                    vector.tensor_copy(c1_b[:, base:base + 512],
                                       c1_f[:, base:base + 512]).then_inc(
                        sem["s_cdve"], 1)

                def c2_chunk(o, pc):
                    idx = 16 + 2 * o + pc
                    sset = idx % 2
                    vector.wait_ge(sem["s_gact"], 3 * (idx + 1))
                    base = o * POS + pc * 512
                    vector.tensor_mul(fig[sset][0][:], fig[sset][0][:],
                                      c1_f[:, base:base + 512])
                    vfence(vector, vector.tensor_mul(
                        fig[sset][1][:], fig[sset][1][:], fig[sset][2][:]))
                    vector.tensor_add(c2_bf[:, base:base + 512],
                                      fig[sset][0][:],
                                      fig[sset][1][:]).then_inc(sem["s_cdve"],
                                                                1)

                for j in range(24):
                    rf_evac(j)
                for o in range(KC):
                    if o >= 2:
                        for pc in range(2):
                            c1_chunk(o - 2, pc)
                for o in (KC - 2, KC - 1):
                    for pc in range(2):
                        c1_chunk(o, pc)
                if ph < 3.7:
                    return
                for o in range(KC):
                    if o >= 2:
                        for pc in range(2):
                            c2_chunk(o - 2, pc)
                for o in (KC - 2, KC - 1):
                    for pc in range(2):
                        c2_chunk(o, pc)

            if ph >= 3.4:
                block.vector(p3_vec)

            def p3_tns(tensor):
                tensor.wait_ge(sem["s_qk"], 48)
                if ph >= 3.9:
                    tensor.wait_ge(sem["s_cc"], 1)
                for g in range(3):
                    for o in range(KC):
                        idx = g * KC + o
                        wait_slices(tensor, idx)
                        if idx >= 2:
                            tensor.wait_ge(sem["s_rfe"], idx - 1)
                        sl = slb[idx % NSLOT]
                        for k in range(KC):
                            mmr = tensor.matmul(
                                ps_r[idx % 2][0:1, 0:P],
                                inc_b[:, k:k + 1],
                                sl[:, k * P:(k + 1) * P],
                                start=(k == 0), stop=(k == KC - 1))
                        mmr.then_inc(sem["s_mv"], 1)
                tensor.wait_ge(sem["s_rfe"], 24)
                for o in range(KC):
                    wait_slices(tensor, 24 + 3 * o + 2)
                    for pc in range(2):
                        idx = 2 * o + pc
                        sset = idx % 2
                        if idx >= 2:
                            tensor.wait_ge(sem["s_gact"], 3 * (idx - 1))
                        for g, nm in enumerate(("f", "i", "c")):
                            sl = slb[(24 + 3 * o + g) % NSLOT]
                            for k in range(KC):
                                tensor.matmul(
                                    zps[sset][g][:],
                                    sl[:, k * P:(k + 1) * P],
                                    xn_fm[:, k * TLOC + pc * 512:
                                          k * TLOC + pc * 512 + 512],
                                    start=(k == 0), stop=False)
                            tensor.matmul(
                                zps[sset][g][:],
                                rf_bf[nm][0:1, o * P:(o + 1) * P],
                                ones_sb[0:1, :],
                                start=False, stop=True).then_inc(sem["s_zmm"],
                                                                 1)
                if ph < 3.7:
                    return
                tensor.wait_ge(sem["s_cdve"], 16)
                for o in range(KC):
                    wait_slices(tensor, 48 + 6 * o + 5)
                    for pc in range(2):
                        idx = 16 + 2 * o + pc
                        sset = idx % 2
                        tensor.wait_ge(sem["s_gact"], 3 * (idx - 1))
                        for g, nm in enumerate(("f", "i", "c")):
                            slw = slb[(48 + 6 * o + g) % NSLOT]
                            slu = slb[(48 + 6 * o + 3 + g) % NSLOT]
                            for k in range(KC):
                                tensor.matmul(
                                    zps[sset][g][:],
                                    slw[:, k * P:(k + 1) * P],
                                    xn_fm[:, k * TLOC + POS + pc * 512:
                                          k * TLOC + POS + pc * 512 + 512],
                                    start=(k == 0), stop=False)
                            for k in range(KC):
                                tensor.matmul(
                                    zps[sset][g][:],
                                    slu[:, k * P:(k + 1) * P],
                                    c1_b[:, k * POS + pc * 512:
                                         k * POS + pc * 512 + 512],
                                    start=False, stop=False)
                            tensor.matmul(
                                zps[sset][g][:],
                                b_bf[nm][0:1, o * P:(o + 1) * P],
                                ones_sb[0:1, :],
                                start=False, stop=True).then_inc(sem["s_zmm"],
                                                                 1)

            if ph >= 3.4:
                block.tensor(p3_tns)

            def p3_scl(scalar):
                for phase in range(2 if ph >= 3.7 else 1):
                    for o in range(KC):
                        for pc in range(2):
                            idx = phase * 16 + 2 * o + pc
                            sset = idx % 2
                            scalar.wait_ge(sem["s_zmm"], 3 * (idx + 1))
                            if idx >= 2:
                                scalar.wait_ge(sem["s_cdve"], idx - 1)
                            if zdbg is not None and phase == 0 and idx == 0:
                                for gg in range(3):
                                    scalar.copy(
                                        zdbg[:, gg * 512:(gg + 1) * 512],
                                        zps[sset][gg][:])
                            scalar.activation(fig[sset][0][:], zps[sset][0][:],
                                              AF.Sigmoid).then_inc(
                                sem["s_gact"], 1)
                            scalar.activation(fig[sset][1][:], zps[sset][1][:],
                                              AF.Sigmoid).then_inc(
                                sem["s_gact"], 1)
                            scalar.activation(fig[sset][2][:], zps[sset][2][:],
                                              AF.Tanh).then_inc(
                                sem["s_gact"], 1)

            if ph >= 3.4:
                block.scalar(p3_scl)

            if dbg and 3.4 <= ph <= 4:
                def pdbg_ffn(gpsimd):
                    gpsimd.wait_ge(sem["s_cdve"], 32 if ph >= 3.7 else 16)
                    gpsimd.dma_start(out=dbg_d["c1"],
                                     in_=c1_f[:, :]).then_inc(sem["s_st"], 16)
                    gpsimd.dma_start(out=dbg_d["c2"],
                                     in_=c2_bf[:, :]).then_inc(sem["s_st"], 16)
                    gpsimd.dma_start(out=dbg_d["xn"],
                                     in_=xn_fm[:, :]).then_inc(sem["s_st"], 16)
                    gpsimd.dma_start(out=dbg_d["stats"][:, 0:NRT],
                                     in_=st_mu[:, :]).then_inc(sem["s_st"], 16)
                    gpsimd.dma_start(out=dbg_d["stats"][:, NRT:2 * NRT],
                                     in_=st_rs[:, :]).then_inc(sem["s_st"], 16)
                    gpsimd.dma_start(out=dbg_d["xq"],
                                     in_=xq_aug[0:RA, :]).then_inc(sem["s_st"],
                                                                   16)
                    gpsimd.wait_ge(sem["s_st"], 16 * 6)

                block.gpsimd(pdbg_ffn)

        with ExitStack() as S3:
            uo_sb = S3.enter_context(nc.sbuf_tensor("uo_sb", [P, KC * D],
                                                    BF16))
            xv_buf = [S3.enter_context(nc.sbuf_tensor(f"xv_buf{i}",
                                                      [P, VCH * D], BF16))
                      for i in range(3)]
            p_sb = [S3.enter_context(nc.sbuf_tensor(f"p_sb{i}", [P, QBS],
                                                    BF16)) for i in range(3)]
            rl_sb = S3.enter_context(nc.sbuf_tensor("rl_sb", [P, 2], F32))
            h_sb = S3.enter_context(nc.sbuf_tensor("h_sb", [P, 2 * D], F32))
            xres = S3.enter_context(nc.sbuf_tensor("xres", [P, 2 * D], F32))
            out_b = S3.enter_context(nc.sbuf_tensor("out_b", [P, 2 * D], F32))
            o_ps = [[S3.enter_context(nc.psum_tensor(f"o_ps{q}_{f}", [P, 512],
                                                     F32))
                     for f in range(2)] for q in range(2)]
            ps_s = [S3.enter_context(nc.psum_tensor(f"ps_s{i}", [P, 512],
                                                    F32)) for i in range(2)]
            l_ps = S3.enter_context(nc.psum_tensor("l_ps", [P, 512], F32))
            ps_h = S3.enter_context(nc.psum_tensor("ps_h", [P, 512], F32))

            def p4_gp(gpsimd):
                gpsimd.wait_ge(sem["s_cc"], 1)
                for r in range(NCORE):
                    gpsimd.dma_start(
                        out=xk_all[0:RA, r * TLOC:(r + 1) * TLOC],
                        in_=cgat_d.ap()[r * CTR + TLOC + 2:
                                        (r + 1) * CTR, :].rearrange(
                            "(a b) c -> a (b c)", b=2),
                    ).then_inc(sem["s_kg"], 16)
                for r in range(NCORE):
                    gpsimd.dma_start(
                        out=mk8[0:1, r:r + 1],
                        in_=cgat_d.ap()[r * CTR + TLOC:r * CTR + TLOC + 1,
                                        0:1],
                    ).then_inc(sem["s_kg"], 16)
                gpsimd.dma_start(out=wo_sb[:], in_=wo_d.rearrange(
                    "p k c -> p (k c)")).then_inc(sem["s_wol"], 16)
                for i, (nm, o) in enumerate(slice_src):
                    if i % 2 == 0 or i < 48:
                        continue
                    slice_owner_wait(gpsimd, i)
                    gpsimd.dma_start(out=slb[i % NSLOT][:],
                                     in_=wbf_d[nm][o]).then_inc(sem["s_slg"],
                                                                16)
                gpsimd.wait_ge(sem["s_zmm"], 96)
                gpsimd.wait_ge(sem["s_cdve"], 32)
                gpsimd.dma_start(out=uo_sb[:], in_=uo_d.rearrange(
                    "p k c -> p (k c)")).then_inc(sem["s_wol"], 16)

            if ph >= 3.9:
                block.gpsimd(p4_gp)

            def p4_vec2(vector):
                vector.wait_ge(sem["s_kg"], 16 * 16)
                vector.wait_ge(sem["s_wol"], 16)
                vector.reduce_max(mkv2[0:1, 0:1], mk8[0:1, :],
                                  axis=AX.X).then_inc(sem["s_mk8r"], 1)
                vector.wait_ge(sem["s_mkv"], 1)
                vector.wait_ge(sem["s_qnx"], 4)
                vector.tensor_scalar(
                    xq_aug[RANK:RA, :], qn_bf[0:1, :], mkv[0:1, 0:1], -1.0,
                    op0=ALU.mult, op1=ALU.mult).then_inc(sem["s_mhat"], 1)

            if ph >= 3.9:
                block.vector(p4_vec2)

            def p4_scl(scalar):
                scalar.wait_ge(sem["s_mk8r"], 1)
                afence(scalar, scalar.activation(tmp_row[0:1, 0:1],
                                                 mkv2[0:1, 0:1], AF.Ln))
                scalar.activation(mkv[0:1, 0:1], tmp_row[0:1, 0:1], AF.Exp,
                                  scale=0.5).then_inc(sem["s_mkv"], 1)

            if ph >= 3.9:
                block.scalar(p4_scl)

            def p5_sync(sync):
                sync.wait_ge(sem["s_cc"], 1)
                sync.wait_ge(sem["s_zmm"], 96)
                sync.wait_ge(sem["s_cdve"], 32)
                for e in range(NEP):
                    b = e // NQB
                    for cv in range(NVC):
                        gc = e * NVC + cv
                        if gc >= 3:
                            sync.wait_ge(sem["s_vfree"], gc - 2)
                        row0 = (cv // 2) * CTR + b * POS + (cv % 2) * 512
                        sync.dma_start(
                            out=xv_buf[gc % 3].ap().rearrange(
                                "p (t f) -> p t f", f=D),
                            in_=cgat_d.ap()[row0:row0 + VCH * P, :].rearrange(
                                "(t p) f -> p t f", p=P),
                        ).then_inc(sem["s_vs"], 16)

            if ph >= 5:
                block.sync(p5_sync)

            sched = [(e, kt) for e in range(NEP) for kt in range(NKT)]

            def p5_tns(tensor):
                tensor.wait_ge(sem["s_kg"], 16 * 16)
                tensor.wait_ge(sem["s_mhat"], 1)
                tensor.wait_ge(sem["s_wol"], 16)
                tensor.wait_ge(sem["s_wb"], 64)
                tensor.wait_ge(sem["s_gact"], 96)
                tensor.wait_ge(sem["s_cdve"], 32)

                def emit_smm(g):
                    e2, kt2 = sched[g]
                    b2, qb2 = e2 // NQB, e2 % NQB
                    q0 = b2 * POS + qb2 * QBS
                    r, t = kt2 // 8, kt2 % 8
                    kcol = r * TLOC + b2 * POS + t * P
                    tensor.matmul(ps_s[g % 2][:, 0:QBS],
                                  xk_all[0:RA, kcol:kcol + P],
                                  xq_aug[0:RA, q0:q0 + QBS],
                                  start=True, stop=True).then_inc(sem["s_smm"],
                                                                  1)

                emit_smm(0)
                emit_smm(1)
                for gb in range(0, len(sched), 2):
                    eb, ktb = sched[gb]
                    cvb = ktb // VCH
                    if ktb % VCH == 0:
                        tensor.wait_ge(sem["s_vs"],
                                       16 * (eb * NVC + cvb + 1))
                    tensor.wait_ge(sem["s_exp"], gb + 2)
                    if gb + 2 < len(sched):
                        emit_smm(gb + 2)
                    if gb + 3 < len(sched):
                        emit_smm(gb + 3)
                    for g in (gb, gb + 1):
                        e, kt = sched[g]
                        b, qb = e // NQB, e % NQB
                        cv = kt // VCH
                        for qs in range(2):
                            if kt == 0 and e >= 1:
                                tensor.wait_ge(sem["s_eadd"],
                                               4 * (e - 1) + 2 * (qs + 1))
                            pt = p_sb[g % 3][:, qs * P:(qs + 1) * P]
                            for fh in range(2):
                                mo = tensor.matmul(
                                    o_ps[qs][fh][:], pt,
                                    xv_buf[(e * NVC + cv) % 3][
                                        :, (kt % VCH) * D + fh * 512:
                                        (kt % VCH) * D + fh * 512 + 512],
                                    start=(kt == 0), stop=(kt == NKT - 1),
                                    skip_group_check=True)
                                if (qs == 1 and fh == 1
                                        and kt % VCH == VCH - 1):
                                    mo.then_inc(sem["s_vfree"], 1)
                            mm = tensor.matmul(
                                l_ps[:, qs:qs + 1], pt, ones_sb[:, 0:1],
                                start=(kt == 0), stop=(kt == NKT - 1),
                                skip_group_check=True)
                            if qs == 1:
                                if kt == NKT - 1:
                                    mm.then_inc(sem["s_att"], 1)
                                else:
                                    mm.then_inc(sem["s_pfree"], 1)
                        if kt != NKT - 1:
                            continue
                        q0 = b * POS + qb * QBS
                        if e == 0:
                            tensor.wait_ge(sem["s_wol"], 32)
                            tensor.wait_ge(sem["s_xr"], 16)
                        for qs in range(2):
                            l0 = q0 + qs * P
                            j0 = qb * QBS + qs * P
                            for fh in range(2):
                                u = 4 * e + 2 * qs + fh
                                tensor.wait_ge(sem["s_tanh"], u)
                                for k in range(KC):
                                    mh = tensor.matmul(
                                        ps_h[:],
                                        xn_fm[:, k * TLOC + l0:
                                              k * TLOC + l0 + P],
                                        wo_sb[:, k * D + fh * 512:
                                              k * D + fh * 512 + 512],
                                        start=(k == 0), stop=False)
                                    if qs == 0 and fh == 0 and k == 0:
                                        mh.then_inc(sem["s_pfree"], 1)
                                for k in range(KC):
                                    tensor.matmul(
                                        ps_h[:],
                                        c2_bf[:, k * POS + j0:
                                              k * POS + j0 + P],
                                        uo_sb[:, k * D + fh * 512:
                                              k * D + fh * 512 + 512],
                                        start=False, stop=False)
                                tensor.matmul(
                                    ps_h[:], ones_sb[0:1, 0:P],
                                    b_bf["o"][0:1, fh * 512:(fh + 1) * 512],
                                    start=False, stop=True).then_inc(
                                    sem["s_hmm"], 1)

            if ph >= 5:
                block.tensor(p5_tns)

            def p5_scl(scalar):
                for e in range(NEP):
                    for kt in range(NKT):
                        g = e * NKT + kt
                        scalar.wait_ge(sem["s_smm"], g + 1)
                        if g >= 3:
                            scalar.wait_ge(sem["s_pfree"], g - 2)
                        scalar.activation(p_sb[g % 3][:],
                                          ps_s[g % 2][:, 0:QBS],
                                          AF.Exp).then_inc(sem["s_exp"], 1)
                    for u in range(4):
                        qs, fh = u // 2, u % 2
                        scalar.wait_ge(sem["s_hmm"], 4 * e + u + 1)
                        if e >= 1 and u == 0:
                            scalar.wait_ge(sem["s_eadd"], 4 * e)
                        scalar.activation(
                            h_sb[:, qs * D + fh * 512:qs * D + fh * 512 + 512],
                            ps_h[:], AF.Tanh).then_inc(sem["s_tanh"], 1)

            if ph >= 5:
                block.scalar(p5_scl)

            def p5_vec(vector):
                for e in range(NEP):
                    vector.wait_ge(sem["s_att"], e + 1)
                    vfence(vector, vector.reciprocal(rl_sb[:], l_ps[:, 0:2]))
                    for qs in range(2):
                        for fh in range(2):
                            u = 4 * e + 2 * qs + fh
                            vector.wait_ge(sem["s_tanh"], u + 1)
                            vector.scalar_tensor_tensor(
                                h_sb[:, qs * D + fh * 512:
                                     qs * D + fh * 512 + 512],
                                o_ps[qs][fh][:], rl_sb[:, qs:qs + 1],
                                h_sb[:, qs * D + fh * 512:
                                     qs * D + fh * 512 + 512],
                                op0=ALU.mult, op1=ALU.add).then_inc(
                                sem["s_eadd"], 1)

            if ph >= 5:
                block.vector(p5_vec)

            def p5_gp(gpsimd):
                gpsimd.wait_ge(sem["s_cdve"], 32)
                gpsimd.dma_start(
                    out=xres.ap().rearrange("p (t f) -> p t f", f=D),
                    in_=x_d[0:QBS, :].rearrange("(t p) f -> p t f",
                                                p=P)).then_inc(sem["s_xr"],
                                                               16)
                for e in range(NEP):
                    b, qb = e // NQB, e % NQB
                    for qs in range(2):
                        gpsimd.wait_ge(sem["s_eadd"], 4 * e + 2 * qs + 2)
                        gpsimd.wait_ge(sem["s_xr"], 16 * (e + 1))
                        if e >= 1 and qs == 0:
                            gpsimd.wait_ge(sem["s_st"], 32 * e)
                        gpsimd.tensor_add(
                            out_b[:, qs * D:(qs + 1) * D],
                            h_sb[:, qs * D:(qs + 1) * D],
                            xres[:, qs * D:(qs + 1) * D]).then_inc(
                            sem["s_pool"], 1)
                        gpsimd.wait_ge(sem["s_pool"], 2 * e + qs + 1)
                        l0 = b * POS + qb * QBS + qs * P
                        gpsimd.dma_start(
                            out=out_d[l0:l0 + P, :],
                            in_=out_b[:, qs * D:(qs + 1) * D]).then_inc(
                            sem["s_st"], 16)
                    if e + 1 < NEP:
                        b2, qb2 = (e + 1) // NQB, (e + 1) % NQB
                        x0 = b2 * POS + qb2 * QBS
                        gpsimd.dma_start(
                            out=xres.ap().rearrange("p (t f) -> p t f", f=D),
                            in_=x_d[x0:x0 + QBS, :].rearrange(
                                "(t p) f -> p t f", p=P)).then_inc(sem["s_xr"],
                                                                   16)
                gpsimd.wait_ge(sem["s_st"], 16 * 16)

            if ph >= 5:
                block.gpsimd(p5_gp)

            if dbg and ph >= 5:
                def pdbg5(gpsimd):
                    gpsimd.wait_ge(sem["s_st"], 16 * 16)
                    gpsimd.dma_start(out=dbg_d["xn"], in_=xn_fm[:, :]).then_inc(sem["s_st"], 16)
                    gpsimd.dma_start(out=dbg_d["xq"], in_=xq_aug[0:RA, :]).then_inc(sem["s_st"], 16)
                    gpsimd.dma_start(out=dbg_d["c2"], in_=c2_bf[:, :]).then_inc(sem["s_st"], 16)
                    gpsimd.dma_start(out=dbg_d["stats"][:, 0:NRT], in_=st_mu[:, :]).then_inc(sem["s_st"], 16)
                    gpsimd.dma_start(out=dbg_d["stats"][:, NRT:2 * NRT], in_=st_rs[:, :]).then_inc(sem["s_st"], 16)
                    gpsimd.dma_start(out=dbg_d["wo"], in_=wo_sb[:, :]).then_inc(sem["s_st"], 16)
                    gpsimd.dma_start(out=dbg_d["uo"], in_=uo_sb[:, :]).then_inc(sem["s_st"], 16)
                    gpsimd.dma_start(out=dbg_d["kall"], in_=xk_all[0:RA, :]).then_inc(sem["s_st"], 16)
                    gpsimd.dma_start(out=dbg_d["mkv"], in_=mkv[0:1, :]).then_inc(sem["s_st"], 16)
                    gpsimd.dma_start(out=dbg_d["hsb"], in_=h_sb[:, :]).then_inc(sem["s_st"], 16)
                    gpsimd.dma_start(out=dbg_d["kctb"][1:CROWS, :], in_=ctb_d.ap()[TLOC + 2:CTR, :].rearrange("(a b) c -> a (b c)", b=2)).then_inc(sem["s_st"], 16)
                    gpsimd.dma_start(out=dbg_d["kctb"][0:1, 0:1], in_=ctb_d.ap()[TLOC:TLOC + 1, 0:1]).then_inc(sem["s_st"], 16)
                    gpsimd.wait_ge(sem["s_st"], 16 * 28)

                block.gpsimd(pdbg5)

    return nc


_NC_CACHE = None


def kernel(**inputs):
    global _NC_CACHE
    x = np.asarray(inputs["L0_attn_x"], dtype=np.float32)
    Q = np.asarray(inputs["L0_attn_Q"], dtype=np.float32)
    K = np.asarray(inputs["L0_attn_K"], dtype=np.float32)
    VO = np.asarray(inputs["L0_attn_VO"], dtype=np.float32)
    assert x.shape == (2, 8192, 1024)

    def full_prep(w):
        return np.ascontiguousarray(
            w.reshape(KC, P, w.shape[1]).transpose(1, 0, 2))

    def slice_prep(wt):
        return np.ascontiguousarray(
            wt.reshape(KC, P, KC, P).transpose(2, 1, 0, 3))

    feed = {
        "qt": full_prep(Q.T),
        "kt": full_prep(K.T),
        "vo": full_prep(VO),
        "wo": full_prep(np.ascontiguousarray(
            np.asarray(inputs["L0_ffn_wo"], np.float32).T)),
        "uo": full_prep(np.ascontiguousarray(
            np.asarray(inputs["L0_ffn_uo"], np.float32).T)),
        "bf": np.asarray(inputs["L0_ffn_bf"], np.float32).reshape(1, D),
        "bi": np.asarray(inputs["L0_ffn_bi"], np.float32).reshape(1, D),
        "bc": np.asarray(inputs["L0_ffn_bc"], np.float32).reshape(1, D),
        "bo": np.asarray(inputs["L0_ffn_bo"], np.float32).reshape(1, D),
        "in_c": np.ascontiguousarray(
            np.asarray(inputs["L0_ffn_in_c"], np.float32).reshape(KC, P).T),
    }
    for nm in ("wf", "wi", "wc", "uf", "ui", "uc"):
        feed[nm] = slice_prep(
            np.ascontiguousarray(np.asarray(inputs[f"L0_ffn_{nm}"],
                                            np.float32).T))

    if _NC_CACHE is None:
        _NC_CACHE = build_nc()
    nc = _NC_CACHE

    in_maps = []
    for c in range(NCORE):
        sl = slice(c * POS, (c + 1) * POS)
        xs = np.ascontiguousarray(
            np.concatenate([x[0, sl], x[1, sl]], axis=0))
        m = dict(feed)
        m["x"] = xs
        in_maps.append(m)

    res = run_bass_kernel_spmd(nc, in_maps, core_ids=list(range(NCORE)))

    y = np.empty((2, 8192, 1024), dtype=np.float32)
    for c in range(NCORE):
        o = np.asarray(res.results[c]["out"])
        y[0, c * POS:(c + 1) * POS] = o[0:POS]
        y[1, c * POS:(c + 1) * POS] = o[POS:TLOC]
    return y


if __name__ == "__main__":
    nc = build_nc()
    print("built ok; instructions:", nc.n_instructions())


# version 23 (final)
# speedup vs baseline: 1.1073x; 1.0074x over previous
import numpy as np
from contextlib import ExitStack

import concourse.bass as bass
import concourse.mybir as mybir
from concourse.bass_utils import run_bass_kernel_spmd
import ml_dtypes

F32 = mybir.dt.float32
BF16 = mybir.dt.bfloat16
AF = mybir.ActivationFunctionType
ALU = mybir.AluOpType
AX = mybir.AxisListType

NCORE = 8
P = 128
D = 1024
KC = 8
TLOC = 2048
NRT = 16
POS = 1024
NB = 2
RANK = 64
RA = 65
CROWS = 66
QBS = 256
NSLOT = 12
NQB = POS // QBS
NEP = NB * NQB
NKT = 64
VCH = 4
NVC = NKT // VCH
EPS = 1e-5
INV32 = 1.0 / 32.0

CONV_W = ("uf", "ui", "uc", "wf", "wi", "wc")
CIDX = {nm: i for i, nm in enumerate(CONV_W)}


def build_nc(ph=5, dbg=False):
    nc = bass.Bass("TRN2", target_bir_lowering=False)

    x_d = nc.dram_tensor("x", [TLOC, D], F32, kind="ExternalInput").ap()
    qt_d = nc.dram_tensor("qt", [P, KC, RANK], F32, kind="ExternalInput").ap()
    kt_d = nc.dram_tensor("kt", [P, KC, RANK], F32, kind="ExternalInput").ap()
    vo_d = nc.dram_tensor("vo", [P, KC, D], F32, kind="ExternalInput").ap()
    wo_d = nc.dram_tensor("wo", [P, KC, D], F32, kind="ExternalInput").ap()
    uo_d = nc.dram_tensor("uo", [P, KC, D], F32, kind="ExternalInput").ap()
    ws_d = {}
    for nm in ("wf", "wi", "wc", "uf", "ui", "uc"):
        ws_d[nm] = nc.dram_tensor(nm, [KC, P, KC, P], F32,
                                  kind="ExternalInput").ap()
    b_d = {}
    for nm in ("bf", "bi", "bc", "bo"):
        b_d[nm] = nc.dram_tensor(nm, [1, D], F32, kind="ExternalInput").ap()
    inc_d = nc.dram_tensor("in_c", [P, KC], F32, kind="ExternalInput").ap()
    out_d = nc.dram_tensor("out", [TLOC, D], F32, kind="ExternalOutput").ap()

    wbf_d = {nm: nc.dram_tensor(f"wbf_{nm}", [KC, P, KC * P], BF16,
                                kind="Internal").ap()
             for nm in CONV_W}

    CTR = TLOC + 2 * CROWS
    ctb_d = nc.dram_tensor("ctb", [CTR, D], BF16, kind="Internal")
    cgat_d = nc.dram_tensor("cgat", [NCORE * CTR, D], BF16, kind="Internal")

    ident_d = nc.inline_tensor(np.eye(P, dtype=ml_dtypes.bfloat16),
                               "ident").ap()
    ones_d = nc.inline_tensor(np.ones((P, 512), dtype=ml_dtypes.bfloat16),
                              "onesc").ap()

    rg = [list(range(NCORE))]

    dbg_d = {}
    if dbg:
        dbg_d["xn"] = nc.dram_tensor("dbg_xn", [P, KC * TLOC], BF16,
                                     kind="ExternalOutput").ap()
        dbg_d["xq"] = nc.dram_tensor("dbg_xq", [RA, TLOC], BF16,
                                     kind="ExternalOutput").ap()
        dbg_d["c2"] = nc.dram_tensor("dbg_c2", [P, KC * POS], BF16,
                                     kind="ExternalOutput").ap()
        dbg_d["c1"] = nc.dram_tensor("dbg_c1", [P, KC * POS], F32,
                                     kind="ExternalOutput").ap()
        dbg_d["stats"] = nc.dram_tensor("dbg_stats", [P, 2 * NRT], F32,
                                        kind="ExternalOutput").ap()
        dbg_d["wo"] = nc.dram_tensor("dbg_wo", [P, KC * D], BF16,
                                     kind="ExternalOutput").ap()
        dbg_d["uo"] = nc.dram_tensor("dbg_uo", [P, KC * D], BF16,
                                     kind="ExternalOutput").ap()
        dbg_d["kall"] = nc.dram_tensor("dbg_kall", [RA, NCORE * TLOC], BF16,
                                       kind="ExternalOutput").ap()
        dbg_d["mkv"] = nc.dram_tensor("dbg_mkv", [1, 1], F32,
                                      kind="ExternalOutput").ap()
        dbg_d["hsb"] = nc.dram_tensor("dbg_hsb", [P, 2 * D], F32,
                                      kind="ExternalOutput").ap()
        dbg_d["kctb"] = nc.dram_tensor("dbg_kctb", [CROWS, TLOC], BF16,
                                       kind="ExternalOutput").ap()
        dbg_d["xkc"] = nc.dram_tensor("dbg_xkc", [RA, TLOC], BF16,
                                      kind="ExternalOutput").ap()

    with ExitStack() as S0:
        ident_sb = S0.enter_context(nc.sbuf_tensor("ident_sb", [P, P], BF16))
        ones_sb = S0.enter_context(nc.sbuf_tensor("ones_sb", [P, 512], BF16))
        xn_fm = S0.enter_context(nc.sbuf_tensor("xn_fm", [P, KC * TLOC], BF16))
        c2_bf = S0.enter_context(nc.sbuf_tensor("c2_bf", [P, KC * POS], BF16))
        xq_aug = S0.enter_context(nc.sbuf_tensor("xq_aug", [RA, TLOC], BF16))
        qn_bf = S0.enter_context(nc.sbuf_tensor("qn_bf", [1, TLOC], BF16))
        inc_f = S0.enter_context(nc.sbuf_tensor("inc_f", [P, KC], F32))
        inc_b = S0.enter_context(nc.sbuf_tensor("inc_b", [P, KC], BF16))
        b_bf = {nm: S0.enter_context(nc.sbuf_tensor(f"b_{nm}", [1, D], BF16))
                for nm in ("f", "i", "c", "o")}
        rf_bf = {nm: S0.enter_context(nc.sbuf_tensor(f"r_{nm}", [1, D], BF16))
                 for nm in ("f", "i", "c")}
        st_sum = S0.enter_context(nc.sbuf_tensor("st_sum", [P, NRT], F32))
        st_sq = S0.enter_context(nc.sbuf_tensor("st_sq", [P, NRT], F32))
        st_mu = S0.enter_context(nc.sbuf_tensor("st_mu", [P, NRT], F32))
        st_t0 = S0.enter_context(nc.sbuf_tensor("st_t0", [P, NRT], F32))
        st_var = S0.enter_context(nc.sbuf_tensor("st_var", [P, NRT], F32))
        st_rs = S0.enter_context(nc.sbuf_tensor("st_rs", [P, NRT], F32))
        tmp_row = S0.enter_context(nc.sbuf_tensor("tmp_row", [1, 512], F32))
        mk4 = S0.enter_context(nc.sbuf_tensor("mk4", [1, 4], F32))
        lmax_bf = S0.enter_context(nc.sbuf_tensor("lmax_bf", [1, 1], BF16))
        mk8 = S0.enter_context(nc.sbuf_tensor("mk8", [1, NCORE], BF16))
        mkv = S0.enter_context(nc.sbuf_tensor("mkv", [1, 1], F32))
        mkv2 = S0.enter_context(nc.sbuf_tensor("mkv2", [1, 1], F32))
        eps_col = S0.enter_context(nc.sbuf_tensor("eps_col", [P, 1], F32))

        sem_names = [
            "s_const", "s_x", "s_xb", "s_sum", "s_sq", "s_var", "s_rstd",
            "s_ap", "s_tp", "s_ev",
            "s_qkst", "s_qk", "s_conv", "s_wb", "s_slg",
            "s_p2mm", "s_xqe", "s_xke", "s_sq2xk", "s_nk", "s_nkr",
            "s_sq2xq", "s_nq", "s_qnl", "s_qnx",
            "s_vmm", "s_vev", "s_ct", "s_cc", "s_kg", "s_mk8r", "s_mkv",
            "s_mhat",
            "s_sl", "s_mv", "s_rfe",
            "s_zmm", "s_gact", "s_cdve",
            "s_wol",
            "s_smm", "s_exp", "s_pfree", "s_vs", "s_vfree",
            "s_att", "s_hmm", "s_tanh", "s_eadd", "s_pool", "s_xr", "s_st",
            "s_vch", "s_ach",
        ]
        sem = {nm: S0.enter_context(nc.semaphore(nm)) for nm in sem_names}
        vch_n = [0]
        ach_n = [0]

        def vfence(vector, inst):
            vch_n[0] += 1
            inst.then_inc(sem["s_vch"], 1)
            vector.wait_ge(sem["s_vch"], vch_n[0])

        def afence(scalar, inst):
            ach_n[0] += 1
            inst.then_inc(sem["s_ach"], 1)
            scalar.wait_ge(sem["s_ach"], ach_n[0])

        block = S0.enter_context(nc.Block())

        slice_src = []
        for nm in ("uf", "ui", "uc"):
            for o in range(KC):
                slice_src.append((nm, o))
        for o in range(KC):
            for nm in ("wf", "wi", "wc"):
                slice_src.append((nm, o))
        for o in range(KC):
            for nm in ("wf", "wi", "wc", "uf", "ui", "uc"):
                slice_src.append((nm, o))
        NSL = len(slice_src)

        def slice_owner_wait(eng, i):
            j = i - NSLOT
            if j < 0:
                return
            if j < 24:
                eng.wait_ge(sem["s_mv"], j + 1)
            elif j < 48:
                o_j = (j - 24) // 3
                eng.wait_ge(sem["s_gact"], 3 * (2 * o_j + 2))
            else:
                o_j = (j - 48) // 6
                eng.wait_ge(sem["s_gact"], 3 * (16 + 2 * o_j + 2))

        def wait_slices(eng, imax):
            if imax < 48:
                eng.wait_ge(sem["s_sl"], 16 * (imax + 1))
            else:
                eng.wait_ge(sem["s_sl"], 16 * (48 + (imax - 48) // 2 + 1))
                eng.wait_ge(sem["s_slg"], 16 * ((imax - 47) // 2))

        with ExitStack() as S1:
            x_full = S1.enter_context(nc.sbuf_tensor("x_full",
                                                     [P, NRT * D], F32))
            xn_tm = [S1.enter_context(nc.sbuf_tensor(f"xn_tm{i}", [P, D],
                                                     BF16)) for i in range(2)]
            sq_dump = S1.enter_context(nc.sbuf_tensor("sq_dump", [P, D], BF16))
            qt_sb = S1.enter_context(nc.sbuf_tensor("qt_sb", [P, KC * RANK],
                                                    BF16))
            kt_sb = S1.enter_context(nc.sbuf_tensor("kt_sb", [P, KC * RANK],
                                                    BF16))
            vo_sb = S1.enter_context(nc.sbuf_tensor("vo_sb", [P, KC * D],
                                                    BF16))
            xkc = S1.enter_context(nc.sbuf_tensor("xkc", [RA, TLOC], BF16))
            sq2 = S1.enter_context(nc.sbuf_tensor("sq2", [RANK, TLOC], BF16))
            xvc = [S1.enter_context(nc.sbuf_tensor(f"xvc{i}", [P, D], BF16))
                   for i in range(6)]
            ps_t = [S1.enter_context(nc.psum_tensor(f"ps_t{i}", [P, D], BF16))
                    for i in range(2)]
            ps_q = [S1.enter_context(nc.psum_tensor(f"ps_q{i}", [RANK, 512],
                                                    F32)) for i in range(2)]
            ps_v = [S1.enter_context(nc.psum_tensor(f"ps_v{i}", [P, 512], F32))
                    for i in range(2)]
            ps_n = [S1.enter_context(nc.psum_tensor(f"ps_n{i}", [1, 512], F32))
                    for i in range(2)]

            def p12_sync(sync):
                sync.dma_start(out=ident_sb[:], in_=ident_d).then_inc(
                    sem["s_const"], 16)
                sync.dma_start(out=ones_sb[:], in_=ones_d).then_inc(
                    sem["s_const"], 16)
                for rt in range(NRT):
                    sync.dma_start(out=x_full[:, rt * D:(rt + 1) * D],
                                   in_=x_d[rt * P:(rt + 1) * P, :]).then_inc(
                        sem["s_x"], 16)
                sync.dma_start(out=inc_f[:], in_=inc_d).then_inc(
                    sem["s_qkst"], 16)
                for rc in range(NRT):
                    sync.wait_ge(sem["s_vev"], 2 * rc + 2)
                    sync.dma_start(out=ctb_d.ap()[rc * P:(rc + 1) * P, :],
                                   in_=xvc[rc % 6][:]).then_inc(sem["s_ct"],
                                                                16)
                sync.wait_ge(sem["s_xke"], 5)
                sync.dma_start(out=ctb_d.ap()[TLOC + 2:CTR, :].rearrange(
                    "(a b) c -> a (b c)", b=2),
                    in_=xkc[0:RA, :]).then_inc(sem["s_ct"], 16)
                sync.wait_ge(sem["s_nkr"], 5)
                sync.dma_start(out=ctb_d.ap()[TLOC:TLOC + 1, 0:1],
                               in_=lmax_bf[0:1, 0:1]).then_inc(sem["s_ct"], 16)
                if dbg:
                    sync.dma_start(out=dbg_d["xkc"],
                                   in_=xkc[0:RA, :]).then_inc(sem["s_const"],
                                                              16)

            block.sync(p12_sync)

            def p1_gp(gpsimd):
                for nm, dst in (("bf", b_bf["f"]), ("bi", b_bf["i"]),
                                ("bc", b_bf["c"]), ("bo", b_bf["o"])):
                    gpsimd.dma_start(out=dst[0:1, :], in_=b_d[nm]).then_inc(
                        sem["s_wb"], 16)
                gpsimd.dma_start(out=qt_sb[:], in_=qt_d.rearrange(
                    "p k c -> p (k c)")).then_inc(sem["s_qk"], 16)
                gpsimd.dma_start(out=kt_sb[:], in_=kt_d.rearrange(
                    "p k c -> p (k c)")).then_inc(sem["s_qk"], 16)
                gpsimd.dma_start(out=inc_b[:], in_=inc_d).then_inc(
                    sem["s_qk"], 16)
                gpsimd.wait_ge(sem["s_x"], 16 * NRT)
                gpsimd.dma_start(out=vo_sb[:], in_=vo_d.rearrange(
                    "p k c -> p (k c)")).then_inc(sem["s_qk"], 16)
                for nm in CONV_W:
                    gpsimd.dma_start(
                        out=wbf_d[nm].rearrange("o p f -> (o p) f"),
                        in_=ws_d[nm].rearrange("o p k c -> (o p) (k c)"),
                    ).then_inc(sem["s_conv"], 16)
                gpsimd.dma_start(out=sq_dump[0:1, 0:2],
                                 in_=x_d[0:1, 0:2]).then_inc(sem["s_conv"], 16)

            block.gpsimd(p1_gp)

            def p2_gp(gpsimd):
                gpsimd.wait_ge(sem["s_ct"], 16 * 18)
                gpsimd.wait_ge(sem["s_qk"], 64)
                gpsimd.wait_ge(sem["s_conv"], 16 * 7)
                gpsimd.collective_compute(
                    "AllGather", ALU.bypass, replica_groups=rg,
                    ins=[ctb_d.ap().opt()], outs=[cgat_d.ap().opt()],
                ).then_inc(sem["s_cc"])

            if ph >= 3.9:
                block.gpsimd(p2_gp)

            def p1_vec(vector):
                vector.memset(eps_col[:], EPS)
                for rt in range(NRT):
                    vector.wait_ge(sem["s_x"], 16 * min(rt + 3, NRT))
                    if rt == NRT - 1:
                        vector.wait_ge(sem["s_qkst"], 16)
                    vector.reduce_sum(st_sum[:, rt:rt + 1],
                                      x_full[:, rt * D:(rt + 1) * D],
                                      axis=AX.X).then_inc(sem["s_sum"], 1)
                vector.wait_ge(sem["s_sum"], NRT)
                vector.wait_ge(sem["s_sq"], NRT)
                vfence(vector, vector.tensor_scalar_mul(st_mu[:], st_sum[:],
                                                        1.0 / D))
                vfence(vector, vector.tensor_scalar_mul(st_t0[:], st_sq[:],
                                                        1.0 / D))
                vfence(vector, vector.tensor_mul(st_var[:], st_mu[:],
                                                 st_mu[:]))
                vector.tensor_sub(st_var[:], st_t0[:], st_var[:]).then_inc(
                    sem["s_var"], 1)
                for rt in range(NRT):
                    if rt == 0:
                        vector.wait_ge(sem["s_rstd"], 1)
                    if rt >= 2:
                        vector.wait_ge(sem["s_tp"], KC * (rt - 1))
                    vector.tensor_scalar(
                        xn_tm[rt % 2][:], x_full[:, rt * D:(rt + 1) * D],
                        st_mu[:, rt:rt + 1], st_rs[:, rt:rt + 1],
                        op0=ALU.subtract, op1=ALU.mult).then_inc(sem["s_ap"],
                                                                 1)

            block.vector(p1_vec)

            def p1_scl(scalar):
                for rt in range(NRT):
                    scalar.wait_ge(sem["s_x"], 16 * min(rt + 3, NRT))
                    if rt == NRT - 1:
                        scalar.wait_ge(sem["s_qkst"], 16)
                    scalar.activation(sq_dump[:], x_full[:, rt * D:(rt + 1) * D],
                                      AF.Square,
                                      accum_out=st_sq[:, rt:rt + 1]).then_inc(
                        sem["s_sq"], 1)
                scalar.wait_ge(sem["s_var"], 1)
                afence(scalar, scalar.activation(st_t0[:], st_var[:], AF.Ln,
                                                 bias=eps_col[:, 0:1]))
                scalar.activation(st_rs[:], st_t0[:], AF.Exp,
                                  scale=-0.5).then_inc(sem["s_rstd"], 1)
                for rt in range(NRT):
                    for k in range(KC):
                        g = KC * rt + k
                        scalar.wait_ge(sem["s_tp"], g + 1)
                        scalar.copy(
                            xn_fm[:, k * TLOC + rt * P:
                                  k * TLOC + (rt + 1) * P],
                            ps_t[g % 2][:, 0:P]).then_inc(sem["s_ev"], 1)

            block.scalar(p1_scl)

            def p1_tns(tensor):
                tensor.wait_ge(sem["s_const"], 32)
                for rt in range(NRT):
                    tensor.wait_ge(sem["s_ap"], rt + 1)
                    for k in range(KC):
                        g = KC * rt + k
                        if g >= 2:
                            tensor.wait_ge(sem["s_ev"], g - 1)
                        tensor.transpose(ps_t[g % 2][:, 0:P],
                                         xn_tm[rt % 2][:, k * P:(k + 1) * P],
                                         ident_sb[:]).then_inc(sem["s_tp"], 1)

            block.tensor(p1_tns)

            def p2_vec(vector):
                if ph >= 2.15:
                    for n in range(4):
                        vector.wait_ge(sem["s_p2mm"], n + 1)
                        vector.tensor_scalar_mul(
                            xq_aug[0:RANK, n * 512:(n + 1) * 512],
                            ps_q[n % 2][:], INV32).then_inc(sem["s_xqe"], 1)
                    for n in range(4):
                        vector.wait_ge(sem["s_p2mm"], 4 + n + 1)
                        vector.tensor_copy(
                            xkc[0:RANK, n * 512:(n + 1) * 512],
                            ps_q[n % 2][:]).then_inc(sem["s_xke"], 1)
                    vector.memset(xkc[RANK:RA, :], 1.0).then_inc(sem["s_xke"],
                                                                 1)
                if ph >= 2.25:
                    for n in range(4):
                        vector.wait_ge(sem["s_nk"], n + 1)
                        vector.reduce_max(mk4[0:1, n:n + 1],
                                          ps_n[n % 2][0:1, :],
                                          axis=AX.X).then_inc(sem["s_nkr"], 1)
                    vector.wait_ge(sem["s_nkr"], 4)
                    vector.reduce_max(lmax_bf[0:1, 0:1], mk4[0:1, :],
                                      axis=AX.X).then_inc(sem["s_nkr"], 1)

            block.vector(p2_vec)

            def p2_tns(tensor):
                tensor.wait_ge(sem["s_ev"], NRT * KC)
                tensor.wait_ge(sem["s_qk"], 48)
                for proj, wsb in ((0, qt_sb), (1, kt_sb)):
                    for n in range(4):
                        gi = proj * 4 + n
                        if proj == 0 and n >= 2:
                            tensor.wait_ge(sem["s_xqe"], n - 1)
                        if proj == 1 and n == 0:
                            tensor.wait_ge(sem["s_xqe"], 3)
                        if proj == 1 and n == 1:
                            tensor.wait_ge(sem["s_xqe"], 4)
                        if proj == 1 and n >= 2:
                            tensor.wait_ge(sem["s_xke"], n - 1)
                        for k in range(KC):
                            mmq = tensor.matmul(
                                ps_q[gi % 2][:],
                                wsb[:, k * RANK:(k + 1) * RANK],
                                xn_fm[:, k * TLOC + n * 512:
                                      k * TLOC + (n + 1) * 512],
                                start=(k == 0), stop=(k == KC - 1))
                        mmq.then_inc(sem["s_p2mm"], 1)
                if ph < 2.25:
                    return
                tensor.wait_ge(sem["s_sq2xk"], 1)
                for n in range(4):
                    if n >= 2:
                        tensor.wait_ge(sem["s_nkr"], n - 1)
                    tensor.matmul(ps_n[n % 2][:], ones_sb[0:RANK, 0:1],
                                  sq2[:, n * 512:(n + 1) * 512],
                                  start=True, stop=True).then_inc(sem["s_nk"],
                                                                  1)
                tensor.wait_ge(sem["s_sq2xq"], 1)
                tensor.wait_ge(sem["s_nkr"], 4)
                for n in range(4):
                    if n >= 2:
                        tensor.wait_ge(sem["s_qnl"], n - 1)
                    tensor.matmul(ps_n[n % 2][:], ones_sb[0:RANK, 0:1],
                                  sq2[:, n * 512:(n + 1) * 512],
                                  start=True, stop=True).then_inc(sem["s_nq"],
                                                                  1)
                if ph < 2.35:
                    return
                tensor.wait_ge(sem["s_qk"], 64)
                tensor.wait_ge(sem["s_conv"], 16)
                for rc in range(NRT):
                    for fh in range(2):
                        idx = 2 * rc + fh
                        if idx >= 2:
                            tensor.wait_ge(sem["s_vev"], idx - 1)
                        for k in range(KC):
                            mmv = tensor.matmul(
                                ps_v[idx % 2][:],
                                xn_fm[:, k * TLOC + rc * P:
                                      k * TLOC + (rc + 1) * P],
                                vo_sb[:, k * D + fh * 512:
                                      k * D + (fh + 1) * 512],
                                start=(k == 0), stop=(k == KC - 1))
                        mmv.then_inc(sem["s_vmm"], 1)

            if ph >= 2.15:
                block.tensor(p2_tns)

            def p2_scl(scalar):
                scalar.wait_ge(sem["s_xke"], 4)
                scalar.activation(sq2[:], xkc[0:RANK, :], AF.Square).then_inc(
                    sem["s_sq2xk"], 1)
                scalar.wait_ge(sem["s_nk"], 4)
                scalar.wait_ge(sem["s_xqe"], 4)
                scalar.activation(sq2[:], xq_aug[0:RANK, :],
                                  AF.Square).then_inc(sem["s_sq2xq"], 1)
                for n in range(4):
                    scalar.wait_ge(sem["s_nq"], n + 1)
                    scalar.activation(tmp_row[0:1, :], ps_n[n % 2][0:1, :],
                                      AF.Ln).then_inc(sem["s_qnl"], 1)
                    scalar.wait_ge(sem["s_qnl"], n + 1)
                    scalar.activation(qn_bf[0:1, n * 512:(n + 1) * 512],
                                      tmp_row[0:1, :], AF.Exp,
                                      scale=0.5).then_inc(sem["s_qnx"], 1)
                if ph < 2.35:
                    return
                for rc in range(NRT):
                    for fh in range(2):
                        idx = 2 * rc + fh
                        scalar.wait_ge(sem["s_vmm"], idx + 1)
                        if rc >= 6 and fh == 0:
                            scalar.wait_ge(sem["s_ct"], 16 * (rc - 5))
                        scalar.copy(xvc[rc % 6][:, fh * 512:(fh + 1) * 512],
                                    ps_v[idx % 2][:]).then_inc(sem["s_vev"], 1)

            if ph >= 2.25:
                block.scalar(p2_scl)

        with ExitStack() as S2:
            xk_all = S2.enter_context(nc.sbuf_tensor(
                "xk_all", [RA, NCORE * TLOC], BF16, side="right"))
            wo_sb = S2.enter_context(nc.sbuf_tensor(
                "wo_sb", [P, KC * D], BF16, side="right"))
            slb = [S2.enter_context(nc.sbuf_tensor(f"slb{i}", [P, D], BF16))
                   for i in range(NSLOT)]
            c1_f = S2.enter_context(nc.sbuf_tensor("c1_f", [P, KC * POS], F32))
            c1_b = S2.enter_context(nc.sbuf_tensor("c1_b", [P, KC * POS],
                                                   BF16))
            fig = [[S2.enter_context(nc.sbuf_tensor(f"fig{s}_{g}", [P, 512],
                                                    F32))
                    for g in range(3)] for s in range(2)]
            zps = [[S2.enter_context(nc.psum_tensor(f"zps{s}_{g}", [P, 512],
                                                    F32))
                    for g in range(3)] for s in range(2)]
            ps_r = [S2.enter_context(nc.psum_tensor(f"ps_r{i}", [1, 512],
                                                    F32)) for i in range(2)]
            zdbg = S2.enter_context(nc.sbuf_tensor(
                "zdbg", [P, 3 * 512], F32)) if (dbg and ph <= 4) else None

            def p3_sync(sync):
                sync.wait_ge(sem["s_ap"], NRT)
                sync.wait_ge(sem["s_tp"], NRT * KC)
                sync.wait_ge(sem["s_sum"], NRT)
                sync.wait_ge(sem["s_sq"], NRT)
                sync.wait_ge(sem["s_p2mm"], 8)
                for i, (nm, o) in enumerate(slice_src):
                    if i >= 48 and i % 2 == 1:
                        continue
                    if i == NSLOT:
                        sync.wait_ge(sem["s_cc"], 1)
                    sync.wait_ge(sem["s_conv"], 16 * (CIDX[nm] + 2))
                    slice_owner_wait(sync, i)
                    sync.dma_start(out=slb[i % NSLOT][:],
                                   in_=wbf_d[nm][o]).then_inc(sem["s_sl"], 16)

            if ph >= 3.4:
                block.sync(p3_sync)

            def p3_vec(vector):
                gate_names = ("f", "i", "c")

                def rf_evac(idx):
                    g, o = idx // KC, idx % KC
                    nm = gate_names[g]
                    vector.wait_ge(sem["s_mv"], idx + 1)
                    vector.wait_ge(sem["s_wb"], 16 * (g + 1))
                    vector.tensor_add(
                        rf_bf[nm][0:1, o * P:(o + 1) * P],
                        ps_r[idx % 2][0:1, 0:P],
                        b_bf[nm][0:1, o * P:(o + 1) * P]).then_inc(
                        sem["s_rfe"], 1)

                def c1_chunk(o, pc):
                    idx = 2 * o + pc
                    sset = idx % 2
                    vector.wait_ge(sem["s_gact"], 3 * (idx + 1))
                    base = o * POS + pc * 512
                    vector.tensor_scalar_mul(fig[sset][0][:], fig[sset][0][:],
                                             inc_f[:, o:o + 1])
                    vfence(vector, vector.tensor_mul(
                        fig[sset][1][:], fig[sset][1][:], fig[sset][2][:]))
                    vfence(vector, vector.tensor_add(
                        c1_f[:, base:base + 512],
                        fig[sset][0][:], fig[sset][1][:]))
                    vector.tensor_copy(c1_b[:, base:base + 512],
                                       c1_f[:, base:base + 512]).then_inc(
                        sem["s_cdve"], 1)

                def c2_chunk(o, pc):
                    idx = 16 + 2 * o + pc
                    sset = idx % 2
                    vector.wait_ge(sem["s_gact"], 3 * (idx + 1))
                    base = o * POS + pc * 512
                    vector.tensor_mul(fig[sset][0][:], fig[sset][0][:],
                                      c1_f[:, base:base + 512])
                    vfence(vector, vector.tensor_mul(
                        fig[sset][1][:], fig[sset][1][:], fig[sset][2][:]))
                    vector.tensor_add(c2_bf[:, base:base + 512],
                                      fig[sset][0][:],
                                      fig[sset][1][:]).then_inc(sem["s_cdve"],
                                                                1)

                for j in range(24):
                    rf_evac(j)
                for o in range(KC):
                    if o >= 2:
                        for pc in range(2):
                            c1_chunk(o - 2, pc)
                for o in (KC - 2, KC - 1):
                    for pc in range(2):
                        c1_chunk(o, pc)
                if ph < 3.7:
                    return
                for o in range(KC):
                    if o >= 2:
                        for pc in range(2):
                            c2_chunk(o - 2, pc)
                for o in (KC - 2, KC - 1):
                    for pc in range(2):
                        c2_chunk(o, pc)

            if ph >= 3.4:
                block.vector(p3_vec)

            def p3_tns(tensor):
                tensor.wait_ge(sem["s_qk"], 48)
                if ph >= 3.9:
                    tensor.wait_ge(sem["s_cc"], 1)
                for g in range(3):
                    for o in range(KC):
                        idx = g * KC + o
                        wait_slices(tensor, idx)
                        if idx >= 2:
                            tensor.wait_ge(sem["s_rfe"], idx - 1)
                        sl = slb[idx % NSLOT]
                        for k in range(KC):
                            mmr = tensor.matmul(
                                ps_r[idx % 2][0:1, 0:P],
                                inc_b[:, k:k + 1],
                                sl[:, k * P:(k + 1) * P],
                                start=(k == 0), stop=(k == KC - 1))
                        mmr.then_inc(sem["s_mv"], 1)
                tensor.wait_ge(sem["s_rfe"], 24)
                for o in range(KC):
                    wait_slices(tensor, 24 + 3 * o + 2)
                    for pc in range(2):
                        idx = 2 * o + pc
                        sset = idx % 2
                        if idx >= 2:
                            tensor.wait_ge(sem["s_gact"], 3 * (idx - 1))
                        for g, nm in enumerate(("f", "i", "c")):
                            sl = slb[(24 + 3 * o + g) % NSLOT]
                            for k in range(KC):
                                tensor.matmul(
                                    zps[sset][g][:],
                                    sl[:, k * P:(k + 1) * P],
                                    xn_fm[:, k * TLOC + pc * 512:
                                          k * TLOC + pc * 512 + 512],
                                    start=(k == 0), stop=False)
                            tensor.matmul(
                                zps[sset][g][:],
                                rf_bf[nm][0:1, o * P:(o + 1) * P],
                                ones_sb[0:1, :],
                                start=False, stop=True).then_inc(sem["s_zmm"],
                                                                 1)
                if ph < 3.7:
                    return
                tensor.wait_ge(sem["s_cdve"], 16)
                for o in range(KC):
                    wait_slices(tensor, 48 + 6 * o + 5)
                    for pc in range(2):
                        idx = 16 + 2 * o + pc
                        sset = idx % 2
                        tensor.wait_ge(sem["s_gact"], 3 * (idx - 1))
                        for g, nm in enumerate(("f", "i", "c")):
                            slw = slb[(48 + 6 * o + g) % NSLOT]
                            slu = slb[(48 + 6 * o + 3 + g) % NSLOT]
                            for k in range(KC):
                                tensor.matmul(
                                    zps[sset][g][:],
                                    slw[:, k * P:(k + 1) * P],
                                    xn_fm[:, k * TLOC + POS + pc * 512:
                                          k * TLOC + POS + pc * 512 + 512],
                                    start=(k == 0), stop=False)
                            for k in range(KC):
                                tensor.matmul(
                                    zps[sset][g][:],
                                    slu[:, k * P:(k + 1) * P],
                                    c1_b[:, k * POS + pc * 512:
                                         k * POS + pc * 512 + 512],
                                    start=False, stop=False)
                            tensor.matmul(
                                zps[sset][g][:],
                                b_bf[nm][0:1, o * P:(o + 1) * P],
                                ones_sb[0:1, :],
                                start=False, stop=True).then_inc(sem["s_zmm"],
                                                                 1)

            if ph >= 3.4:
                block.tensor(p3_tns)

            def p3_scl(scalar):
                for phase in range(2 if ph >= 3.7 else 1):
                    for o in range(KC):
                        for pc in range(2):
                            idx = phase * 16 + 2 * o + pc
                            sset = idx % 2
                            scalar.wait_ge(sem["s_zmm"], 3 * (idx + 1))
                            if idx >= 2:
                                scalar.wait_ge(sem["s_cdve"], idx - 1)
                            if zdbg is not None and phase == 0 and idx == 0:
                                for gg in range(3):
                                    scalar.copy(
                                        zdbg[:, gg * 512:(gg + 1) * 512],
                                        zps[sset][gg][:])
                            scalar.activation(fig[sset][0][:], zps[sset][0][:],
                                              AF.Sigmoid).then_inc(
                                sem["s_gact"], 1)
                            scalar.activation(fig[sset][1][:], zps[sset][1][:],
                                              AF.Sigmoid).then_inc(
                                sem["s_gact"], 1)
                            scalar.activation(fig[sset][2][:], zps[sset][2][:],
                                              AF.Tanh).then_inc(
                                sem["s_gact"], 1)

            if ph >= 3.4:
                block.scalar(p3_scl)

            if dbg and 3.4 <= ph <= 4:
                def pdbg_ffn(gpsimd):
                    gpsimd.wait_ge(sem["s_cdve"], 32 if ph >= 3.7 else 16)
                    gpsimd.dma_start(out=dbg_d["c1"],
                                     in_=c1_f[:, :]).then_inc(sem["s_st"], 16)
                    gpsimd.dma_start(out=dbg_d["c2"],
                                     in_=c2_bf[:, :]).then_inc(sem["s_st"], 16)
                    gpsimd.dma_start(out=dbg_d["xn"],
                                     in_=xn_fm[:, :]).then_inc(sem["s_st"], 16)
                    gpsimd.dma_start(out=dbg_d["stats"][:, 0:NRT],
                                     in_=st_mu[:, :]).then_inc(sem["s_st"], 16)
                    gpsimd.dma_start(out=dbg_d["stats"][:, NRT:2 * NRT],
                                     in_=st_rs[:, :]).then_inc(sem["s_st"], 16)
                    gpsimd.dma_start(out=dbg_d["xq"],
                                     in_=xq_aug[0:RA, :]).then_inc(sem["s_st"],
                                                                   16)
                    gpsimd.wait_ge(sem["s_st"], 16 * 6)

                block.gpsimd(pdbg_ffn)

        with ExitStack() as S3:
            uo_sb = S3.enter_context(nc.sbuf_tensor("uo_sb", [P, KC * D],
                                                    BF16))
            xv_buf = [S3.enter_context(nc.sbuf_tensor(f"xv_buf{i}",
                                                      [P, VCH * D], BF16))
                      for i in range(4)]
            p_sb = [S3.enter_context(nc.sbuf_tensor(f"p_sb{i}", [P, QBS],
                                                    BF16)) for i in range(3)]
            rl_sb = S3.enter_context(nc.sbuf_tensor("rl_sb", [P, 2], F32))
            h_sb = S3.enter_context(nc.sbuf_tensor("h_sb", [P, 2 * D], F32))
            xres = S3.enter_context(nc.sbuf_tensor("xres", [P, 2 * D], F32))
            out_b = S3.enter_context(nc.sbuf_tensor("out_b", [P, 2 * D], F32))
            o_ps = [[S3.enter_context(nc.psum_tensor(f"o_ps{q}_{f}", [P, 512],
                                                     F32))
                     for f in range(2)] for q in range(2)]
            ps_s = [S3.enter_context(nc.psum_tensor(f"ps_s{i}", [P, 512],
                                                    F32)) for i in range(2)]
            l_ps = S3.enter_context(nc.psum_tensor("l_ps", [P, 512], F32))
            ps_h = S3.enter_context(nc.psum_tensor("ps_h", [P, 512], F32))

            def p4_gp(gpsimd):
                gpsimd.wait_ge(sem["s_cc"], 1)
                for r in range(NCORE):
                    gpsimd.dma_start(
                        out=xk_all[0:RA, r * TLOC:(r + 1) * TLOC],
                        in_=cgat_d.ap()[r * CTR + TLOC + 2:
                                        (r + 1) * CTR, :].rearrange(
                            "(a b) c -> a (b c)", b=2),
                    ).then_inc(sem["s_kg"], 16)
                for r in range(NCORE):
                    gpsimd.dma_start(
                        out=mk8[0:1, r:r + 1],
                        in_=cgat_d.ap()[r * CTR + TLOC:r * CTR + TLOC + 1,
                                        0:1],
                    ).then_inc(sem["s_kg"], 16)
                gpsimd.dma_start(out=wo_sb[:], in_=wo_d.rearrange(
                    "p k c -> p (k c)")).then_inc(sem["s_wol"], 16)
                for i, (nm, o) in enumerate(slice_src):
                    if i % 2 == 0 or i < 48:
                        continue
                    slice_owner_wait(gpsimd, i)
                    gpsimd.dma_start(out=slb[i % NSLOT][:],
                                     in_=wbf_d[nm][o]).then_inc(sem["s_slg"],
                                                                16)
                gpsimd.wait_ge(sem["s_zmm"], 96)
                gpsimd.wait_ge(sem["s_cdve"], 32)
                gpsimd.dma_start(out=uo_sb[:], in_=uo_d.rearrange(
                    "p k c -> p (k c)")).then_inc(sem["s_wol"], 16)

            if ph >= 3.9:
                block.gpsimd(p4_gp)

            def p4_vec2(vector):
                vector.wait_ge(sem["s_kg"], 16 * 16)
                vector.wait_ge(sem["s_wol"], 16)
                vector.reduce_max(mkv2[0:1, 0:1], mk8[0:1, :],
                                  axis=AX.X).then_inc(sem["s_mk8r"], 1)
                vector.wait_ge(sem["s_mkv"], 1)
                vector.wait_ge(sem["s_qnx"], 4)
                vector.tensor_scalar(
                    xq_aug[RANK:RA, :], qn_bf[0:1, :], mkv[0:1, 0:1], -1.0,
                    op0=ALU.mult, op1=ALU.mult).then_inc(sem["s_mhat"], 1)

            if ph >= 3.9:
                block.vector(p4_vec2)

            def p4_scl(scalar):
                scalar.wait_ge(sem["s_mk8r"], 1)
                afence(scalar, scalar.activation(tmp_row[0:1, 0:1],
                                                 mkv2[0:1, 0:1], AF.Ln))
                scalar.activation(mkv[0:1, 0:1], tmp_row[0:1, 0:1], AF.Exp,
                                  scale=0.5).then_inc(sem["s_mkv"], 1)

            if ph >= 3.9:
                block.scalar(p4_scl)

            def p5_sync(sync):
                sync.wait_ge(sem["s_cc"], 1)
                sync.wait_ge(sem["s_zmm"], 96)
                sync.wait_ge(sem["s_cdve"], 32)
                for e in range(NEP):
                    b = e // NQB
                    for cv in range(NVC):
                        gc = e * NVC + cv
                        if gc >= 4:
                            sync.wait_ge(sem["s_vfree"], gc - 3)
                        row0 = (cv // 2) * CTR + b * POS + (cv % 2) * 512
                        sync.dma_start(
                            out=xv_buf[gc % 4].ap().rearrange(
                                "p (t f) -> p t f", f=D),
                            in_=cgat_d.ap()[row0:row0 + VCH * P, :].rearrange(
                                "(t p) f -> p t f", p=P),
                        ).then_inc(sem["s_vs"], 16)

            if ph >= 5:
                block.sync(p5_sync)

            sched = [(e, kt) for e in range(NEP) for kt in range(NKT)]

            def p5_tns(tensor):
                tensor.wait_ge(sem["s_kg"], 16 * 16)
                tensor.wait_ge(sem["s_mhat"], 1)
                tensor.wait_ge(sem["s_wol"], 16)
                tensor.wait_ge(sem["s_wb"], 64)
                tensor.wait_ge(sem["s_gact"], 96)
                tensor.wait_ge(sem["s_cdve"], 32)

                def emit_smm(g):
                    e2, kt2 = sched[g]
                    b2, qb2 = e2 // NQB, e2 % NQB
                    q0 = b2 * POS + qb2 * QBS
                    r, t = kt2 // 8, kt2 % 8
                    kcol = r * TLOC + b2 * POS + t * P
                    tensor.matmul(ps_s[g % 2][:, 0:QBS],
                                  xk_all[0:RA, kcol:kcol + P],
                                  xq_aug[0:RA, q0:q0 + QBS],
                                  start=True, stop=True).then_inc(sem["s_smm"],
                                                                  1)

                emit_smm(0)
                emit_smm(1)
                for gb in range(0, len(sched), 2):
                    eb, ktb = sched[gb]
                    cvb = ktb // VCH
                    if ktb % VCH == 0:
                        tensor.wait_ge(sem["s_vs"],
                                       16 * (eb * NVC + cvb + 1))
                    tensor.wait_ge(sem["s_exp"], gb + 2)
                    if gb + 2 < len(sched):
                        emit_smm(gb + 2)
                    if gb + 3 < len(sched):
                        emit_smm(gb + 3)
                    for g in (gb, gb + 1):
                        e, kt = sched[g]
                        b, qb = e // NQB, e % NQB
                        cv = kt // VCH
                        for qs in range(2):
                            if kt == 0 and e >= 1:
                                tensor.wait_ge(sem["s_eadd"],
                                               4 * (e - 1) + 2 * (qs + 1))
                            pt = p_sb[g % 3][:, qs * P:(qs + 1) * P]
                            for fh in range(2):
                                mo = tensor.matmul(
                                    o_ps[qs][fh][:], pt,
                                    xv_buf[(e * NVC + cv) % 4][
                                        :, (kt % VCH) * D + fh * 512:
                                        (kt % VCH) * D + fh * 512 + 512],
                                    start=(kt == 0), stop=(kt == NKT - 1),
                                    skip_group_check=True)
                                if (qs == 1 and fh == 1
                                        and kt % VCH == VCH - 1):
                                    mo.then_inc(sem["s_vfree"], 1)
                            mm = tensor.matmul(
                                l_ps[:, qs:qs + 1], pt, ones_sb[:, 0:1],
                                start=(kt == 0), stop=(kt == NKT - 1),
                                skip_group_check=True)
                            if qs == 1:
                                if kt == NKT - 1:
                                    mm.then_inc(sem["s_att"], 1)
                                else:
                                    mm.then_inc(sem["s_pfree"], 1)
                        if kt != NKT - 1:
                            continue
                        q0 = b * POS + qb * QBS
                        if e == 0:
                            tensor.wait_ge(sem["s_wol"], 32)
                            tensor.wait_ge(sem["s_xr"], 16)
                        for qs in range(2):
                            l0 = q0 + qs * P
                            j0 = qb * QBS + qs * P
                            for fh in range(2):
                                u = 4 * e + 2 * qs + fh
                                tensor.wait_ge(sem["s_tanh"], u)
                                for k in range(KC):
                                    mh = tensor.matmul(
                                        ps_h[:],
                                        xn_fm[:, k * TLOC + l0:
                                              k * TLOC + l0 + P],
                                        wo_sb[:, k * D + fh * 512:
                                              k * D + fh * 512 + 512],
                                        start=(k == 0), stop=False)
                                    if qs == 0 and fh == 0 and k == 0:
                                        mh.then_inc(sem["s_pfree"], 1)
                                for k in range(KC):
                                    tensor.matmul(
                                        ps_h[:],
                                        c2_bf[:, k * POS + j0:
                                              k * POS + j0 + P],
                                        uo_sb[:, k * D + fh * 512:
                                              k * D + fh * 512 + 512],
                                        start=False, stop=False)
                                tensor.matmul(
                                    ps_h[:], ones_sb[0:1, 0:P],
                                    b_bf["o"][0:1, fh * 512:(fh + 1) * 512],
                                    start=False, stop=True).then_inc(
                                    sem["s_hmm"], 1)

            if ph >= 5:
                block.tensor(p5_tns)

            def p5_scl(scalar):
                for e in range(NEP):
                    for kt in range(NKT):
                        g = e * NKT + kt
                        scalar.wait_ge(sem["s_smm"], g + 1)
                        if g >= 3:
                            scalar.wait_ge(sem["s_pfree"], g - 2)
                        scalar.activation(p_sb[g % 3][:],
                                          ps_s[g % 2][:, 0:QBS],
                                          AF.Exp).then_inc(sem["s_exp"], 1)
                    for u in range(4):
                        qs, fh = u // 2, u % 2
                        scalar.wait_ge(sem["s_hmm"], 4 * e + u + 1)
                        if e >= 1 and u == 0:
                            scalar.wait_ge(sem["s_eadd"], 4 * e)
                        scalar.activation(
                            h_sb[:, qs * D + fh * 512:qs * D + fh * 512 + 512],
                            ps_h[:], AF.Tanh).then_inc(sem["s_tanh"], 1)

            if ph >= 5:
                block.scalar(p5_scl)

            def p5_vec(vector):
                for e in range(NEP):
                    vector.wait_ge(sem["s_att"], e + 1)
                    vfence(vector, vector.reciprocal(rl_sb[:], l_ps[:, 0:2]))
                    for qs in range(2):
                        for fh in range(2):
                            u = 4 * e + 2 * qs + fh
                            vector.wait_ge(sem["s_tanh"], u + 1)
                            vector.scalar_tensor_tensor(
                                h_sb[:, qs * D + fh * 512:
                                     qs * D + fh * 512 + 512],
                                o_ps[qs][fh][:], rl_sb[:, qs:qs + 1],
                                h_sb[:, qs * D + fh * 512:
                                     qs * D + fh * 512 + 512],
                                op0=ALU.mult, op1=ALU.add).then_inc(
                                sem["s_eadd"], 1)

            if ph >= 5:
                block.vector(p5_vec)

            def p5_gp(gpsimd):
                gpsimd.wait_ge(sem["s_cdve"], 32)
                gpsimd.dma_start(
                    out=xres.ap().rearrange("p (t f) -> p t f", f=D),
                    in_=x_d[0:QBS, :].rearrange("(t p) f -> p t f",
                                                p=P)).then_inc(sem["s_xr"],
                                                               16)
                for e in range(NEP):
                    b, qb = e // NQB, e % NQB
                    for qs in range(2):
                        gpsimd.wait_ge(sem["s_eadd"], 4 * e + 2 * qs + 2)
                        gpsimd.wait_ge(sem["s_xr"], 16 * (e + 1))
                        if e >= 1 and qs == 0:
                            gpsimd.wait_ge(sem["s_st"], 32 * e)
                        gpsimd.tensor_add(
                            out_b[:, qs * D:(qs + 1) * D],
                            h_sb[:, qs * D:(qs + 1) * D],
                            xres[:, qs * D:(qs + 1) * D]).then_inc(
                            sem["s_pool"], 1)
                        gpsimd.wait_ge(sem["s_pool"], 2 * e + qs + 1)
                        l0 = b * POS + qb * QBS + qs * P
                        gpsimd.dma_start(
                            out=out_d[l0:l0 + P, :],
                            in_=out_b[:, qs * D:(qs + 1) * D]).then_inc(
                            sem["s_st"], 16)
                    if e + 1 < NEP:
                        b2, qb2 = (e + 1) // NQB, (e + 1) % NQB
                        x0 = b2 * POS + qb2 * QBS
                        gpsimd.dma_start(
                            out=xres.ap().rearrange("p (t f) -> p t f", f=D),
                            in_=x_d[x0:x0 + QBS, :].rearrange(
                                "(t p) f -> p t f", p=P)).then_inc(sem["s_xr"],
                                                                   16)
                gpsimd.wait_ge(sem["s_st"], 16 * 16)

            if ph >= 5:
                block.gpsimd(p5_gp)

            if dbg and ph >= 5:
                def pdbg5(gpsimd):
                    gpsimd.wait_ge(sem["s_st"], 16 * 16)
                    gpsimd.dma_start(out=dbg_d["xn"], in_=xn_fm[:, :]).then_inc(sem["s_st"], 16)
                    gpsimd.dma_start(out=dbg_d["xq"], in_=xq_aug[0:RA, :]).then_inc(sem["s_st"], 16)
                    gpsimd.dma_start(out=dbg_d["c2"], in_=c2_bf[:, :]).then_inc(sem["s_st"], 16)
                    gpsimd.dma_start(out=dbg_d["stats"][:, 0:NRT], in_=st_mu[:, :]).then_inc(sem["s_st"], 16)
                    gpsimd.dma_start(out=dbg_d["stats"][:, NRT:2 * NRT], in_=st_rs[:, :]).then_inc(sem["s_st"], 16)
                    gpsimd.dma_start(out=dbg_d["wo"], in_=wo_sb[:, :]).then_inc(sem["s_st"], 16)
                    gpsimd.dma_start(out=dbg_d["uo"], in_=uo_sb[:, :]).then_inc(sem["s_st"], 16)
                    gpsimd.dma_start(out=dbg_d["kall"], in_=xk_all[0:RA, :]).then_inc(sem["s_st"], 16)
                    gpsimd.dma_start(out=dbg_d["mkv"], in_=mkv[0:1, :]).then_inc(sem["s_st"], 16)
                    gpsimd.dma_start(out=dbg_d["hsb"], in_=h_sb[:, :]).then_inc(sem["s_st"], 16)
                    gpsimd.dma_start(out=dbg_d["kctb"][1:CROWS, :], in_=ctb_d.ap()[TLOC + 2:CTR, :].rearrange("(a b) c -> a (b c)", b=2)).then_inc(sem["s_st"], 16)
                    gpsimd.dma_start(out=dbg_d["kctb"][0:1, 0:1], in_=ctb_d.ap()[TLOC:TLOC + 1, 0:1]).then_inc(sem["s_st"], 16)
                    gpsimd.wait_ge(sem["s_st"], 16 * 28)

                block.gpsimd(pdbg5)

    return nc


_NC_CACHE = None


def kernel(**inputs):
    global _NC_CACHE
    x = np.asarray(inputs["L0_attn_x"], dtype=np.float32)
    Q = np.asarray(inputs["L0_attn_Q"], dtype=np.float32)
    K = np.asarray(inputs["L0_attn_K"], dtype=np.float32)
    VO = np.asarray(inputs["L0_attn_VO"], dtype=np.float32)
    assert x.shape == (2, 8192, 1024)

    def full_prep(w):
        return np.ascontiguousarray(
            w.reshape(KC, P, w.shape[1]).transpose(1, 0, 2))

    def slice_prep(wt):
        return np.ascontiguousarray(
            wt.reshape(KC, P, KC, P).transpose(2, 1, 0, 3))

    feed = {
        "qt": full_prep(Q.T),
        "kt": full_prep(K.T),
        "vo": full_prep(VO),
        "wo": full_prep(np.ascontiguousarray(
            np.asarray(inputs["L0_ffn_wo"], np.float32).T)),
        "uo": full_prep(np.ascontiguousarray(
            np.asarray(inputs["L0_ffn_uo"], np.float32).T)),
        "bf": np.asarray(inputs["L0_ffn_bf"], np.float32).reshape(1, D),
        "bi": np.asarray(inputs["L0_ffn_bi"], np.float32).reshape(1, D),
        "bc": np.asarray(inputs["L0_ffn_bc"], np.float32).reshape(1, D),
        "bo": np.asarray(inputs["L0_ffn_bo"], np.float32).reshape(1, D),
        "in_c": np.ascontiguousarray(
            np.asarray(inputs["L0_ffn_in_c"], np.float32).reshape(KC, P).T),
    }
    for nm in ("wf", "wi", "wc", "uf", "ui", "uc"):
        feed[nm] = slice_prep(
            np.ascontiguousarray(np.asarray(inputs[f"L0_ffn_{nm}"],
                                            np.float32).T))

    if _NC_CACHE is None:
        _NC_CACHE = build_nc()
    nc = _NC_CACHE

    in_maps = []
    for c in range(NCORE):
        sl = slice(c * POS, (c + 1) * POS)
        xs = np.ascontiguousarray(
            np.concatenate([x[0, sl], x[1, sl]], axis=0))
        m = dict(feed)
        m["x"] = xs
        in_maps.append(m)

    res = run_bass_kernel_spmd(nc, in_maps, core_ids=list(range(NCORE)))

    y = np.empty((2, 8192, 1024), dtype=np.float32)
    for c in range(NCORE):
        o = np.asarray(res.results[c]["out"])
        y[0, c * POS:(c + 1) * POS] = o[0:POS]
        y[1, c * POS:(c + 1) * POS] = o[POS:TLOC]
    return y


if __name__ == "__main__":
    nc = build_nc()
    print("built ok; instructions:", nc.n_instructions())
